# version 01
# baseline (speedup 1.0000x reference)
import os
os.environ.setdefault("TRN_ALLOW_ICI_ALL_TO_ALL", "1")
import numpy as np
import ml_dtypes
from collections import defaultdict
from contextlib import ExitStack

import concourse.bass as bass
import concourse.mybir as mybir
from concourse.bass_utils import run_bass_kernel_spmd

F32 = mybir.dt.float32
F32R = mybir.dt.float32r
BF16 = mybir.dt.bfloat16
AF = mybir.ActivationFunctionType
ALU = mybir.AluOpType

R = 8
NW = 2
L = 8192
D = 1024
E = 2048
NST = 16
H = 2752
TW = L // R
DC, EC = 8, 16
HC, H_LAST = 22, 64
TSA = 256
TSB = 256


class Bld:
    ENGINES = ("sync", "gpsimd", "vector", "scalar", "tensor")

    def __init__(self, nc):
        self.nc = nc
        self.ops = defaultdict(list)
        self.cnt = defaultdict(int)
        self.sems = {}
        self.pending = {}

    def sem_names(self):
        return ["syncd", "gpsimdd", "scalard", "vector", "scalar",
                "tensor", "gpsimd", "cc"]

    DMAQ = ("syncd", "gpsimdd", "scalard", "cc")

    def _emit(self, eng, waits, inc_sem, inc_n, fn):
        w = [(s, self.cnt[s] if s in self.DMAQ else v)
             for (s, v) in waits if (s, v) != (None, None) and s is not None
             ] if False else []
        mode = os.environ.get("MK_FENCE", "off")
        lim = int(os.environ.get("MK_FENCE_N", "999999"))
        for x in waits:
            if x is None:
                continue
            s, v = x
            promote = s in self.DMAQ and mode != "off"
            if promote and self.cnt[s] > v:
                self.nprom = getattr(self, "nprom", 0) + 1
                if self.nprom > lim:
                    promote = False
            if promote:
                v = self.cnt[s]
            w.append((s, v))
        w += self.pending.pop(eng, [])
        sems = self.sems

        def cl(e, fn=fn, w=w, inc_sem=inc_sem, inc_n=inc_n):
            for s, v in w:
                e.wait_ge(sems[s], v)
            ins = fn(e)
            if inc_sem is not None:
                ins.then_inc(sems[inc_sem], inc_n)
        self.ops[eng].append(cl)

    def op(self, eng, fn, waits=()):
        self.cnt[eng] += 1
        tok = (eng, self.cnt[eng])
        self._emit(eng, waits, eng, 1, fn)
        return tok

    def dma(self, q, out, in_, waits=()):
        s = {"sync": "syncd", "gpsimd": "gpsimdd", "scalar": "scalard"}[q]
        prev = (s, self.cnt[s])
        self.cnt[s] += 16
        tok = (s, self.cnt[s])
        self._emit(q, list(waits) + [prev], s, 16,
                   lambda e, out=out, in_=in_: e.dma_start(out=out, in_=in_))
        return tok

    def cc(self, kind, in_, out, waits=()):
        prev = ("cc", self.cnt["cc"])
        self.cnt["cc"] += 1
        tok = ("cc", self.cnt["cc"])
        self._emit("gpsimd", list(waits) + [prev], "cc", 1,
                   lambda e, k=kind, i=in_, o=out:
                   e.collective_compute(k, ALU.bypass, [list(range(R))],
                                        ins=[i.ap().opt()], outs=[o.ap().opt()]))
        return tok

    def barrier(self):
        snap = [(s, c) for s, c in self.cnt.items() if c > 0]
        for eng in self.ENGINES:
            self.pending[eng] = list(snap)

    def flush_barrier(self):
        sems = self.sems
        for eng in self.ENGINES:
            w = self.pending.pop(eng, [])
            if not w:
                continue

            def cl(e, w=w):
                for s, v in w:
                    e.wait_ge(sems[s], v)
            self.ops[eng].append(cl)


def build(debug=False):
    nc = bass.Bass("TRN2", target_bir_lowering=False)

    def din(name, shape, dt=F32):
        return nc.dram_tensor(name, shape, dt, kind="ExternalInput")

    def dout(name, shape, dt=F32):
        return nc.dram_tensor(name, shape, dt, kind="ExternalOutput")

    def dint(name, shape, dt=F32, dbg_ok=True):
        kind = "ExternalOutput" if (debug and dbg_ok) else "Internal"
        return nc.dram_tensor(name, shape, dt, kind=kind)

    x_in = din("x", [NW * TW, D])
    wig = din("wig", [D, 2 * E], F32R)
    dtT = din("dtT", [E, E], F32R)
    xpT = din("xpT", [128, EC * 2 * NST], F32R)
    outT = din("outT", [E, D], F32R)
    gupT = din("gupT", [D, 2 * H])
    dnT = din("dnT", [H, D])
    cpj = din("cpj", [16384, D])
    makerJ = din("makerJ", [NST, 4096], F32R)
    upJ = din("upJ", [NST, 16384], F32R)
    a_log = din("a_log", [128, 2 * NST])
    dtb = din("dtb", [128, EC])
    dpv = din("dpv", [128, EC])
    n1w = din("n1w", [128, DC])
    fnw = din("fnw", [128, DC])
    eye_c = din("eye", [128, 128])
    sel16 = din("sel16", [NST, NST * 128], F32R)
    ones1 = din("ones1", [1, 128], F32R)
    onesc = din("onesc", [128, 1], F32R)

    out_t = dout("out", [NW * TW, D])
    z_sp = dint("z_sp", [E, NW * TW], F32R)
    xfm_sp = dint("xfm_sp", [D, NW * TW], F32R)
    u0_sp = dint("u0_sp", [D, NW * TW])
    xa_sp = dint("xa_sp", [D, NW * TW], F32R)
    x1_sp = dint("x1_sp", [D, NW * TW], F32R)
    u1_sp = dint("u1_sp", [D, NW * TW], BF16, dbg_ok=False)
    wbig_sp = dint("wbig_sp", [128, 8 * 2048], BF16, dbg_ok=False)
    dt_i = [dint(f"dt_i{w}", [E, TW], dbg_ok=False) for w in range(NW)]
    dt_o = [dint(f"dt_o{w}", [E, TW], dbg_ok=False) for w in range(NW)]
    dz_i = [dint(f"dz_i{w}", [E, TW], dbg_ok=False) for w in range(NW)]
    dz_o = [dint(f"dz_o{w}", [E, TW], dbg_ok=False) for w in range(NW)]
    y_i = [dint(f"y_i{w}", [E, TW], dbg_ok=False) for w in range(NW)]
    y_o = [dint(f"y_o{w}", [E, TW], dbg_ok=False) for w in range(NW)]
    bc_i = [dint(f"bc_i{w}", [2 * NST, TW], dbg_ok=False) for w in range(NW)]
    bc_o = [dint(f"bc_o{w}", [R * 2 * NST, TW], dbg_ok=False) for w in range(NW)]
    if debug:
        dbg_y = [dout(f"dbg_y{w}", [E, TW]) for w in range(NW)]
        dbg_dt = [dout(f"dbg_dt{w}", [E, TW]) for w in range(NW)]
        dbg_dz = [dout(f"dbg_dz{w}", [E, TW]) for w in range(NW)]
        dbg_bc = [dout(f"dbg_bc{w}", [R * 2 * NST, TW]) for w in range(NW)]
        dbg_bci = [dout(f"dbg_bci{w}", [2 * NST, TW]) for w in range(NW)]
        dbg_zz = dout("dbg_zz", [128, 2 * TW])
        dbg_xp = dout("dbg_xp", [128, EC * 2 * NST])
        dbg_xp0 = dout("dbg_xp0", [128, EC * 2 * NST])
        dbg_btl = dout("dbg_btl", [NST, 2 * TW])

    bld = Bld(nc)
    stack = ExitStack()
    with stack:
        for s in bld.sem_names():
            bld.sems[s] = stack.enter_context(nc.semaphore(s))

        def sb(st, name, shape, dt=F32):
            return st.enter_context(nc.sbuf_tensor(name, shape, dt))

        psT = [stack.enter_context(nc.psum_tensor(f"ps{i}", [128, 1024], F32))
               for i in range(4)]

        def bank(i):
            return psT[i // 2][:, (i % 2) * 512:(i % 2 + 1) * 512]

        eye = sb(stack, "eye_s", [128, 128])
        sel = sb(stack, "sel_s", [NST, NST * 128], F32R)
        on1 = sb(stack, "on1_s", [1, 128], F32R)
        onc = sb(stack, "onc_s", [128, 1], F32R)
        an = sb(stack, "an_s", [128, 2 * NST])
        dtb_s = sb(stack, "dtb_s", [128, EC])
        dpv_s = sb(stack, "dpv_s", [128, EC])
        n1w_s = sb(stack, "n1w_s", [128, DC])
        fnw_s = sb(stack, "fnw_s", [128, DC])
        rows = sb(stack, "rows", [1, 4096], F32R)

        def rs(slot, width=TSA):
            return rows[0:1, slot * 512:slot * 512 + width]

        def TT(out, a, b, alu, waits=(), eng="vector"):
            return bld.op(eng, lambda e, out=out, a=a, b=b, alu=alu:
                          e.tensor_tensor(out, a, b, alu), waits)

        def CP(out, in_, waits=(), eng="vector"):
            return bld.op(eng, lambda e, out=out, in_=in_:
                          e.tensor_copy(out, in_), waits)

        def TSC(out, a, s1, alu, waits=(), eng="vector"):
            return bld.op(eng, lambda e, out=out, a=a, s1=s1, alu=alu:
                          e.tensor_single_scalar(out, a, s1, alu), waits)

        def STT(out, in0, sc, in1, op0, op1, waits=(), eng="vector"):
            return bld.op(eng,
                          lambda e, out=out, in0=in0, sc=sc, in1=in1,
                          op0=op0, op1=op1:
                          e.scalar_tensor_tensor(out, in0, sc, in1, op0, op1),
                          waits)

        def ACTI(out, in_, f, bias=0.0, scale=1.0, waits=()):
            return bld.op("scalar",
                          lambda e, out=out, in_=in_, f=f, b=bias, s=scale:
                          e.activation(out, in_, f, bias=b, scale=s), waits)

        def MM(out, lhsT, rhs, start, stop, waits=()):
            return bld.op("tensor",
                          lambda e, out=out, l=lhsT, r=rhs, st=start, sp=stop:
                          e.matmul(out, l, r, start=st, stop=sp), waits)

        def TRA(out, in_, waits=()):
            return bld.op("tensor",
                          lambda e, out=out, in_=in_:
                          e.transpose(out, in_, eye[:, :]), waits)

        def SCAN(out, d0, d1, init, waits=()):
            return bld.op("vector",
                          lambda e, out=out, d0=d0, d1=d1, init=init:
                          e.tensor_tensor_scan(out, d0, d1, init,
                                               ALU.mult, ALU.add), waits)

        def MSET(t, v, waits=(), eng="vector"):
            return bld.op(eng, lambda e, t=t, v=v: e.memset(t, v), waits)

        def stats_ln(src_tiles, sq_pair, width, with_mean=True):
            m1 = m2 = None
            for dc in range(DC):
                sv = ACTI(sq_pair[dc % 2][:, 0:width],
                          src_tiles[dc][:, 0:width].bitcast(F32), AF.Square)
                if with_mean:
                    m1 = MM(bank(0)[0:1, 0:width], onc[:, :],
                            src_tiles[dc][:, 0:width], dc == 0, dc == DC - 1)
                m2 = MM(bank(1)[0:1, 0:width], onc[:, :],
                        sq_pair[dc % 2][:, 0:width], dc == 0, dc == DC - 1,
                        waits=[sv])
            toks = []
            if with_mean:
                v1 = CP(rs(0, width), bank(0)[0:1, 0:width], waits=[m1])
                toks.append(v1)
            v2 = CP(rs(1, width), bank(1)[0:1, 0:width], waits=[m2])
            toks.append(v2)
            if with_mean:
                a = TSC(rs(2, width), rs(0, width), 1.0 / D, ALU.mult,
                        waits=[v1])
                a = TT(rs(7, width), rs(2, width), rs(2, width), ALU.mult,
                       waits=[a])
                a = STT(rs(3, width), rs(1, width), 1.0 / D, rs(7, width),
                        ALU.mult, ALU.subtract, waits=[a, v2])
                a = TSC(rs(7, width), rs(3, width), 1e-5, ALU.add, waits=[a])
            else:
                a = TSC(rs(3, width), rs(1, width), 1.0 / D, ALU.mult,
                        waits=[v2])
                a = TSC(rs(7, width), rs(3, width), 1e-6, ALU.add, waits=[a])
            s1 = ACTI(rs(4, width), rs(7, width).bitcast(F32), AF.Ln,
                      waits=[a])
            s1 = ACTI(rs(4, width), rs(4, width).bitcast(F32), AF.Exp,
                      scale=-0.5, waits=[s1])
            return s1, a

        for t, src in ((eye, eye_c), (sel, sel16), (on1, ones1), (onc, onesc),
                       (an, a_log), (dtb_s, dtb), (dpv_s, dpv),
                       (n1w_s, n1w), (fnw_s, fnw)):
            bld.dma("sync", t[:, :], src.ap())
        ld = ("syncd", bld.cnt["syncd"])
        t_ = ACTI(an[:, :], an[:, :], AF.Exp, waits=[ld])
        bld.op("vector", lambda e: e.tensor_scalar_mul(an[:, :], an[:, :], -1.0),
               waits=[t_])

        STOP = int(os.environ.get("MK_STOP", "99"))
        with ExitStack() as pst:
            mkj = sb(pst, "mkj", [NST, 4096], F32R)
            upj = sb(pst, "upj", [NST, 16384], F32R)
            zt = sb(pst, "zt", [128, 2048], BF16)
            stg = sb(pst, "stg", [NST, 8 * 64], BF16)
            bld.dma("sync", mkj[:, :], makerJ.ap())
            bld.dma("sync", upj[:, :], upJ.ap())
            wtok = ("syncd", bld.cnt["syncd"])
            zv = MSET(zt[:, :], 0.0)
            for g in range(8):
                bld.dma("sync", wbig_sp.ap()[:, g * 2048:(g + 1) * 2048],
                        zt[:, :], waits=[zv])
            zdone = ("syncd", bld.cnt["syncd"])
            ev_tok = [None] * 8
            dm_tok = [None] * 8
            for h in range(4):
                for r in range(64):
                    hr = h * 64 + r
                    slot = hr % 8
                    pb = bank(slot)[0:NST, 0:64]
                    tt = MM(pb, mkj[:, hr * 16:hr * 16 + 16],
                            upj[:, hr * 64:hr * 64 + 64], True, True,
                            waits=[wtok, ev_tok[slot]])
                    ss_ = stg[0:NST, slot * 64:(slot + 1) * 64]
                    vv = CP(ss_, pb, waits=[tt, dm_tok[slot]])
                    g = r // 8
                    kcl = 2 * (r % 8) + h // 2
                    col = g * 2048 + kcl * 128 + (h % 2) * 64
                    dm_tok[slot] = bld.dma(
                        "sync",
                        wbig_sp.ap()[16 * (r % 8):16 * (r % 8) + 16,
                                     col:col + 64],
                        ss_, waits=[vv, zdone])
                    ev_tok[slot] = vv
        bld.barrier()

        with ExitStack() as pst:
            wig_s = sb(pst, "wig_s", [128, 8 * 4096], F32R)
            for k in range(8):
                bld.dma("scalar", wig_s[:, k * 4096:(k + 1) * 4096],
                        wig.ap()[k * 128:(k + 1) * 128, :])
            wig_tok = ("scalard", bld.cnt["scalard"])
            stage = [sb(pst, f"xstg{i}", [128, D]) for i in range(2)]
            xfm = [sb(pst, f"xfm{i}", [128, TSA], F32R) for i in range(DC)]
            sqp = [sb(pst, f"sq{i}", [128, TSA], F32R) for i in range(2)]
            u0t = [sb(pst, f"u0t{i}", [128, TSA]) for i in range(2)]
            h1t = sb(pst, "h1t", [128, TSA])
            h1 = [sb(pst, f"h1_{i}", [128, TSA], F32R) for i in range(DC)]
            sgt = [sb(pst, f"sgt{i}", [128, TSA]) for i in range(2)]
            zot = [sb(pst, f"zot{i}", [128, TSA], F32R) for i in range(2)]

            for w in range(NW if STOP >= 1 else 0):
                for ts in range(TW // TSA):
                    bld.barrier()
                    tb = w * TW + ts * TSA
                    ntt = TSA // 128
                    ltok = []
                    for tt in range(ntt):
                        ltok.append(bld.dma(
                            "sync", stage[tt][:, :],
                            x_in.ap()[tb + tt * 128:tb + (tt + 1) * 128, :]))
                    ev8 = []
                    tra_ev = [None, None]
                    for dc in range(DC):
                        bi = 4 + (dc % 2)
                        ttk = None
                        for tt in range(ntt):
                            ttk = TRA(bank(bi)[:, tt * 128:(tt + 1) * 128],
                                      stage[tt][:, dc * 128:(dc + 1) * 128],
                                      waits=[ltok[tt], tra_ev[dc % 2]])
                        ev = bld.op("scalar",
                                    lambda e, dc=dc, bi=bi:
                                    e.copy(xfm[dc][:, :], bank(bi)[:, 0:TSA]),
                                    waits=[ttk])
                        tra_ev[dc % 2] = ev
                        ev8.append(ev)
                        bld.dma("sync",
                                xfm_sp.ap()[dc * 128:(dc + 1) * 128,
                                            tb:tb + TSA],
                                xfm[dc][:, :], waits=[ev])
                    m1 = m2 = None
                    sq_rd = [None, None]
                    for dc in range(DC):
                        sv = ACTI(sqp[dc % 2][:, :], xfm[dc][:, :].bitcast(F32),
                                  AF.Square, waits=[ev8[dc], sq_rd[dc % 2]])
                        m1 = MM(bank(0)[0:1, 0:TSA], onc[:, :], xfm[dc][:, :],
                                dc == 0, dc == DC - 1, waits=[ev8[dc]])
                        m2 = MM(bank(1)[0:1, 0:TSA], onc[:, :],
                                sqp[dc % 2][:, :], dc == 0, dc == DC - 1,
                                waits=[sv])
                        sq_rd[dc % 2] = m2
                    v1 = CP(rs(0), bank(0)[0:1, 0:TSA], waits=[m1, m2])
                    v2 = CP(rs(1), bank(1)[0:1, 0:TSA], waits=[m2])
                    a = TSC(rs(2), rs(0), 1.0 / D, ALU.mult, waits=[v1])
                    a = TT(rs(7), rs(2), rs(2), ALU.mult, waits=[a])
                    a = STT(rs(3), rs(1), 1.0 / D, rs(7), ALU.mult,
                            ALU.subtract, waits=[a, v2])
                    a = TSC(rs(7), rs(3), 1e-5, ALU.add, waits=[a])
                    s1 = ACTI(rs(4), rs(7).bitcast(F32), AF.Ln, waits=[a])
                    s1 = ACTI(rs(4), rs(4).bitcast(F32), AF.Exp, scale=-0.5,
                              waits=[s1])
                    mr = TT(rs(6), rs(2), rs(4), ALU.mult, waits=[s1])
                    rc = bld.op("vector", lambda e: e.reciprocal(rs(0), rs(7)),
                                waits=[s1])
                    q = TT(rs(1), rs(3), rs(0), ALU.mult, waits=[rc])
                    q = TSC(rs(7), rs(1), 1e-6, ALU.add, waits=[q])
                    s2 = ACTI(rs(5), rs(7).bitcast(F32), AF.Ln, waits=[q])
                    s2 = ACTI(rs(5), rs(5).bitcast(F32), AF.Exp, scale=-0.5,
                              waits=[s2])
                    b_r1 = MM(bank(2)[:, 0:TSA], on1[:, :], rs(4), True, True,
                              waits=[s1])
                    b_mr = MM(bank(3)[:, 0:TSA], on1[:, :], rs(6), True, True,
                              waits=[mr])
                    b_s2 = MM(bank(4)[:, 0:TSA], on1[:, :], rs(5), True, True,
                              waits=[s2, tra_ev[0]])
                    h1_last = None
                    u0_fr = [None, None]
                    for dc in range(DC):
                        ui = dc % 2
                        a1 = TT(u0t[ui][:, :], xfm[dc][:, :].bitcast(F32),
                                bank(2)[:, 0:TSA], ALU.mult,
                                waits=[b_r1, u0_fr[ui]])
                        a2 = TT(u0t[ui][:, :], u0t[ui][:, :], bank(3)[:, 0:TSA],
                                ALU.subtract, waits=[a1, b_mr])
                        u0_fr[ui] = bld.dma(
                            "sync",
                            u0_sp.ap()[dc * 128:(dc + 1) * 128, tb:tb + TSA],
                            u0t[ui][:, :], waits=[a2])
                        a3 = TT(h1t[:, :], u0t[ui][:, :], bank(4)[:, 0:TSA],
                                ALU.mult, waits=[a2, b_s2, h1_last])
                        h1_last = ACTI(h1[dc][:, :], h1t[:, :], AF.Identity,
                                       scale=n1w_s[:, dc:dc + 1], waits=[a3])
                    zg_rd = {0: v1, 1: v2, 6: None, 7: None}
                    zo_fr = [None, None]
                    for ei in range(EC):
                        zb, gb = (6, 7) if ei % 2 == 0 else (0, 1)
                        mmz = mmg = None
                        for k in range(8):
                            mmz = MM(bank(zb)[:, 0:TSA],
                                     wig_s[:, k * 4096 + ei * 128:
                                           k * 4096 + (ei + 1) * 128],
                                     h1[k][:, :], k == 0, k == 7,
                                     waits=[h1_last, zg_rd[zb], wig_tok])
                            mmg = MM(bank(gb)[:, 0:TSA],
                                     wig_s[:, k * 4096 + 2048 + ei * 128:
                                           k * 4096 + 2048 + (ei + 1) * 128],
                                     h1[k][:, :], k == 0, k == 7,
                                     waits=[zg_rd[gb]])
                        sgv = ACTI(sgt[ei % 2][:, :], bank(gb)[:, 0:TSA],
                                   AF.Sigmoid, waits=[mmg, zo_fr[ei % 2]])
                        zv = TT(zot[ei % 2][:, :], bank(zb)[:, 0:TSA],
                                sgt[ei % 2][:, :], ALU.mult, waits=[mmz, sgv])
                        zo_fr[ei % 2] = bld.dma(
                            "sync",
                            z_sp.ap()[ei * 128:(ei + 1) * 128, tb:tb + TSA],
                            zot[ei % 2][:, :], waits=[zv])
                        zg_rd[zb] = zv
                        zg_rd[gb] = sgv
        bld.barrier()

        with ExitStack() as pst:
            if not os.environ.get("MK_A2_NODT"):
                dt_s = sb(pst, "dt_s", [128, 16 * 2048], F32R)
                for k in range(EC):
                    bld.dma("scalar", dt_s[:, k * 2048:(k + 1) * 2048],
                            dtT.ap()[k * 128:(k + 1) * 128, :])
            xp_s = sb(pst, "xp_s", [128, 16 * 32], F32R)
            xp_ld = bld.dma("scalar", xp_s[:, :], xpT.ap())
            if debug:
                bld.dma("sync", dbg_xp0.ap(), xp_s[:, :].bitcast(F32),
                        waits=[xp_ld])
            w_tok = ("scalard", bld.cnt["scalard"])
            TS2 = 256
            NZB = 4 if os.environ.get("MK_A2_ZZ4") else 1
            zz_all = [[sb(pst, f"zz{j}_{i}", [128, TS2], F32R)
                       for i in range(EC)] for j in range(NZB)]
            pt = [sb(pst, f"pt{i}", [128, TS2]) for i in range(2)]
            ab = [sb(pst, f"ab{i}", [128, TS2]) for i in range(2)]
            rl = [sb(pst, f"rl{i}", [128, TS2]) for i in range(2)]
            dtv = [sb(pst, f"dtv{i}", [128, TS2]) for i in range(2)]
            dzv = [sb(pst, f"dzv{i}", [128, TS2]) for i in range(2)]
            bt_l = sb(pst, "bt_l", [NST, TW])
            ct_l = sb(pst, "ct_l", [NST, TW])

            for w in range(NW if STOP >= 2 else 0):
                bc_ev = [None, None]
                for ts in range(TW // TS2):
                    bld.barrier()
                    zz = zz_all[ts % len(zz_all)]
                    tb = w * TW + ts * TS2
                    ztk = []
                    for k in range(EC):
                        ztk.append(bld.dma(
                            "sync", zz[k][:, :],
                            z_sp.ap()[k * 128:(k + 1) * 128, tb:tb + TS2]))
                    zdone = ztk[-1]
                    mb = mc_ = None
                    for k in range(EC):
                        mb = MM(bank(2)[0:NST, 0:TS2],
                                xp_s[:, k * 32:k * 32 + 16], zz[k][:, :],
                                k == 0, k == EC - 1, waits=[ztk[k], w_tok])
                        mc_ = MM(bank(3)[0:NST, 0:TS2],
                                 xp_s[:, k * 32 + 16:k * 32 + 32], zz[k][:, :],
                                 k == 0, k == EC - 1)
                    bc_ev[0] = CP(bt_l[:, ts * TS2:(ts + 1) * TS2],
                                  bank(2)[0:NST, 0:TS2], waits=[mb, mc_])
                    bc_ev[1] = CP(ct_l[:, ts * TS2:(ts + 1) * TS2],
                                  bank(3)[0:NST, 0:TS2], waits=[mc_])
                    if debug:
                        bld.dma("sync",
                                dbg_zz.ap()[:, w * TW + ts * TS2:
                                            w * TW + (ts + 1) * TS2],
                                zz[0][:, :].bitcast(F32), waits=[bc_ev[1]])
                    dtbank_rd = [None, None]
                    tmp_fr = [None, None]
                    for ei in range(0 if os.environ.get("MK_A2_NODT") else EC):
                        bi = ei % 2
                        mmd = None
                        for k in range(EC):
                            mmd = MM(bank(bi)[:, 0:TS2],
                                     dt_s[:, k * 2048 + ei * 128:
                                          k * 2048 + (ei + 1) * 128],
                                     zz[k][:, :], k == 0, k == EC - 1,
                                     waits=[zdone, w_tok, dtbank_rd[bi]])
                        p1 = ACTI(pt[bi][:, :], bank(bi)[:, 0:TS2], AF.Identity,
                                  bias=dtb_s[:, ei:ei + 1],
                                  waits=[mmd, tmp_fr[bi]])
                        p2 = ACTI(ab[bi][:, :], bank(bi)[:, 0:TS2], AF.Abs,
                                  bias=dtb_s[:, ei:ei + 1], waits=[mmd, p1])
                        dtbank_rd[bi] = p2
                        p3 = ACTI(ab[bi][:, :], ab[bi][:, :], AF.Exp,
                                  scale=-1.0, waits=[p2])
                        p4 = ACTI(ab[bi][:, :], ab[bi][:, :], AF.Ln, bias=1.0,
                                  waits=[p3])
                        v1 = TSC(rl[bi][:, :], pt[bi][:, :], 0.0, ALU.max,
                                 waits=[p1, tmp_fr[bi]])
                        v2 = TT(dtv[bi][:, :], rl[bi][:, :], ab[bi][:, :],
                                ALU.add, waits=[v1, p4, tmp_fr[bi]])
                        d1 = bld.dma("sync",
                                     dt_i[w].ap()[ei * 128:(ei + 1) * 128,
                                                  ts * TS2:(ts + 1) * TS2],
                                     dtv[bi][:, :], waits=[v2])
                        v3 = TT(dzv[bi][:, :], dtv[bi][:, :],
                                zz[ei][:, :].bitcast(F32), ALU.mult,
                                waits=[v2, tmp_fr[bi]])
                        d2 = bld.dma("sync",
                                     dz_i[w].ap()[ei * 128:(ei + 1) * 128,
                                                  ts * TS2:(ts + 1) * TS2],
                                     dzv[bi][:, :], waits=[v3])
                        tmp_fr[bi] = d2
                bld.dma("sync", bc_i[w].ap()[0:NST, :], bt_l[:, :],
                        waits=[bc_ev[0]])
                if debug:
                    bld.dma("sync", dbg_btl.ap()[:, w * TW:(w + 1) * TW],
                            bt_l[:, :], waits=[bc_ev[0]])
                    if w == NW - 1:
                        bld.dma("sync", dbg_xp.ap(), xp_s[:, :].bitcast(F32))
                bld.dma("sync", bc_i[w].ap()[NST:2 * NST, :], ct_l[:, :],
                        waits=[bc_ev[1]])
                spill_done = ("syncd", bld.cnt["syncd"])
                bld.cc("AllToAll", dt_i[w], dt_o[w], waits=[spill_done])
                bld.cc("AllToAll", dz_i[w], dz_o[w])
                bld.cc("AllGather", bc_i[w], bc_o[w])
        bld.barrier()

        with ExitStack() as pst:
            dtt = [sb(pst, f"dtt{i}", [128, TW]) for i in range(2)]
            dzt = [sb(pst, f"dzt{i}", [128, TW]) for i in range(2)]
            dat = [sb(pst, f"dat{i}", [128, TW]) for i in range(2)]
            dbz = [sb(pst, f"dbz{i}", [128, TW]) for i in range(2)]
            ht = [sb(pst, f"ht{i}", [128, TW]) for i in range(2)]
            ymt = [sb(pst, f"ymt{i}", [128, TW]) for i in range(2)]
            yac = [sb(pst, f"yac{i}", [128, TW]) for i in range(2)]
            btile = sb(pst, "btile", [NST, TW], F32R)
            ctile = sb(pst, "ctile", [NST, TW], F32R)
            carry = [sb(pst, f"carry{i}", [128, NST]) for i in range(2)]

            for w in range(NW if STOP >= 3 else 0):
                cctok = ("cc", 3 * (w + 1))
                bld.barrier()
                ct_toks = [MSET(carry[0][:, :], 0.0),
                           MSET(carry[1][:, :], 0.0)]
                carry_tok = ct_toks
                for c in range(R):
                    bld.barrier()
                    dtk, dzk = [], []
                    for dc in range(2):
                        dtk.append(bld.dma(
                            "sync", dtt[dc][:, :],
                            dt_o[w].ap()[c * 256 + dc * 128:
                                         c * 256 + (dc + 1) * 128, :],
                            waits=[cctok]))
                        dzk.append(bld.dma(
                            "sync", dzt[dc][:, :],
                            dz_o[w].ap()[c * 256 + dc * 128:
                                         c * 256 + (dc + 1) * 128, :],
                            waits=[cctok]))
                    btk = bld.dma("sync", btile[:, :],
                                  bc_o[w].ap()[32 * c:32 * c + NST, :]
                                  .bitcast(F32R), waits=[cctok])
                    ctk = bld.dma("sync", ctile[:, :],
                                  bc_o[w].ap()[32 * c + NST:32 * c + 2 * NST, :]
                                  .bitcast(F32R), waits=[cctok])
                    bb_rd = [None, None]
                    cc_rd = [None, None]
                    sc_tok = [None, None]
                    cv_tok = [None, None]
                    ya_tok = [None, None]
                    yac_fr = [None, None]
                    for n in range(NST):
                        p = n % 2
                        bbm = ccm = None
                        for hf in range(2):
                            bbm = MM(psT[2 * p][:, hf * 512:(hf + 1) * 512],
                                     sel[:, n * 128:(n + 1) * 128],
                                     btile[:, hf * 512:(hf + 1) * 512],
                                     True, True, waits=[btk, bb_rd[p]])
                            ccm = MM(psT[2 * p + 1][:, hf * 512:(hf + 1) * 512],
                                     sel[:, n * 128:(n + 1) * 128],
                                     ctile[:, hf * 512:(hf + 1) * 512],
                                     True, True, waits=[ctk, cc_rd[p]])
                        for dc in range(2):
                            i2 = (2 * n + dc) % 2
                            av = ACTI(dat[i2][:, :], dtt[dc][:, :], AF.Exp,
                                      scale=an[:, dc * 16 + n:dc * 16 + n + 1],
                                      waits=[dtk[dc], sc_tok[i2]])
                            bv = TT(dbz[i2][:, :], dzt[dc][:, :],
                                    psT[2 * p][:, :], ALU.mult,
                                    waits=[dzk[dc], bbm, sc_tok[i2]])
                            bb_rd[p] = bv
                            hv = SCAN(ht[i2][:, :], dat[i2][:, :],
                                      dbz[i2][:, :], carry[dc][:, n:n + 1],
                                      waits=[av, bv, carry_tok[dc],
                                             ya_tok[i2], cv_tok[i2]])
                            sc_tok[i2] = hv
                            cv = CP(carry[dc][:, n:n + 1], ht[i2][:, TW - 1:TW],
                                    waits=[hv])
                            cv_tok[i2] = cv
                            carry_tok[dc] = cv
                            yv = TT(ymt[i2][:, :], ht[i2][:, :],
                                    psT[2 * p + 1][:, :], ALU.mult,
                                    waits=[hv, ccm, ya_tok[i2]])
                            cc_rd[p] = yv
                            if n == 0:
                                ya = CP(yac[dc][:, :], ymt[i2][:, :],
                                        waits=[yv, yac_fr[dc]], eng="gpsimd")
                            else:
                                ya = TT(yac[dc][:, :], yac[dc][:, :],
                                        ymt[i2][:, :], ALU.add,
                                        waits=[yv, yac_fr[dc]], eng="gpsimd")
                            ya_tok[i2] = ya
                            yac_fr[dc] = ya
                    for dc in range(2):
                        bld.dma("sync",
                                y_i[w].ap()[c * 256 + dc * 128:
                                            c * 256 + (dc + 1) * 128, :],
                                yac[dc][:, :], waits=[yac_fr[dc]])
                ysp = ("syncd", bld.cnt["syncd"])
                bld.cc("AllToAll", y_i[w], y_o[w], waits=[ysp])
        bld.barrier()
        if debug:
            with ExitStack() as pst:
                tb_ = sb(pst, "dbgt", [128, TW])
                prev = None
                for w in range(NW if STOP >= 4 else 0):
                    for src_t, dst_t, nch in ((y_o[w], dbg_y[w], EC),
                                              (dt_o[w], dbg_dt[w], EC),
                                              (dz_o[w], dbg_dz[w], EC),
                                              (bc_o[w], dbg_bc[w], 2)):
                        pass
                    for src_t, dst_t, nch in ((y_o[w], dbg_y[w], EC),
                                              (dt_o[w], dbg_dt[w], EC),
                                              (dz_o[w], dbg_dz[w], EC),
                                              (bc_o[w], dbg_bc[w], 2)):
                        for e_ in range(nch):
                            t1 = bld.dma("sync", tb_[:, :],
                                         src_t.ap()[e_ * 128:(e_ + 1) * 128, :],
                                         waits=[prev])
                            prev = bld.dma(
                                "sync",
                                dst_t.ap()[e_ * 128:(e_ + 1) * 128, :],
                                tb_[:, :], waits=[t1])
                    t1 = bld.dma("sync", tb_[0:32, :], bc_i[w].ap()[:, :],
                                 waits=[prev])
                    prev = bld.dma("sync", dbg_bci[w].ap()[:, :],
                                   tb_[0:32, :], waits=[t1])
            bld.barrier()

        with ExitStack() as pst:
            ot_s = sb(pst, "ot_s", [128, 16 * 1024], F32R)
            for k in range(EC):
                bld.dma("scalar", ot_s[:, k * 1024:(k + 1) * 1024],
                        outT.ap()[k * 128:(k + 1) * 128, :])
            w_tok = ("scalard", bld.cnt["scalard"])
            yy = [sb(pst, f"yy{i}", [128, TSB]) for i in range(EC)]
            zz2 = [sb(pst, f"zz2{i}", [128, TSB], F32R) for i in range(EC)]
            yf = [sb(pst, f"yf{i}", [128, TSB], F32R) for i in range(EC)]
            uu = [sb(pst, f"uu{i}", [128, TSB]) for i in range(DC)]
            xat = [sb(pst, f"xat{i}", [128, TSB], F32R) for i in range(2)]
            for w in range(NW if STOP >= 5 else 0):
                for ts in range(TW // TSB):
                    bld.barrier()
                    tb = w * TW + ts * TSB
                    yftk = []
                    for k in range(EC):
                        a_ = bld.dma("sync", yy[k][:, :],
                                     y_o[w].ap()[k * 128:(k + 1) * 128,
                                                 ts * TSB:(ts + 1) * TSB])
                        b_ = bld.dma("sync", zz2[k][:, :],
                                     z_sp.ap()[k * 128:(k + 1) * 128,
                                               tb:tb + TSB])
                        yftk.append(STT(yf[k][:, :], zz2[k][:, :].bitcast(F32),
                                        dpv_s[:, k:k + 1], yy[k][:, :],
                                        ALU.mult, ALU.add, waits=[a_, b_]))
                    utk = []
                    for dc in range(DC):
                        utk.append(bld.dma(
                            "sync", uu[dc][:, :],
                            u0_sp.ap()[dc * 128:(dc + 1) * 128, tb:tb + TSB]))
                    bank_rd = [None, None]
                    xa_fr = [None, None]
                    for dm in range(DC):
                        bi = dm % 2
                        mm_ = None
                        for k in range(EC):
                            mm_ = MM(bank(bi)[:, 0:TSB],
                                     ot_s[:, k * 1024 + dm * 128:
                                          k * 1024 + (dm + 1) * 128],
                                     yf[k][:, :], k == 0, k == EC - 1,
                                     waits=[yftk[k], w_tok, bank_rd[bi]])
                        xv = TT(xat[bi][:, :], bank(bi)[:, 0:TSB],
                                uu[dm][:, :], ALU.add,
                                waits=[mm_, utk[dm], xa_fr[bi]])
                        bank_rd[bi] = xv
                        xa_fr[bi] = bld.dma(
                            "sync",
                            xa_sp.ap()[dm * 128:(dm + 1) * 128, tb:tb + TSB],
                            xat[bi][:, :], waits=[xv])
        bld.barrier()

        with ExitStack() as pst:
            gup_s = sb(pst, "gup_s", [128, 8 * 5504], BF16)
            for k in range(8):
                bld.dma("gpsimd", gup_s[:, k * 5504:(k + 1) * 5504],
                        gupT.ap()[k * 128:(k + 1) * 128, :])
            dn_s = sb(pst, "dn_s", [128, 22 * 1024], BF16)
            for kc in range(HC):
                rows_n = 128 if kc < HC - 1 else H_LAST
                bld.dma("gpsimd", dn_s[0:rows_n, kc * 1024:(kc + 1) * 1024],
                        dnT.ap()[kc * 128:kc * 128 + rows_n, :])
            w_tok = ("gpsimdd", bld.cnt["gpsimdd"])
            xat2 = [sb(pst, f"xat2_{i}", [128, TSB], F32R) for i in range(DC)]
            hnb = [sb(pst, f"hnb{i}", [128, TSB], BF16) for i in range(DC)]
            hnt = sb(pst, "hnt", [128, TSB])
            sq2 = [sb(pst, f"sq2_{i}", [128, TSB], F32R) for i in range(2)]
            sgt2 = [sb(pst, f"sgt2_{i}", [128, TSB]) for i in range(2)]
            slab = sb(pst, "slab", [128, HC * TSB], BF16)
            xfmt = [sb(pst, f"xfmt{i}", [128, TSB]) for i in range(DC)]
            x1t = [sb(pst, f"x1t{i}", [128, TSB], F32R) for i in range(DC)]
            u1b = sb(pst, "u1b", [128, TSB], BF16)
            tmp2 = sb(pst, "tmp2", [128, TSB])

            for w in range(NW if STOP >= 6 else 0):
                for ts in range(TW // TSB):
                    bld.barrier()
                    tb = w * TW + ts * TSB
                    xtk = []
                    for dc in range(DC):
                        xtk.append(bld.dma(
                            "sync", xat2[dc][:, :],
                            xa_sp.ap()[dc * 128:(dc + 1) * 128, tb:tb + TSB]))
                    m2_ = None
                    sq_rd = [None, None]
                    for dc in range(DC):
                        sv = ACTI(sq2[dc % 2][:, :],
                                  xat2[dc][:, :].bitcast(F32),
                                  AF.Square, waits=[xtk[dc], sq_rd[dc % 2]])
                        m2_ = MM(bank(1)[0:1, 0:TSB], onc[:, :],
                                 sq2[dc % 2][:, :], dc == 0, dc == DC - 1,
                                 waits=[sv])
                        sq_rd[dc % 2] = m2_
                    v2 = CP(rs(1), bank(1)[0:1, 0:TSB], waits=[m2_])
                    a = TSC(rs(3), rs(1), 1.0 / D, ALU.mult, waits=[v2])
                    a = TSC(rs(7), rs(3), 1e-6, ALU.add, waits=[a])
                    s1 = ACTI(rs(4), rs(7).bitcast(F32), AF.Ln, waits=[a])
                    s1 = ACTI(rs(4), rs(4).bitcast(F32), AF.Exp, scale=-0.5,
                              waits=[s1])
                    bs = MM(bank(2)[:, 0:TSB], on1[:, :], rs(4), True, True,
                            waits=[s1])
                    hn_last = None
                    for dc in range(DC):
                        t1 = TT(hnt[:, :], xat2[dc][:, :].bitcast(F32),
                                bank(2)[:, 0:TSB], ALU.mult,
                                waits=[bs, hn_last])
                        hn_last = ACTI(hnb[dc][:, :], hnt[:, :], AF.Identity,
                                       scale=fnw_s[:, dc:dc + 1], waits=[t1])
                    gu_rd = [None, None]
                    sg_fr = [None, None]
                    pv = None
                    for mc in range(HC):
                        rows_n = 128 if mc < HC - 1 else H_LAST
                        mg = mu = None
                        for k in range(8):
                            mg = MM(bank(4)[0:rows_n, 0:TSB],
                                    gup_s[:, k * 5504 + mc * 128:
                                          k * 5504 + mc * 128 + rows_n],
                                    hnb[k][:, :], k == 0, k == 7,
                                    waits=[hn_last, w_tok, gu_rd[0]])
                            mu = MM(bank(5)[0:rows_n, 0:TSB],
                                    gup_s[:, k * 5504 + H + mc * 128:
                                          k * 5504 + H + mc * 128 + rows_n],
                                    hnb[k][:, :], k == 0, k == 7,
                                    waits=[gu_rd[1]])
                        sv = ACTI(sgt2[mc % 2][0:rows_n, :],
                                  bank(4)[0:rows_n, 0:TSB], AF.Silu,
                                  waits=[mg, sg_fr[mc % 2]])
                        pv = TT(slab[0:rows_n, mc * TSB:(mc + 1) * TSB],
                                sgt2[mc % 2][0:rows_n, :],
                                bank(5)[0:rows_n, 0:TSB], ALU.mult,
                                waits=[sv, mu])
                        gu_rd[0] = sv
                        gu_rd[1] = pv
                        sg_fr[mc % 2] = pv
                    prod_all = pv
                    dn_rd = [None, None]
                    ao_toks = []
                    for dm in range(DC):
                        bi = 6 + dm % 2
                        md = None
                        for kc in range(HC):
                            rows_n = 128 if kc < HC - 1 else H_LAST
                            md = MM(bank(bi)[:, 0:TSB],
                                    dn_s[0:rows_n, kc * 1024 + dm * 128:
                                         kc * 1024 + (dm + 1) * 128],
                                    slab[0:rows_n, kc * TSB:(kc + 1) * TSB],
                                    kc == 0, kc == HC - 1,
                                    waits=[prod_all, dn_rd[dm % 2]])
                        av = TT(xat2[dm][:, :], bank(bi)[:, 0:TSB],
                                xat2[dm][:, :].bitcast(F32), ALU.add,
                                waits=[md])
                        dn_rd[dm % 2] = av
                        ao_toks.append(av)
                    m1_ = m2_ = None
                    sq_rd = [None, None]
                    for dc in range(DC):
                        sv = ACTI(sq2[dc % 2][:, :],
                                  xat2[dc][:, :].bitcast(F32), AF.Square,
                                  waits=[ao_toks[dc], sq_rd[dc % 2]])
                        m1_ = MM(bank(0)[0:1, 0:TSB], onc[:, :],
                                 xat2[dc][:, :], dc == 0, dc == DC - 1,
                                 waits=[ao_toks[dc]])
                        m2_ = MM(bank(1)[0:1, 0:TSB], onc[:, :],
                                 sq2[dc % 2][:, :], dc == 0, dc == DC - 1,
                                 waits=[sv])
                        sq_rd[dc % 2] = m2_
                    v1 = CP(rs(0), bank(0)[0:1, 0:TSB], waits=[m1_, m2_])
                    v2 = CP(rs(1), bank(1)[0:1, 0:TSB], waits=[m2_])
                    a = TSC(rs(2), rs(0), 1.0 / D, ALU.mult, waits=[v1])
                    a = TT(rs(7), rs(2), rs(2), ALU.mult, waits=[a])
                    a = STT(rs(3), rs(1), 1.0 / D, rs(7), ALU.mult,
                            ALU.subtract, waits=[a, v2])
                    a = TSC(rs(7), rs(3), 1e-5, ALU.add, waits=[a])
                    s1 = ACTI(rs(4), rs(7).bitcast(F32), AF.Ln, waits=[a])
                    s1 = ACTI(rs(4), rs(4).bitcast(F32), AF.Exp, scale=-0.5,
                              waits=[s1])
                    bm = MM(bank(2)[:, 0:TSB], on1[:, :], rs(2), True, True,
                            waits=[a])
                    br = MM(bank(3)[:, 0:TSB], on1[:, :], rs(4), True, True,
                            waits=[s1])
                    xf_tk = []
                    for dc in range(DC):
                        xf_tk.append(bld.dma(
                            "sync", xfmt[dc][:, :],
                            xfm_sp.ap()[dc * 128:(dc + 1) * 128, tb:tb + TSB]
                            .bitcast(F32)))
                    x1_toks = []
                    for dc in range(DC):
                        t1 = TT(tmp2[:, :], xat2[dc][:, :].bitcast(F32),
                                bank(2)[:, 0:TSB], ALU.subtract,
                                waits=[bm, x1_toks[-1] if x1_toks else None])
                        t2 = TT(tmp2[:, :], tmp2[:, :], bank(3)[:, 0:TSB],
                                ALU.mult, waits=[t1, br])
                        t3 = TT(x1t[dc][:, :], tmp2[:, :], xfmt[dc][:, :],
                                ALU.add, waits=[t2, xf_tk[dc]])
                        bld.dma("sync",
                                x1_sp.ap()[dc * 128:(dc + 1) * 128,
                                           tb:tb + TSB],
                                x1t[dc][:, :], waits=[t3])
                        x1_toks.append(t3)
                    m1_ = m2_ = None
                    sq_rd = [None, None]
                    for dc in range(DC):
                        sv = ACTI(sq2[dc % 2][:, :], x1t[dc][:, :].bitcast(F32),
                                  AF.Square, waits=[x1_toks[dc], sq_rd[dc % 2]])
                        m1_ = MM(bank(0)[0:1, 0:TSB], onc[:, :],
                                 x1t[dc][:, :], dc == 0, dc == DC - 1,
                                 waits=[x1_toks[dc]])
                        m2_ = MM(bank(1)[0:1, 0:TSB], onc[:, :],
                                 sq2[dc % 2][:, :], dc == 0, dc == DC - 1,
                                 waits=[sv])
                        sq_rd[dc % 2] = m2_
                    v1 = CP(rs(0), bank(0)[0:1, 0:TSB], waits=[m1_, m2_])
                    v2 = CP(rs(1), bank(1)[0:1, 0:TSB], waits=[m2_])
                    a = TSC(rs(2), rs(0), 1.0 / D, ALU.mult, waits=[v1])
                    a = TT(rs(7), rs(2), rs(2), ALU.mult, waits=[a])
                    a = STT(rs(3), rs(1), 1.0 / D, rs(7), ALU.mult,
                            ALU.subtract, waits=[a, v2])
                    a = TSC(rs(7), rs(3), 1e-5, ALU.add, waits=[a])
                    s1 = ACTI(rs(4), rs(7).bitcast(F32), AF.Ln, waits=[a])
                    s1 = ACTI(rs(4), rs(4).bitcast(F32), AF.Exp, scale=-0.5,
                              waits=[s1])
                    bm = MM(bank(2)[:, 0:TSB], on1[:, :], rs(2), True, True,
                            waits=[a])
                    br = MM(bank(3)[:, 0:TSB], on1[:, :], rs(4), True, True,
                            waits=[s1])
                    u1_fr = None
                    for dc in range(DC):
                        t1 = TT(tmp2[:, :], x1t[dc][:, :].bitcast(F32),
                                bank(2)[:, 0:TSB], ALU.subtract,
                                waits=[bm, u1_fr])
                        t2 = TT(u1b[:, :], tmp2[:, :], bank(3)[:, 0:TSB],
                                ALU.mult, waits=[t1, br])
                        u1_fr = bld.dma(
                            "sync",
                            u1_sp.ap()[dc * 128:(dc + 1) * 128, tb:tb + TSB],
                            u1b[:, :], waits=[t2])
        bld.barrier()

        with ExitStack() as pst:
            TS3 = 512
            u1r = [[sb(pst, f"u1r{w}_{c}", [128, TW], BF16) for c in range(DC)]
                   for w in range(NW)]
            ptl = [[sb(pst, f"ptl{w}_{m}", [128, TW], F32R) for m in range(DC)]
                   for w in range(NW)]
            cpjt = [sb(pst, f"cpjt{i}", [128, 1024], BF16) for i in range(EC)]
            wbig = sb(pst, "wbig", [128, 2048], BF16)
            slab3 = sb(pst, "slab3", [128, 16 * TS3], BF16)
            x1r = [sb(pst, f"x1r{i}", [128, TS3]) for i in range(DC)]
            sq3 = [sb(pst, f"sq3_{i}", [128, TS3], F32R) for i in range(2)]
            tmp3 = sb(pst, "tmp3", [128, TS3])
            ostg = sb(pst, "ostg", [128, 1024])

            u1tk = {}
            for w in range(NW if STOP >= 7 else 0):
                for cch in range(DC):
                    u1tk[(w, cch)] = bld.dma(
                        "sync", u1r[w][cch][:, :],
                        u1_sp.ap()[cch * 128:(cch + 1) * 128,
                                   w * TW:(w + 1) * TW])
            pt_wr = {}
            for g in range(8 if STOP >= 7 else 0):
                bld.barrier()
                wb_tok = bld.dma("sync", wbig[:, :],
                                 wbig_sp.ap()[:, g * 2048:(g + 1) * 2048])
                ctk = []
                for kcl in range(EC):
                    ctk.append(bld.dma(
                        "gpsimd", cpjt[kcl][:, :],
                        cpj.ap()[2048 * g + kcl * 128:
                                 2048 * g + (kcl + 1) * 128, :]))
                for w in range(NW):
                    for ts in range(TW // TS3):
                        zb_rd = [None, None]
                        gl_toks = []
                        for kcl in range(EC):
                            bi = kcl % 2
                            zm = MM(bank(bi)[:, 0:TS3],
                                    wbig[:, kcl * 128:(kcl + 1) * 128],
                                    u1r[w][g][:, ts * TS3:(ts + 1) * TS3],
                                    True, True,
                                    waits=[wb_tok, u1tk[(w, g)], zb_rd[bi]])
                            gv = ACTI(slab3[:, kcl * TS3:(kcl + 1) * TS3],
                                      bank(bi)[:, 0:TS3], AF.Gelu, waits=[zm])
                            zb_rd[bi] = gv
                            gl_toks.append(gv)
                        gl_all = gl_toks[-1]
                        cp_rd = [None, None]
                        for dm in range(DC):
                            bi = 2 + dm % 2
                            md = None
                            for kcl in range(EC):
                                md = MM(bank(bi)[:, 0:TS3],
                                        cpjt[kcl][:, dm * 128:(dm + 1) * 128],
                                        slab3[:, kcl * TS3:(kcl + 1) * TS3],
                                        kcl == 0, kcl == EC - 1,
                                        waits=[gl_all, ctk[kcl],
                                               cp_rd[dm % 2]])
                            dst = ptl[w][dm][:, ts * TS3:(ts + 1) * TS3]
                            if g == 0:
                                pv = CP(dst, bank(bi)[:, 0:TS3], waits=[md])
                            else:
                                pv = TT(dst, dst.bitcast(F32),
                                        bank(bi)[:, 0:TS3], ALU.add,
                                        waits=[md, pt_wr[(w, dm, ts)]])
                            cp_rd[dm % 2] = pv
                            pt_wr[(w, dm, ts)] = pv
            for w in range(NW if STOP >= 7 else 0):
                for ts in range(TW // TS3):
                    bld.barrier()
                    col = slice(ts * TS3, (ts + 1) * TS3)
                    tb = w * TW + ts * TS3
                    m1_ = m2_ = None
                    sq_rd = [None, None]
                    for dc in range(DC):
                        sv = ACTI(sq3[dc % 2][:, :],
                                  ptl[w][dc][:, col].bitcast(F32), AF.Square,
                                  waits=[sq_rd[dc % 2]])
                        m1_ = MM(bank(0)[0:1, 0:TS3], onc[:, :],
                                 ptl[w][dc][:, col], dc == 0, dc == DC - 1)
                        m2_ = MM(bank(1)[0:1, 0:TS3], onc[:, :],
                                 sq3[dc % 2][:, :], dc == 0, dc == DC - 1,
                                 waits=[sv])
                        sq_rd[dc % 2] = m2_
                    v1 = CP(rs(0, TS3), bank(0)[0:1, 0:TS3], waits=[m1_, m2_])
                    v2 = CP(rs(1, TS3), bank(1)[0:1, 0:TS3], waits=[m2_])
                    a = TSC(rs(2, TS3), rs(0, TS3), 1.0 / D, ALU.mult,
                            waits=[v1])
                    a = TT(rs(7, TS3), rs(2, TS3), rs(2, TS3), ALU.mult,
                           waits=[a])
                    a = STT(rs(3, TS3), rs(1, TS3), 1.0 / D, rs(7, TS3),
                            ALU.mult, ALU.subtract, waits=[a, v2])
                    a = TSC(rs(7, TS3), rs(3, TS3), 1e-5, ALU.add, waits=[a])
                    s1 = ACTI(rs(4, TS3), rs(7, TS3).bitcast(F32), AF.Ln,
                              waits=[a])
                    s1 = ACTI(rs(4, TS3), rs(4, TS3).bitcast(F32), AF.Exp,
                              scale=-0.5, waits=[s1])
                    bm = MM(bank(2)[:, 0:TS3], on1[:, :], rs(2, TS3),
                            True, True, waits=[a])
                    br = MM(bank(3)[:, 0:TS3], on1[:, :], rs(4, TS3),
                            True, True, waits=[s1])
                    x1tk = []
                    for dc in range(DC):
                        x1tk.append(bld.dma(
                            "sync", x1r[dc][:, :],
                            x1_sp.ap()[dc * 128:(dc + 1) * 128, tb:tb + TS3]
                            .bitcast(F32)))
                    x2_toks = []
                    for dc in range(DC):
                        t1 = TT(tmp3[:, :], ptl[w][dc][:, col].bitcast(F32),
                                bank(2)[:, 0:TS3], ALU.subtract,
                                waits=[bm, x2_toks[-1] if x2_toks else None])
                        t2 = TT(tmp3[:, :], tmp3[:, :], bank(3)[:, 0:TS3],
                                ALU.mult, waits=[t1, br])
                        t3 = TT(x1r[dc][:, :], tmp3[:, :], x1r[dc][:, :],
                                ALU.add, waits=[t2, x1tk[dc]])
                        x2_toks.append(t3)
                    x2_all = x2_toks[-1]
                    ev_fr = [None, None]
                    ostg_fr = None
                    for tt in range(TS3 // 128):
                        pi = 2 + tt % 2
                        trk = None
                        for dc in range(DC):
                            trk = TRA(psT[pi][:, dc * 128:(dc + 1) * 128],
                                      x1r[dc][:, tt * 128:(tt + 1) * 128],
                                      waits=[x2_all, ev_fr[tt % 2]])
                        ev = bld.op("scalar",
                                    lambda e, pi=pi: e.copy(ostg[:, :],
                                                            psT[pi][:, :]),
                                    waits=[trk, ostg_fr])
                        ev_fr[tt % 2] = bld.dma(
                            "sync",
                            out_t.ap()[tb + tt * 128:tb + (tt + 1) * 128, :],
                            ostg[:, :], waits=[ev])
                        ostg_fr = ev_fr[tt % 2]
        bld.barrier()
        bld.flush_barrier()

        with nc.allow_low_precision(reason="bf16 operands by design"), \
                nc.Block() as blk:
            @blk.sync
            def _(e):
                for f in bld.ops["sync"]:
                    f(e)

            @blk.gpsimd
            def _(e):
                for f in bld.ops["gpsimd"]:
                    f(e)

            @blk.vector
            def _(e):
                for f in bld.ops["vector"]:
                    f(e)

            @blk.scalar
            def _(e):
                for f in bld.ops["scalar"]:
                    f(e)

            @blk.tensor
            def _(e):
                for f in bld.ops["tensor"]:
                    f(e)
    return nc


def _prep(inputs):
    f = np.ascontiguousarray
    x = inputs["L0_attn_x"]
    ww = {}
    ww["wig"] = f(np.concatenate([inputs["L0_attn_in_proj_w"],
                                  inputs["L0_attn_gate_proj_w"]], 0).T)
    ww["dtT"] = f(inputs["L0_attn_dt_w"].T)
    ww["xpT"] = f(inputs["L0_attn_x_proj_w"].T.reshape(EC, 128, 2 * NST)
                  .transpose(1, 0, 2).reshape(128, EC * 2 * NST))
    ww["outT"] = f(inputs["L0_attn_out_proj_w"].T)
    ww["gupT"] = f(np.concatenate([inputs["L0_attn_ffn_gate_w"],
                                   inputs["L0_attn_ffn_up_w"]], 0).T)
    ww["dnT"] = f(inputs["L0_attn_ffn_down_w"].T)
    ww["cpj"] = f(inputs["L0_ffn_w_cproj"].reshape(D, 4, 64, 64)
                  .transpose(2, 1, 3, 0).reshape(16384, D))
    ww["makerJ"] = f(inputs["L0_ffn_rail_maker"].transpose(3, 0, 1, 2)
                     .reshape(NST, 4096))
    ww["upJ"] = f(inputs["L0_ffn_rail_up_proj"].transpose(3, 0, 1, 2)
                  .reshape(NST, 16384))
    ww["dtb"] = f(inputs["L0_attn_dt_b"].reshape(EC, 128).T)
    ww["dpv"] = f(inputs["L0_attn_Dp"].reshape(EC, 128).T)
    ww["n1w"] = f(inputs["L0_attn_norm1_w"].reshape(DC, 128).T)
    ww["fnw"] = f(inputs["L0_attn_ffn_norm_w"].reshape(DC, 128).T)
    ww["eye"] = np.eye(128, dtype=np.float32)
    selm = np.zeros((NST, NST * 128), np.float32)
    for n in range(NST):
        selm[n, n * 128:(n + 1) * 128] = 1.0
    ww["sel16"] = selm
    ww["ones1"] = np.ones((1, 128), np.float32)
    ww["onesc"] = np.ones((128, 1), np.float32)
    in_maps = []
    al = inputs["L0_attn_A_log"]
    for c in range(R):
        m = dict(ww)
        m["x"] = f(np.concatenate([x[0, c * TW:(c + 1) * TW],
                                   x[1, c * TW:(c + 1) * TW]], 0))
        m["a_log"] = f(al[c * 256:(c + 1) * 256].reshape(2, 128, NST)
                       .transpose(1, 0, 2).reshape(128, 2 * NST))
        in_maps.append(m)
    return in_maps


def kernel(**inputs):
    inputs = {k: np.asarray(v, dtype=np.float32) for k, v in inputs.items()}
    in_maps = _prep(inputs)
    nc = build(debug=False)
    res = run_bass_kernel_spmd(nc, in_maps, core_ids=list(range(R)))
    out = np.empty((2, L, D), np.float32)
    for c in range(R):
        o = res.results[c]["out"]
        out[0, c * TW:(c + 1) * TW] = o[:TW]
        out[1, c * TW:(c + 1) * TW] = o[TW:]
    return out


# version 02
# speedup vs baseline: 1.0096x; 1.0096x over previous
import os
os.environ.setdefault("TRN_ALLOW_ICI_ALL_TO_ALL", "1")
import numpy as np
import ml_dtypes
from collections import defaultdict
from contextlib import ExitStack

import concourse.bass as bass
import concourse.mybir as mybir
from concourse.bass_utils import run_bass_kernel_spmd

F32 = mybir.dt.float32
F32R = mybir.dt.float32r
BF16 = mybir.dt.bfloat16
AF = mybir.ActivationFunctionType
ALU = mybir.AluOpType

R = 8
NW = 2
L = 8192
D = 1024
E = 2048
NST = 16
H = 2752
TW = L // R
DC, EC = 8, 16
HC, H_LAST = 22, 64
TSA = 256
TSB = 256


class Bld:
    ENGINES = ("sync", "gpsimd", "vector", "scalar", "tensor")

    def __init__(self, nc):
        self.nc = nc
        self.ops = defaultdict(list)
        self.cnt = defaultdict(int)
        self.sems = {}
        self.pending = {}
        self.rr = {}

    KPOOL = {"sync": 8, "gpsimd": 4, "scalar": 4}

    def sem_names(self):
        names = ["vector", "scalar", "tensor", "gpsimd", "cc"]
        for q, k in self.KPOOL.items():
            names += [f"{q}d{i}" for i in range(k)]
        return names

    def fence(self, q):
        return [(f"{q}d{i}", self.cnt[f"{q}d{i}"])
                for i in range(self.KPOOL[q]) if self.cnt[f"{q}d{i}"] > 0]

    def fence_all(self):
        w = []
        for q in self.KPOOL:
            w += self.fence(q)
        return w

    def _emit(self, eng, waits, inc_sem, inc_n, fn):
        w = []
        for x in waits:
            if x is None:
                continue
            if isinstance(x, list):
                w.extend(x)
            else:
                w.append(x)
        w += self.pending.pop(eng, [])
        sems = self.sems

        def cl(e, fn=fn, w=w, inc_sem=inc_sem, inc_n=inc_n):
            for s, v in w:
                e.wait_ge(sems[s], v)
            ins = fn(e)
            if inc_sem is not None:
                ins.then_inc(sems[inc_sem], inc_n)
        self.ops[eng].append(cl)

    def op(self, eng, fn, waits=()):
        self.cnt[eng] += 1
        tok = (eng, self.cnt[eng])
        self._emit(eng, waits, eng, 1, fn)
        return tok

    def dma(self, q, out, in_, waits=()):
        i = self.rr.get(q, 0)
        self.rr[q] = (i + 1) % self.KPOOL[q]
        s = f"{q}d{i}"
        prev = (s, self.cnt[s])
        self.cnt[s] += 16
        tok = (s, self.cnt[s])
        self._emit(q, list(waits) + [prev], s, 16,
                   lambda e, out=out, in_=in_: e.dma_start(out=out, in_=in_))
        return tok

    def cc(self, kind, in_, out, waits=()):
        prev = ("cc", self.cnt["cc"])
        self.cnt["cc"] += 1
        tok = ("cc", self.cnt["cc"])
        self._emit("gpsimd", list(waits) + [prev], "cc", 1,
                   lambda e, k=kind, i=in_, o=out:
                   e.collective_compute(k, ALU.bypass, [list(range(R))],
                                        ins=[i.ap().opt()], outs=[o.ap().opt()]))
        return tok

    def barrier(self):
        snap = [(s, c) for s, c in self.cnt.items() if c > 0]
        for eng in self.ENGINES:
            self.pending[eng] = list(snap)

    def flush_barrier(self):
        sems = self.sems
        for eng in self.ENGINES:
            w = self.pending.pop(eng, [])
            if not w:
                continue

            def cl(e, w=w):
                for s, v in w:
                    e.wait_ge(sems[s], v)
            self.ops[eng].append(cl)


def build(debug=False):
    nc = bass.Bass("TRN2", target_bir_lowering=False)

    def din(name, shape, dt=F32):
        return nc.dram_tensor(name, shape, dt, kind="ExternalInput")

    def dout(name, shape, dt=F32):
        return nc.dram_tensor(name, shape, dt, kind="ExternalOutput")

    def dint(name, shape, dt=F32, dbg_ok=True):
        kind = "ExternalOutput" if (debug and dbg_ok) else "Internal"
        return nc.dram_tensor(name, shape, dt, kind=kind)

    x_in = din("x", [NW * TW, D])
    wig = din("wig", [D, 2 * E], F32R)
    dtT = din("dtT", [E, E], F32R)
    xpT = din("xpT", [128, EC * 2 * NST], F32R)
    outT = din("outT", [E, D], F32R)
    gupT = din("gupT", [D, 2 * H])
    dnT = din("dnT", [H, D])
    cpj = din("cpj", [16384, D])
    makerJ = din("makerJ", [NST, 4096], F32R)
    upJ = din("upJ", [NST, 16384], F32R)
    a_log = din("a_log", [128, 2 * NST])
    dtb = din("dtb", [128, EC])
    dpv = din("dpv", [128, EC])
    n1w = din("n1w", [128, DC])
    fnw = din("fnw", [128, DC])
    eye_c = din("eye", [128, 128])
    sel16 = din("sel16", [NST, NST * 128], F32R)
    ones1 = din("ones1", [1, 128], F32R)
    onesc = din("onesc", [128, 1], F32R)

    out_t = dout("out", [NW * TW, D])
    z_sp = dint("z_sp", [E, NW * TW], F32R)
    xfm_sp = dint("xfm_sp", [D, NW * TW], F32R)
    u0_sp = dint("u0_sp", [D, NW * TW])
    xa_sp = dint("xa_sp", [D, NW * TW], F32R)
    x1_sp = dint("x1_sp", [D, NW * TW], F32R)
    u1_sp = dint("u1_sp", [D, NW * TW], BF16, dbg_ok=False)
    wbig_sp = dint("wbig_sp", [128, 8 * 2048], BF16, dbg_ok=False)
    dt_i = [dint(f"dt_i{w}", [E, TW], dbg_ok=False) for w in range(NW)]
    dt_o = [dint(f"dt_o{w}", [E, TW], dbg_ok=False) for w in range(NW)]
    dz_i = [dint(f"dz_i{w}", [E, TW], dbg_ok=False) for w in range(NW)]
    dz_o = [dint(f"dz_o{w}", [E, TW], dbg_ok=False) for w in range(NW)]
    y_i = [dint(f"y_i{w}", [E, TW], dbg_ok=False) for w in range(NW)]
    y_o = [dint(f"y_o{w}", [E, TW], dbg_ok=False) for w in range(NW)]
    bc_i = [dint(f"bc_i{w}", [2 * NST, TW], dbg_ok=False) for w in range(NW)]
    bc_o = [dint(f"bc_o{w}", [R * 2 * NST, TW], dbg_ok=False) for w in range(NW)]
    if debug:
        dbg_y = [dout(f"dbg_y{w}", [E, TW]) for w in range(NW)]
        dbg_dt = [dout(f"dbg_dt{w}", [E, TW]) for w in range(NW)]
        dbg_dz = [dout(f"dbg_dz{w}", [E, TW]) for w in range(NW)]
        dbg_bc = [dout(f"dbg_bc{w}", [R * 2 * NST, TW]) for w in range(NW)]
        dbg_bci = [dout(f"dbg_bci{w}", [2 * NST, TW]) for w in range(NW)]
        dbg_zz = dout("dbg_zz", [128, 2 * TW])
        dbg_xp = dout("dbg_xp", [128, EC * 2 * NST])
        dbg_xp0 = dout("dbg_xp0", [128, EC * 2 * NST])
        dbg_btl = dout("dbg_btl", [NST, 2 * TW])

    bld = Bld(nc)
    stack = ExitStack()
    with stack:
        for s in bld.sem_names():
            bld.sems[s] = stack.enter_context(nc.semaphore(s))

        def sb(st, name, shape, dt=F32):
            return st.enter_context(nc.sbuf_tensor(name, shape, dt))

        psT = [stack.enter_context(nc.psum_tensor(f"ps{i}", [128, 1024], F32))
               for i in range(4)]

        def bank(i):
            return psT[i // 2][:, (i % 2) * 512:(i % 2 + 1) * 512]

        eye = sb(stack, "eye_s", [128, 128])
        sel = sb(stack, "sel_s", [NST, NST * 128], F32R)
        on1 = sb(stack, "on1_s", [1, 128], F32R)
        onc = sb(stack, "onc_s", [128, 1], F32R)
        an = sb(stack, "an_s", [128, 2 * NST])
        dtb_s = sb(stack, "dtb_s", [128, EC])
        dpv_s = sb(stack, "dpv_s", [128, EC])
        n1w_s = sb(stack, "n1w_s", [128, DC])
        fnw_s = sb(stack, "fnw_s", [128, DC])
        rows = sb(stack, "rows", [1, 4096], F32R)

        def rs(slot, width=TSA):
            return rows[0:1, slot * 512:slot * 512 + width]

        def TT(out, a, b, alu, waits=(), eng="vector"):
            return bld.op(eng, lambda e, out=out, a=a, b=b, alu=alu:
                          e.tensor_tensor(out, a, b, alu), waits)

        def CP(out, in_, waits=(), eng="vector"):
            return bld.op(eng, lambda e, out=out, in_=in_:
                          e.tensor_copy(out, in_), waits)

        def TSC(out, a, s1, alu, waits=(), eng="vector"):
            return bld.op(eng, lambda e, out=out, a=a, s1=s1, alu=alu:
                          e.tensor_single_scalar(out, a, s1, alu), waits)

        def STT(out, in0, sc, in1, op0, op1, waits=(), eng="vector"):
            return bld.op(eng,
                          lambda e, out=out, in0=in0, sc=sc, in1=in1,
                          op0=op0, op1=op1:
                          e.scalar_tensor_tensor(out, in0, sc, in1, op0, op1),
                          waits)

        def ACTI(out, in_, f, bias=0.0, scale=1.0, waits=()):
            return bld.op("scalar",
                          lambda e, out=out, in_=in_, f=f, b=bias, s=scale:
                          e.activation(out, in_, f, bias=b, scale=s), waits)

        def MM(out, lhsT, rhs, start, stop, waits=()):
            return bld.op("tensor",
                          lambda e, out=out, l=lhsT, r=rhs, st=start, sp=stop:
                          e.matmul(out, l, r, start=st, stop=sp), waits)

        def TRA(out, in_, waits=()):
            return bld.op("tensor",
                          lambda e, out=out, in_=in_:
                          e.transpose(out, in_, eye[:, :]), waits)

        def SCAN(out, d0, d1, init, waits=()):
            return bld.op("vector",
                          lambda e, out=out, d0=d0, d1=d1, init=init:
                          e.tensor_tensor_scan(out, d0, d1, init,
                                               ALU.mult, ALU.add), waits)

        def MSET(t, v, waits=(), eng="vector"):
            return bld.op(eng, lambda e, t=t, v=v: e.memset(t, v), waits)

        def stats_ln(src_tiles, sq_pair, width, with_mean=True):
            m1 = m2 = None
            for dc in range(DC):
                sv = ACTI(sq_pair[dc % 2][:, 0:width],
                          src_tiles[dc][:, 0:width].bitcast(F32), AF.Square)
                if with_mean:
                    m1 = MM(bank(0)[0:1, 0:width], onc[:, :],
                            src_tiles[dc][:, 0:width], dc == 0, dc == DC - 1)
                m2 = MM(bank(1)[0:1, 0:width], onc[:, :],
                        sq_pair[dc % 2][:, 0:width], dc == 0, dc == DC - 1,
                        waits=[sv])
            toks = []
            if with_mean:
                v1 = CP(rs(0, width), bank(0)[0:1, 0:width], waits=[m1])
                toks.append(v1)
            v2 = CP(rs(1, width), bank(1)[0:1, 0:width], waits=[m2])
            toks.append(v2)
            if with_mean:
                a = TSC(rs(2, width), rs(0, width), 1.0 / D, ALU.mult,
                        waits=[v1])
                a = TT(rs(7, width), rs(2, width), rs(2, width), ALU.mult,
                       waits=[a])
                a = STT(rs(3, width), rs(1, width), 1.0 / D, rs(7, width),
                        ALU.mult, ALU.subtract, waits=[a, v2])
                a = TSC(rs(7, width), rs(3, width), 1e-5, ALU.add, waits=[a])
            else:
                a = TSC(rs(3, width), rs(1, width), 1.0 / D, ALU.mult,
                        waits=[v2])
                a = TSC(rs(7, width), rs(3, width), 1e-6, ALU.add, waits=[a])
            s1 = ACTI(rs(4, width), rs(7, width).bitcast(F32), AF.Ln,
                      waits=[a])
            s1 = ACTI(rs(4, width), rs(4, width).bitcast(F32), AF.Exp,
                      scale=-0.5, waits=[s1])
            return s1, a

        for t, src in ((eye, eye_c), (sel, sel16), (on1, ones1), (onc, onesc),
                       (an, a_log), (dtb_s, dtb), (dpv_s, dpv),
                       (n1w_s, n1w), (fnw_s, fnw)):
            bld.dma("sync", t[:, :], src.ap())
        ld = bld.fence("sync")
        t_ = ACTI(an[:, :], an[:, :], AF.Exp, waits=[ld])
        bld.op("vector", lambda e: e.tensor_scalar_mul(an[:, :], an[:, :], -1.0),
               waits=[t_])

        STOP = int(os.environ.get("MK_STOP", "99"))
        with ExitStack() as pst:
            mkj = sb(pst, "mkj", [NST, 4096], F32R)
            upj = sb(pst, "upj", [NST, 16384], F32R)
            zt = sb(pst, "zt", [128, 2048], BF16)
            stg = sb(pst, "stg", [NST, 8 * 64], BF16)
            bld.dma("sync", mkj[:, :], makerJ.ap())
            bld.dma("sync", upj[:, :], upJ.ap())
            wtok = bld.fence("sync")
            zv = MSET(zt[:, :], 0.0)
            for g in range(8):
                bld.dma("sync", wbig_sp.ap()[:, g * 2048:(g + 1) * 2048],
                        zt[:, :], waits=[zv])
            zdone = bld.fence("sync")
            ev_tok = [None] * 8
            dm_tok = [None] * 8
            for h in range(4):
                for r in range(64):
                    hr = h * 64 + r
                    slot = hr % 8
                    pb = bank(slot)[0:NST, 0:64]
                    tt = MM(pb, mkj[:, hr * 16:hr * 16 + 16],
                            upj[:, hr * 64:hr * 64 + 64], True, True,
                            waits=[wtok, ev_tok[slot]])
                    ss_ = stg[0:NST, slot * 64:(slot + 1) * 64]
                    vv = CP(ss_, pb, waits=[tt, dm_tok[slot]])
                    g = r // 8
                    kcl = 2 * (r % 8) + h // 2
                    col = g * 2048 + kcl * 128 + (h % 2) * 64
                    dm_tok[slot] = bld.dma(
                        "sync",
                        wbig_sp.ap()[16 * (r % 8):16 * (r % 8) + 16,
                                     col:col + 64],
                        ss_, waits=[vv, zdone])
                    ev_tok[slot] = vv
        bld.barrier()

        with ExitStack() as pst:
            wig_s = sb(pst, "wig_s", [128, 8 * 4096], F32R)
            for k in range(8):
                bld.dma("scalar", wig_s[:, k * 4096:(k + 1) * 4096],
                        wig.ap()[k * 128:(k + 1) * 128, :])
            wig_tok = bld.fence("scalar")
            stage = [sb(pst, f"xstg{i}", [128, D]) for i in range(2)]
            xfm = [sb(pst, f"xfm{i}", [128, TSA], F32R) for i in range(DC)]
            sqp = [sb(pst, f"sq{i}", [128, TSA], F32R) for i in range(2)]
            u0t = [sb(pst, f"u0t{i}", [128, TSA]) for i in range(2)]
            h1t = sb(pst, "h1t", [128, TSA])
            h1 = [sb(pst, f"h1_{i}", [128, TSA], F32R) for i in range(DC)]
            sgt = [sb(pst, f"sgt{i}", [128, TSA]) for i in range(2)]
            zot = [sb(pst, f"zot{i}", [128, TSA], F32R) for i in range(2)]

            for w in range(NW if STOP >= 1 else 0):
                for ts in range(TW // TSA):
                    bld.barrier()
                    tb = w * TW + ts * TSA
                    ntt = TSA // 128
                    ltok = []
                    for tt in range(ntt):
                        ltok.append(bld.dma(
                            "sync", stage[tt][:, :],
                            x_in.ap()[tb + tt * 128:tb + (tt + 1) * 128, :]))
                    ev8 = []
                    tra_ev = [None, None]
                    for dc in range(DC):
                        bi = 4 + (dc % 2)
                        ttk = None
                        for tt in range(ntt):
                            ttk = TRA(bank(bi)[:, tt * 128:(tt + 1) * 128],
                                      stage[tt][:, dc * 128:(dc + 1) * 128],
                                      waits=[ltok[tt], tra_ev[dc % 2]])
                        ev = bld.op("scalar",
                                    lambda e, dc=dc, bi=bi:
                                    e.copy(xfm[dc][:, :], bank(bi)[:, 0:TSA]),
                                    waits=[ttk])
                        tra_ev[dc % 2] = ev
                        ev8.append(ev)
                        bld.dma("sync",
                                xfm_sp.ap()[dc * 128:(dc + 1) * 128,
                                            tb:tb + TSA],
                                xfm[dc][:, :], waits=[ev])
                    m1 = m2 = None
                    sq_rd = [None, None]
                    for dc in range(DC):
                        sv = ACTI(sqp[dc % 2][:, :], xfm[dc][:, :].bitcast(F32),
                                  AF.Square, waits=[ev8[dc], sq_rd[dc % 2]])
                        m1 = MM(bank(0)[0:1, 0:TSA], onc[:, :], xfm[dc][:, :],
                                dc == 0, dc == DC - 1, waits=[ev8[dc]])
                        m2 = MM(bank(1)[0:1, 0:TSA], onc[:, :],
                                sqp[dc % 2][:, :], dc == 0, dc == DC - 1,
                                waits=[sv])
                        sq_rd[dc % 2] = m2
                    v1 = CP(rs(0), bank(0)[0:1, 0:TSA], waits=[m1, m2])
                    v2 = CP(rs(1), bank(1)[0:1, 0:TSA], waits=[m2])
                    a = TSC(rs(2), rs(0), 1.0 / D, ALU.mult, waits=[v1])
                    a = TT(rs(7), rs(2), rs(2), ALU.mult, waits=[a])
                    a = STT(rs(3), rs(1), 1.0 / D, rs(7), ALU.mult,
                            ALU.subtract, waits=[a, v2])
                    a = TSC(rs(7), rs(3), 1e-5, ALU.add, waits=[a])
                    s1 = ACTI(rs(4), rs(7).bitcast(F32), AF.Ln, waits=[a])
                    s1 = ACTI(rs(4), rs(4).bitcast(F32), AF.Exp, scale=-0.5,
                              waits=[s1])
                    mr = TT(rs(6), rs(2), rs(4), ALU.mult, waits=[s1])
                    rc = bld.op("vector", lambda e: e.reciprocal(rs(0), rs(7)),
                                waits=[s1])
                    q = TT(rs(1), rs(3), rs(0), ALU.mult, waits=[rc])
                    q = TSC(rs(7), rs(1), 1e-6, ALU.add, waits=[q])
                    s2 = ACTI(rs(5), rs(7).bitcast(F32), AF.Ln, waits=[q])
                    s2 = ACTI(rs(5), rs(5).bitcast(F32), AF.Exp, scale=-0.5,
                              waits=[s2])
                    b_r1 = MM(bank(2)[:, 0:TSA], on1[:, :], rs(4), True, True,
                              waits=[s1])
                    b_mr = MM(bank(3)[:, 0:TSA], on1[:, :], rs(6), True, True,
                              waits=[mr])
                    b_s2 = MM(bank(4)[:, 0:TSA], on1[:, :], rs(5), True, True,
                              waits=[s2, tra_ev[0]])
                    h1_last = None
                    u0_fr = [None, None]
                    for dc in range(DC):
                        ui = dc % 2
                        a1 = TT(u0t[ui][:, :], xfm[dc][:, :].bitcast(F32),
                                bank(2)[:, 0:TSA], ALU.mult,
                                waits=[b_r1, u0_fr[ui]])
                        a2 = TT(u0t[ui][:, :], u0t[ui][:, :], bank(3)[:, 0:TSA],
                                ALU.subtract, waits=[a1, b_mr])
                        u0_fr[ui] = bld.dma(
                            "sync",
                            u0_sp.ap()[dc * 128:(dc + 1) * 128, tb:tb + TSA],
                            u0t[ui][:, :], waits=[a2])
                        a3 = TT(h1t[:, :], u0t[ui][:, :], bank(4)[:, 0:TSA],
                                ALU.mult, waits=[a2, b_s2, h1_last])
                        h1_last = ACTI(h1[dc][:, :], h1t[:, :], AF.Identity,
                                       scale=n1w_s[:, dc:dc + 1], waits=[a3])
                    zg_rd = {0: v1, 1: v2, 6: None, 7: None}
                    zo_fr = [None, None]
                    for ei in range(EC):
                        zb, gb = (6, 7) if ei % 2 == 0 else (0, 1)
                        mmz = mmg = None
                        for k in range(8):
                            mmz = MM(bank(zb)[:, 0:TSA],
                                     wig_s[:, k * 4096 + ei * 128:
                                           k * 4096 + (ei + 1) * 128],
                                     h1[k][:, :], k == 0, k == 7,
                                     waits=[h1_last, zg_rd[zb], wig_tok])
                            mmg = MM(bank(gb)[:, 0:TSA],
                                     wig_s[:, k * 4096 + 2048 + ei * 128:
                                           k * 4096 + 2048 + (ei + 1) * 128],
                                     h1[k][:, :], k == 0, k == 7,
                                     waits=[zg_rd[gb]])
                        sgv = ACTI(sgt[ei % 2][:, :], bank(gb)[:, 0:TSA],
                                   AF.Sigmoid, waits=[mmg, zo_fr[ei % 2]])
                        zv = TT(zot[ei % 2][:, :], bank(zb)[:, 0:TSA],
                                sgt[ei % 2][:, :], ALU.mult, waits=[mmz, sgv])
                        zo_fr[ei % 2] = bld.dma(
                            "sync",
                            z_sp.ap()[ei * 128:(ei + 1) * 128, tb:tb + TSA],
                            zot[ei % 2][:, :], waits=[zv])
                        zg_rd[zb] = zv
                        zg_rd[gb] = sgv
        bld.barrier()

        with ExitStack() as pst:
            if not os.environ.get("MK_A2_NODT"):
                dt_s = sb(pst, "dt_s", [128, 16 * 2048], F32R)
                for k in range(EC):
                    bld.dma("scalar", dt_s[:, k * 2048:(k + 1) * 2048],
                            dtT.ap()[k * 128:(k + 1) * 128, :])
            xp_s = sb(pst, "xp_s", [128, 16 * 32], F32R)
            xp_ld = bld.dma("scalar", xp_s[:, :], xpT.ap())
            if debug:
                bld.dma("sync", dbg_xp0.ap(), xp_s[:, :].bitcast(F32),
                        waits=[xp_ld])
            w_tok = bld.fence("scalar")
            TS2 = 256
            NZB = 4 if os.environ.get("MK_A2_ZZ4") else 1
            zz_all = [[sb(pst, f"zz{j}_{i}", [128, TS2], F32R)
                       for i in range(EC)] for j in range(NZB)]
            pt = [sb(pst, f"pt{i}", [128, TS2]) for i in range(2)]
            ab = [sb(pst, f"ab{i}", [128, TS2]) for i in range(2)]
            rl = [sb(pst, f"rl{i}", [128, TS2]) for i in range(2)]
            dtv = [sb(pst, f"dtv{i}", [128, TS2]) for i in range(2)]
            dzv = [sb(pst, f"dzv{i}", [128, TS2]) for i in range(2)]
            bt_l = sb(pst, "bt_l", [NST, TW])
            ct_l = sb(pst, "ct_l", [NST, TW])

            for w in range(NW if STOP >= 2 else 0):
                bc_ev = [None, None]
                for ts in range(TW // TS2):
                    bld.barrier()
                    zz = zz_all[ts % len(zz_all)]
                    tb = w * TW + ts * TS2
                    ztk = []
                    for k in range(EC):
                        ztk.append(bld.dma(
                            "sync", zz[k][:, :],
                            z_sp.ap()[k * 128:(k + 1) * 128, tb:tb + TS2]))
                    zdone = bld.fence("sync")
                    mb = mc_ = None
                    for k in range(EC):
                        mb = MM(bank(2)[0:NST, 0:TS2],
                                xp_s[:, k * 32:k * 32 + 16], zz[k][:, :],
                                k == 0, k == EC - 1, waits=[ztk[k], w_tok])
                        mc_ = MM(bank(3)[0:NST, 0:TS2],
                                 xp_s[:, k * 32 + 16:k * 32 + 32], zz[k][:, :],
                                 k == 0, k == EC - 1)
                    bc_ev[0] = CP(bt_l[:, ts * TS2:(ts + 1) * TS2],
                                  bank(2)[0:NST, 0:TS2], waits=[mb, mc_])
                    bc_ev[1] = CP(ct_l[:, ts * TS2:(ts + 1) * TS2],
                                  bank(3)[0:NST, 0:TS2], waits=[mc_])
                    if debug:
                        bld.dma("sync",
                                dbg_zz.ap()[:, w * TW + ts * TS2:
                                            w * TW + (ts + 1) * TS2],
                                zz[0][:, :].bitcast(F32), waits=[bc_ev[1]])
                    dtbank_rd = [None, None]
                    tmp_fr = [None, None]
                    for ei in range(0 if os.environ.get("MK_A2_NODT") else EC):
                        bi = ei % 2
                        mmd = None
                        for k in range(EC):
                            mmd = MM(bank(bi)[:, 0:TS2],
                                     dt_s[:, k * 2048 + ei * 128:
                                          k * 2048 + (ei + 1) * 128],
                                     zz[k][:, :], k == 0, k == EC - 1,
                                     waits=[zdone, w_tok, dtbank_rd[bi]])
                        p1 = ACTI(pt[bi][:, :], bank(bi)[:, 0:TS2], AF.Identity,
                                  bias=dtb_s[:, ei:ei + 1],
                                  waits=[mmd, tmp_fr[bi]])
                        p2 = ACTI(ab[bi][:, :], bank(bi)[:, 0:TS2], AF.Abs,
                                  bias=dtb_s[:, ei:ei + 1], waits=[mmd, p1])
                        dtbank_rd[bi] = p2
                        p3 = ACTI(ab[bi][:, :], ab[bi][:, :], AF.Exp,
                                  scale=-1.0, waits=[p2])
                        p4 = ACTI(ab[bi][:, :], ab[bi][:, :], AF.Ln, bias=1.0,
                                  waits=[p3])
                        v1 = TSC(rl[bi][:, :], pt[bi][:, :], 0.0, ALU.max,
                                 waits=[p1, tmp_fr[bi]])
                        v2 = TT(dtv[bi][:, :], rl[bi][:, :], ab[bi][:, :],
                                ALU.add, waits=[v1, p4, tmp_fr[bi]])
                        d1 = bld.dma("sync",
                                     dt_i[w].ap()[ei * 128:(ei + 1) * 128,
                                                  ts * TS2:(ts + 1) * TS2],
                                     dtv[bi][:, :], waits=[v2])
                        v3 = TT(dzv[bi][:, :], dtv[bi][:, :],
                                zz[ei][:, :].bitcast(F32), ALU.mult,
                                waits=[v2, tmp_fr[bi]])
                        d2 = bld.dma("sync",
                                     dz_i[w].ap()[ei * 128:(ei + 1) * 128,
                                                  ts * TS2:(ts + 1) * TS2],
                                     dzv[bi][:, :], waits=[v3])
                        tmp_fr[bi] = d2
                bld.dma("sync", bc_i[w].ap()[0:NST, :], bt_l[:, :],
                        waits=[bc_ev[0]])
                if debug:
                    bld.dma("sync", dbg_btl.ap()[:, w * TW:(w + 1) * TW],
                            bt_l[:, :], waits=[bc_ev[0]])
                    if w == NW - 1:
                        bld.dma("sync", dbg_xp.ap(), xp_s[:, :].bitcast(F32))
                bld.dma("sync", bc_i[w].ap()[NST:2 * NST, :], ct_l[:, :],
                        waits=[bc_ev[1]])
                spill_done = bld.fence("sync")
                bld.cc("AllToAll", dt_i[w], dt_o[w], waits=[spill_done])
                bld.cc("AllToAll", dz_i[w], dz_o[w])
                bld.cc("AllGather", bc_i[w], bc_o[w])
        bld.barrier()

        with ExitStack() as pst:
            dtt = [sb(pst, f"dtt{i}", [128, TW]) for i in range(2)]
            dzt = [sb(pst, f"dzt{i}", [128, TW]) for i in range(2)]
            dat = [sb(pst, f"dat{i}", [128, TW]) for i in range(2)]
            dbz = [sb(pst, f"dbz{i}", [128, TW]) for i in range(2)]
            ht = [sb(pst, f"ht{i}", [128, TW]) for i in range(2)]
            ymt = [sb(pst, f"ymt{i}", [128, TW]) for i in range(2)]
            yac = [sb(pst, f"yac{i}", [128, TW]) for i in range(2)]
            btile = sb(pst, "btile", [NST, TW], F32R)
            ctile = sb(pst, "ctile", [NST, TW], F32R)
            carry = [sb(pst, f"carry{i}", [128, NST]) for i in range(2)]

            for w in range(NW if STOP >= 3 else 0):
                cctok = ("cc", 3 * (w + 1))
                bld.barrier()
                ct_toks = [MSET(carry[0][:, :], 0.0),
                           MSET(carry[1][:, :], 0.0)]
                carry_tok = ct_toks
                for c in range(R):
                    bld.barrier()
                    dtk, dzk = [], []
                    for dc in range(2):
                        dtk.append(bld.dma(
                            "sync", dtt[dc][:, :],
                            dt_o[w].ap()[c * 256 + dc * 128:
                                         c * 256 + (dc + 1) * 128, :],
                            waits=[cctok]))
                        dzk.append(bld.dma(
                            "sync", dzt[dc][:, :],
                            dz_o[w].ap()[c * 256 + dc * 128:
                                         c * 256 + (dc + 1) * 128, :],
                            waits=[cctok]))
                    btk = bld.dma("sync", btile[:, :],
                                  bc_o[w].ap()[32 * c:32 * c + NST, :]
                                  .bitcast(F32R), waits=[cctok])
                    ctk = bld.dma("sync", ctile[:, :],
                                  bc_o[w].ap()[32 * c + NST:32 * c + 2 * NST, :]
                                  .bitcast(F32R), waits=[cctok])
                    bb_rd = [None, None]
                    cc_rd = [None, None]
                    sc_tok = [None, None]
                    cv_tok = [None, None]
                    ya_tok = [None, None]
                    yac_fr = [None, None]
                    for n in range(NST):
                        p = n % 2
                        bbm = ccm = None
                        for hf in range(2):
                            bbm = MM(psT[2 * p][:, hf * 512:(hf + 1) * 512],
                                     sel[:, n * 128:(n + 1) * 128],
                                     btile[:, hf * 512:(hf + 1) * 512],
                                     True, True, waits=[btk, bb_rd[p]])
                            ccm = MM(psT[2 * p + 1][:, hf * 512:(hf + 1) * 512],
                                     sel[:, n * 128:(n + 1) * 128],
                                     ctile[:, hf * 512:(hf + 1) * 512],
                                     True, True, waits=[ctk, cc_rd[p]])
                        for dc in range(2):
                            i2 = (2 * n + dc) % 2
                            av = ACTI(dat[i2][:, :], dtt[dc][:, :], AF.Exp,
                                      scale=an[:, dc * 16 + n:dc * 16 + n + 1],
                                      waits=[dtk[dc], sc_tok[i2]])
                            bv = TT(dbz[i2][:, :], dzt[dc][:, :],
                                    psT[2 * p][:, :], ALU.mult,
                                    waits=[dzk[dc], bbm, sc_tok[i2]])
                            bb_rd[p] = bv
                            hv = SCAN(ht[i2][:, :], dat[i2][:, :],
                                      dbz[i2][:, :], carry[dc][:, n:n + 1],
                                      waits=[av, bv, carry_tok[dc],
                                             ya_tok[i2], cv_tok[i2]])
                            sc_tok[i2] = hv
                            cv = CP(carry[dc][:, n:n + 1], ht[i2][:, TW - 1:TW],
                                    waits=[hv])
                            cv_tok[i2] = cv
                            carry_tok[dc] = cv
                            yv = TT(ymt[i2][:, :], ht[i2][:, :],
                                    psT[2 * p + 1][:, :], ALU.mult,
                                    waits=[hv, ccm, ya_tok[i2]])
                            cc_rd[p] = yv
                            if n == 0:
                                ya = CP(yac[dc][:, :], ymt[i2][:, :],
                                        waits=[yv, yac_fr[dc]], eng="gpsimd")
                            else:
                                ya = TT(yac[dc][:, :], yac[dc][:, :],
                                        ymt[i2][:, :], ALU.add,
                                        waits=[yv, yac_fr[dc]], eng="gpsimd")
                            ya_tok[i2] = ya
                            yac_fr[dc] = ya
                    for dc in range(2):
                        bld.dma("sync",
                                y_i[w].ap()[c * 256 + dc * 128:
                                            c * 256 + (dc + 1) * 128, :],
                                yac[dc][:, :], waits=[yac_fr[dc]])
                ysp = bld.fence("sync")
                bld.cc("AllToAll", y_i[w], y_o[w], waits=[ysp])
        bld.barrier()
        if debug:
            with ExitStack() as pst:
                tb_ = sb(pst, "dbgt", [128, TW])
                prev = None
                for w in range(NW if STOP >= 4 else 0):
                    for src_t, dst_t, nch in ((y_o[w], dbg_y[w], EC),
                                              (dt_o[w], dbg_dt[w], EC),
                                              (dz_o[w], dbg_dz[w], EC),
                                              (bc_o[w], dbg_bc[w], 2)):
                        pass
                    for src_t, dst_t, nch in ((y_o[w], dbg_y[w], EC),
                                              (dt_o[w], dbg_dt[w], EC),
                                              (dz_o[w], dbg_dz[w], EC),
                                              (bc_o[w], dbg_bc[w], 2)):
                        for e_ in range(nch):
                            t1 = bld.dma("sync", tb_[:, :],
                                         src_t.ap()[e_ * 128:(e_ + 1) * 128, :],
                                         waits=[prev])
                            prev = bld.dma(
                                "sync",
                                dst_t.ap()[e_ * 128:(e_ + 1) * 128, :],
                                tb_[:, :], waits=[t1])
                    t1 = bld.dma("sync", tb_[0:32, :], bc_i[w].ap()[:, :],
                                 waits=[prev])
                    prev = bld.dma("sync", dbg_bci[w].ap()[:, :],
                                   tb_[0:32, :], waits=[t1])
            bld.barrier()

        with ExitStack() as pst:
            ot_s = sb(pst, "ot_s", [128, 16 * 1024], F32R)
            for k in range(EC):
                bld.dma("scalar", ot_s[:, k * 1024:(k + 1) * 1024],
                        outT.ap()[k * 128:(k + 1) * 128, :])
            w_tok = bld.fence("scalar")
            yy = [sb(pst, f"yy{i}", [128, TSB]) for i in range(EC)]
            zz2 = [sb(pst, f"zz2{i}", [128, TSB], F32R) for i in range(EC)]
            yf = [sb(pst, f"yf{i}", [128, TSB], F32R) for i in range(EC)]
            uu = [sb(pst, f"uu{i}", [128, TSB]) for i in range(DC)]
            xat = [sb(pst, f"xat{i}", [128, TSB], F32R) for i in range(2)]
            for w in range(NW if STOP >= 5 else 0):
                for ts in range(TW // TSB):
                    bld.barrier()
                    tb = w * TW + ts * TSB
                    yftk = []
                    for k in range(EC):
                        a_ = bld.dma("sync", yy[k][:, :],
                                     y_o[w].ap()[k * 128:(k + 1) * 128,
                                                 ts * TSB:(ts + 1) * TSB])
                        b_ = bld.dma("sync", zz2[k][:, :],
                                     z_sp.ap()[k * 128:(k + 1) * 128,
                                               tb:tb + TSB])
                        yftk.append(STT(yf[k][:, :], zz2[k][:, :].bitcast(F32),
                                        dpv_s[:, k:k + 1], yy[k][:, :],
                                        ALU.mult, ALU.add, waits=[a_, b_]))
                    utk = []
                    for dc in range(DC):
                        utk.append(bld.dma(
                            "sync", uu[dc][:, :],
                            u0_sp.ap()[dc * 128:(dc + 1) * 128, tb:tb + TSB]))
                    bank_rd = [None, None]
                    xa_fr = [None, None]
                    for dm in range(DC):
                        bi = dm % 2
                        mm_ = None
                        for k in range(EC):
                            mm_ = MM(bank(bi)[:, 0:TSB],
                                     ot_s[:, k * 1024 + dm * 128:
                                          k * 1024 + (dm + 1) * 128],
                                     yf[k][:, :], k == 0, k == EC - 1,
                                     waits=[yftk[k], w_tok, bank_rd[bi]])
                        xv = TT(xat[bi][:, :], bank(bi)[:, 0:TSB],
                                uu[dm][:, :], ALU.add,
                                waits=[mm_, utk[dm], xa_fr[bi]])
                        bank_rd[bi] = xv
                        xa_fr[bi] = bld.dma(
                            "sync",
                            xa_sp.ap()[dm * 128:(dm + 1) * 128, tb:tb + TSB],
                            xat[bi][:, :], waits=[xv])
        bld.barrier()

        with ExitStack() as pst:
            gup_s = sb(pst, "gup_s", [128, 8 * 5504], BF16)
            for k in range(8):
                bld.dma("gpsimd", gup_s[:, k * 5504:(k + 1) * 5504],
                        gupT.ap()[k * 128:(k + 1) * 128, :])
            dn_s = sb(pst, "dn_s", [128, 22 * 1024], BF16)
            for kc in range(HC):
                rows_n = 128 if kc < HC - 1 else H_LAST
                bld.dma("gpsimd", dn_s[0:rows_n, kc * 1024:(kc + 1) * 1024],
                        dnT.ap()[kc * 128:kc * 128 + rows_n, :])
            w_tok = bld.fence("gpsimd")
            xat2 = [sb(pst, f"xat2_{i}", [128, TSB], F32R) for i in range(DC)]
            hnb = [sb(pst, f"hnb{i}", [128, TSB], BF16) for i in range(DC)]
            hnt = sb(pst, "hnt", [128, TSB])
            sq2 = [sb(pst, f"sq2_{i}", [128, TSB], F32R) for i in range(2)]
            sgt2 = [sb(pst, f"sgt2_{i}", [128, TSB]) for i in range(2)]
            slab = sb(pst, "slab", [128, HC * TSB], BF16)
            xfmt = [sb(pst, f"xfmt{i}", [128, TSB]) for i in range(DC)]
            x1t = [sb(pst, f"x1t{i}", [128, TSB], F32R) for i in range(DC)]
            u1b = sb(pst, "u1b", [128, TSB], BF16)
            tmp2 = sb(pst, "tmp2", [128, TSB])

            for w in range(NW if STOP >= 6 else 0):
                for ts in range(TW // TSB):
                    bld.barrier()
                    tb = w * TW + ts * TSB
                    xtk = []
                    for dc in range(DC):
                        xtk.append(bld.dma(
                            "sync", xat2[dc][:, :],
                            xa_sp.ap()[dc * 128:(dc + 1) * 128, tb:tb + TSB]))
                    m2_ = None
                    sq_rd = [None, None]
                    for dc in range(DC):
                        sv = ACTI(sq2[dc % 2][:, :],
                                  xat2[dc][:, :].bitcast(F32),
                                  AF.Square, waits=[xtk[dc], sq_rd[dc % 2]])
                        m2_ = MM(bank(1)[0:1, 0:TSB], onc[:, :],
                                 sq2[dc % 2][:, :], dc == 0, dc == DC - 1,
                                 waits=[sv])
                        sq_rd[dc % 2] = m2_
                    v2 = CP(rs(1), bank(1)[0:1, 0:TSB], waits=[m2_])
                    a = TSC(rs(3), rs(1), 1.0 / D, ALU.mult, waits=[v2])
                    a = TSC(rs(7), rs(3), 1e-6, ALU.add, waits=[a])
                    s1 = ACTI(rs(4), rs(7).bitcast(F32), AF.Ln, waits=[a])
                    s1 = ACTI(rs(4), rs(4).bitcast(F32), AF.Exp, scale=-0.5,
                              waits=[s1])
                    bs = MM(bank(2)[:, 0:TSB], on1[:, :], rs(4), True, True,
                            waits=[s1])
                    hn_last = None
                    for dc in range(DC):
                        t1 = TT(hnt[:, :], xat2[dc][:, :].bitcast(F32),
                                bank(2)[:, 0:TSB], ALU.mult,
                                waits=[bs, hn_last])
                        hn_last = ACTI(hnb[dc][:, :], hnt[:, :], AF.Identity,
                                       scale=fnw_s[:, dc:dc + 1], waits=[t1])
                    gu_rd = [None, None]
                    sg_fr = [None, None]
                    pv = None
                    for mc in range(HC):
                        rows_n = 128 if mc < HC - 1 else H_LAST
                        mg = mu = None
                        for k in range(8):
                            mg = MM(bank(4)[0:rows_n, 0:TSB],
                                    gup_s[:, k * 5504 + mc * 128:
                                          k * 5504 + mc * 128 + rows_n],
                                    hnb[k][:, :], k == 0, k == 7,
                                    waits=[hn_last, w_tok, gu_rd[0]])
                            mu = MM(bank(5)[0:rows_n, 0:TSB],
                                    gup_s[:, k * 5504 + H + mc * 128:
                                          k * 5504 + H + mc * 128 + rows_n],
                                    hnb[k][:, :], k == 0, k == 7,
                                    waits=[gu_rd[1]])
                        sv = ACTI(sgt2[mc % 2][0:rows_n, :],
                                  bank(4)[0:rows_n, 0:TSB], AF.Silu,
                                  waits=[mg, sg_fr[mc % 2]])
                        pv = TT(slab[0:rows_n, mc * TSB:(mc + 1) * TSB],
                                sgt2[mc % 2][0:rows_n, :],
                                bank(5)[0:rows_n, 0:TSB], ALU.mult,
                                waits=[sv, mu])
                        gu_rd[0] = sv
                        gu_rd[1] = pv
                        sg_fr[mc % 2] = pv
                    prod_all = pv
                    dn_rd = [None, None]
                    ao_toks = []
                    for dm in range(DC):
                        bi = 6 + dm % 2
                        md = None
                        for kc in range(HC):
                            rows_n = 128 if kc < HC - 1 else H_LAST
                            md = MM(bank(bi)[:, 0:TSB],
                                    dn_s[0:rows_n, kc * 1024 + dm * 128:
                                         kc * 1024 + (dm + 1) * 128],
                                    slab[0:rows_n, kc * TSB:(kc + 1) * TSB],
                                    kc == 0, kc == HC - 1,
                                    waits=[prod_all, dn_rd[dm % 2]])
                        av = TT(xat2[dm][:, :], bank(bi)[:, 0:TSB],
                                xat2[dm][:, :].bitcast(F32), ALU.add,
                                waits=[md])
                        dn_rd[dm % 2] = av
                        ao_toks.append(av)
                    m1_ = m2_ = None
                    sq_rd = [None, None]
                    for dc in range(DC):
                        sv = ACTI(sq2[dc % 2][:, :],
                                  xat2[dc][:, :].bitcast(F32), AF.Square,
                                  waits=[ao_toks[dc], sq_rd[dc % 2]])
                        m1_ = MM(bank(0)[0:1, 0:TSB], onc[:, :],
                                 xat2[dc][:, :], dc == 0, dc == DC - 1,
                                 waits=[ao_toks[dc]])
                        m2_ = MM(bank(1)[0:1, 0:TSB], onc[:, :],
                                 sq2[dc % 2][:, :], dc == 0, dc == DC - 1,
                                 waits=[sv])
                        sq_rd[dc % 2] = m2_
                    v1 = CP(rs(0), bank(0)[0:1, 0:TSB], waits=[m1_, m2_])
                    v2 = CP(rs(1), bank(1)[0:1, 0:TSB], waits=[m2_])
                    a = TSC(rs(2), rs(0), 1.0 / D, ALU.mult, waits=[v1])
                    a = TT(rs(7), rs(2), rs(2), ALU.mult, waits=[a])
                    a = STT(rs(3), rs(1), 1.0 / D, rs(7), ALU.mult,
                            ALU.subtract, waits=[a, v2])
                    a = TSC(rs(7), rs(3), 1e-5, ALU.add, waits=[a])
                    s1 = ACTI(rs(4), rs(7).bitcast(F32), AF.Ln, waits=[a])
                    s1 = ACTI(rs(4), rs(4).bitcast(F32), AF.Exp, scale=-0.5,
                              waits=[s1])
                    bm = MM(bank(2)[:, 0:TSB], on1[:, :], rs(2), True, True,
                            waits=[a])
                    br = MM(bank(3)[:, 0:TSB], on1[:, :], rs(4), True, True,
                            waits=[s1])
                    xf_tk = []
                    for dc in range(DC):
                        xf_tk.append(bld.dma(
                            "sync", xfmt[dc][:, :],
                            xfm_sp.ap()[dc * 128:(dc + 1) * 128, tb:tb + TSB]
                            .bitcast(F32)))
                    x1_toks = []
                    for dc in range(DC):
                        t1 = TT(tmp2[:, :], xat2[dc][:, :].bitcast(F32),
                                bank(2)[:, 0:TSB], ALU.subtract,
                                waits=[bm, x1_toks[-1] if x1_toks else None])
                        t2 = TT(tmp2[:, :], tmp2[:, :], bank(3)[:, 0:TSB],
                                ALU.mult, waits=[t1, br])
                        t3 = TT(x1t[dc][:, :], tmp2[:, :], xfmt[dc][:, :],
                                ALU.add, waits=[t2, xf_tk[dc]])
                        bld.dma("sync",
                                x1_sp.ap()[dc * 128:(dc + 1) * 128,
                                           tb:tb + TSB],
                                x1t[dc][:, :], waits=[t3])
                        x1_toks.append(t3)
                    m1_ = m2_ = None
                    sq_rd = [None, None]
                    for dc in range(DC):
                        sv = ACTI(sq2[dc % 2][:, :], x1t[dc][:, :].bitcast(F32),
                                  AF.Square, waits=[x1_toks[dc], sq_rd[dc % 2]])
                        m1_ = MM(bank(0)[0:1, 0:TSB], onc[:, :],
                                 x1t[dc][:, :], dc == 0, dc == DC - 1,
                                 waits=[x1_toks[dc]])
                        m2_ = MM(bank(1)[0:1, 0:TSB], onc[:, :],
                                 sq2[dc % 2][:, :], dc == 0, dc == DC - 1,
                                 waits=[sv])
                        sq_rd[dc % 2] = m2_
                    v1 = CP(rs(0), bank(0)[0:1, 0:TSB], waits=[m1_, m2_])
                    v2 = CP(rs(1), bank(1)[0:1, 0:TSB], waits=[m2_])
                    a = TSC(rs(2), rs(0), 1.0 / D, ALU.mult, waits=[v1])
                    a = TT(rs(7), rs(2), rs(2), ALU.mult, waits=[a])
                    a = STT(rs(3), rs(1), 1.0 / D, rs(7), ALU.mult,
                            ALU.subtract, waits=[a, v2])
                    a = TSC(rs(7), rs(3), 1e-5, ALU.add, waits=[a])
                    s1 = ACTI(rs(4), rs(7).bitcast(F32), AF.Ln, waits=[a])
                    s1 = ACTI(rs(4), rs(4).bitcast(F32), AF.Exp, scale=-0.5,
                              waits=[s1])
                    bm = MM(bank(2)[:, 0:TSB], on1[:, :], rs(2), True, True,
                            waits=[a])
                    br = MM(bank(3)[:, 0:TSB], on1[:, :], rs(4), True, True,
                            waits=[s1])
                    u1_fr = None
                    for dc in range(DC):
                        t1 = TT(tmp2[:, :], x1t[dc][:, :].bitcast(F32),
                                bank(2)[:, 0:TSB], ALU.subtract,
                                waits=[bm, u1_fr])
                        t2 = TT(u1b[:, :], tmp2[:, :], bank(3)[:, 0:TSB],
                                ALU.mult, waits=[t1, br])
                        u1_fr = bld.dma(
                            "sync",
                            u1_sp.ap()[dc * 128:(dc + 1) * 128, tb:tb + TSB],
                            u1b[:, :], waits=[t2])
        bld.barrier()

        with ExitStack() as pst:
            TS3 = 512
            u1r = [[sb(pst, f"u1r{w}_{c}", [128, TW], BF16) for c in range(DC)]
                   for w in range(NW)]
            ptl = [[sb(pst, f"ptl{w}_{m}", [128, TW], F32R) for m in range(DC)]
                   for w in range(NW)]
            cpjt = [sb(pst, f"cpjt{i}", [128, 1024], BF16) for i in range(EC)]
            wbig = sb(pst, "wbig", [128, 2048], BF16)
            slab3 = sb(pst, "slab3", [128, 16 * TS3], BF16)
            x1r = [sb(pst, f"x1r{i}", [128, TS3]) for i in range(DC)]
            sq3 = [sb(pst, f"sq3_{i}", [128, TS3], F32R) for i in range(2)]
            tmp3 = sb(pst, "tmp3", [128, TS3])
            ostg = sb(pst, "ostg", [128, 1024])

            u1tk = {}
            for w in range(NW if STOP >= 7 else 0):
                for cch in range(DC):
                    u1tk[(w, cch)] = bld.dma(
                        "sync", u1r[w][cch][:, :],
                        u1_sp.ap()[cch * 128:(cch + 1) * 128,
                                   w * TW:(w + 1) * TW])
            pt_wr = {}
            for g in range(8 if STOP >= 7 else 0):
                bld.barrier()
                wb_tok = bld.dma("sync", wbig[:, :],
                                 wbig_sp.ap()[:, g * 2048:(g + 1) * 2048])
                ctk = []
                for kcl in range(EC):
                    ctk.append(bld.dma(
                        "gpsimd", cpjt[kcl][:, :],
                        cpj.ap()[2048 * g + kcl * 128:
                                 2048 * g + (kcl + 1) * 128, :]))
                for w in range(NW):
                    for ts in range(TW // TS3):
                        zb_rd = [None, None]
                        gl_toks = []
                        for kcl in range(EC):
                            bi = kcl % 2
                            zm = MM(bank(bi)[:, 0:TS3],
                                    wbig[:, kcl * 128:(kcl + 1) * 128],
                                    u1r[w][g][:, ts * TS3:(ts + 1) * TS3],
                                    True, True,
                                    waits=[wb_tok, u1tk[(w, g)], zb_rd[bi]])
                            gv = ACTI(slab3[:, kcl * TS3:(kcl + 1) * TS3],
                                      bank(bi)[:, 0:TS3], AF.Gelu, waits=[zm])
                            zb_rd[bi] = gv
                            gl_toks.append(gv)
                        gl_all = gl_toks[-1]
                        cp_rd = [None, None]
                        for dm in range(DC):
                            bi = 2 + dm % 2
                            md = None
                            for kcl in range(EC):
                                md = MM(bank(bi)[:, 0:TS3],
                                        cpjt[kcl][:, dm * 128:(dm + 1) * 128],
                                        slab3[:, kcl * TS3:(kcl + 1) * TS3],
                                        kcl == 0, kcl == EC - 1,
                                        waits=[gl_all, ctk[kcl],
                                               cp_rd[dm % 2]])
                            dst = ptl[w][dm][:, ts * TS3:(ts + 1) * TS3]
                            if g == 0:
                                pv = CP(dst, bank(bi)[:, 0:TS3], waits=[md])
                            else:
                                pv = TT(dst, dst.bitcast(F32),
                                        bank(bi)[:, 0:TS3], ALU.add,
                                        waits=[md, pt_wr[(w, dm, ts)]])
                            cp_rd[dm % 2] = pv
                            pt_wr[(w, dm, ts)] = pv
            for w in range(NW if STOP >= 7 else 0):
                for ts in range(TW // TS3):
                    bld.barrier()
                    col = slice(ts * TS3, (ts + 1) * TS3)
                    tb = w * TW + ts * TS3
                    m1_ = m2_ = None
                    sq_rd = [None, None]
                    for dc in range(DC):
                        sv = ACTI(sq3[dc % 2][:, :],
                                  ptl[w][dc][:, col].bitcast(F32), AF.Square,
                                  waits=[sq_rd[dc % 2]])
                        m1_ = MM(bank(0)[0:1, 0:TS3], onc[:, :],
                                 ptl[w][dc][:, col], dc == 0, dc == DC - 1)
                        m2_ = MM(bank(1)[0:1, 0:TS3], onc[:, :],
                                 sq3[dc % 2][:, :], dc == 0, dc == DC - 1,
                                 waits=[sv])
                        sq_rd[dc % 2] = m2_
                    v1 = CP(rs(0, TS3), bank(0)[0:1, 0:TS3], waits=[m1_, m2_])
                    v2 = CP(rs(1, TS3), bank(1)[0:1, 0:TS3], waits=[m2_])
                    a = TSC(rs(2, TS3), rs(0, TS3), 1.0 / D, ALU.mult,
                            waits=[v1])
                    a = TT(rs(7, TS3), rs(2, TS3), rs(2, TS3), ALU.mult,
                           waits=[a])
                    a = STT(rs(3, TS3), rs(1, TS3), 1.0 / D, rs(7, TS3),
                            ALU.mult, ALU.subtract, waits=[a, v2])
                    a = TSC(rs(7, TS3), rs(3, TS3), 1e-5, ALU.add, waits=[a])
                    s1 = ACTI(rs(4, TS3), rs(7, TS3).bitcast(F32), AF.Ln,
                              waits=[a])
                    s1 = ACTI(rs(4, TS3), rs(4, TS3).bitcast(F32), AF.Exp,
                              scale=-0.5, waits=[s1])
                    bm = MM(bank(2)[:, 0:TS3], on1[:, :], rs(2, TS3),
                            True, True, waits=[a])
                    br = MM(bank(3)[:, 0:TS3], on1[:, :], rs(4, TS3),
                            True, True, waits=[s1])
                    x1tk = []
                    for dc in range(DC):
                        x1tk.append(bld.dma(
                            "sync", x1r[dc][:, :],
                            x1_sp.ap()[dc * 128:(dc + 1) * 128, tb:tb + TS3]
                            .bitcast(F32)))
                    x2_toks = []
                    for dc in range(DC):
                        t1 = TT(tmp3[:, :], ptl[w][dc][:, col].bitcast(F32),
                                bank(2)[:, 0:TS3], ALU.subtract,
                                waits=[bm, x2_toks[-1] if x2_toks else None])
                        t2 = TT(tmp3[:, :], tmp3[:, :], bank(3)[:, 0:TS3],
                                ALU.mult, waits=[t1, br])
                        t3 = TT(x1r[dc][:, :], tmp3[:, :], x1r[dc][:, :],
                                ALU.add, waits=[t2, x1tk[dc]])
                        x2_toks.append(t3)
                    x2_all = x2_toks[-1]
                    ev_fr = [None, None]
                    ostg_fr = None
                    for tt in range(TS3 // 128):
                        pi = 2 + tt % 2
                        trk = None
                        for dc in range(DC):
                            trk = TRA(psT[pi][:, dc * 128:(dc + 1) * 128],
                                      x1r[dc][:, tt * 128:(tt + 1) * 128],
                                      waits=[x2_all, ev_fr[tt % 2]])
                        ev = bld.op("scalar",
                                    lambda e, pi=pi: e.copy(ostg[:, :],
                                                            psT[pi][:, :]),
                                    waits=[trk, ostg_fr])
                        ev_fr[tt % 2] = bld.dma(
                            "sync",
                            out_t.ap()[tb + tt * 128:tb + (tt + 1) * 128, :],
                            ostg[:, :], waits=[ev])
                        ostg_fr = ev_fr[tt % 2]
        bld.barrier()
        bld.flush_barrier()

        with nc.allow_low_precision(reason="bf16 operands by design"), \
                nc.Block() as blk:
            @blk.sync
            def _(e):
                for f in bld.ops["sync"]:
                    f(e)

            @blk.gpsimd
            def _(e):
                for f in bld.ops["gpsimd"]:
                    f(e)

            @blk.vector
            def _(e):
                for f in bld.ops["vector"]:
                    f(e)

            @blk.scalar
            def _(e):
                for f in bld.ops["scalar"]:
                    f(e)

            @blk.tensor
            def _(e):
                for f in bld.ops["tensor"]:
                    f(e)
    return nc


def _prep(inputs):
    f = np.ascontiguousarray
    x = inputs["L0_attn_x"]
    ww = {}
    ww["wig"] = f(np.concatenate([inputs["L0_attn_in_proj_w"],
                                  inputs["L0_attn_gate_proj_w"]], 0).T)
    ww["dtT"] = f(inputs["L0_attn_dt_w"].T)
    ww["xpT"] = f(inputs["L0_attn_x_proj_w"].T.reshape(EC, 128, 2 * NST)
                  .transpose(1, 0, 2).reshape(128, EC * 2 * NST))
    ww["outT"] = f(inputs["L0_attn_out_proj_w"].T)
    ww["gupT"] = f(np.concatenate([inputs["L0_attn_ffn_gate_w"],
                                   inputs["L0_attn_ffn_up_w"]], 0).T)
    ww["dnT"] = f(inputs["L0_attn_ffn_down_w"].T)
    ww["cpj"] = f(inputs["L0_ffn_w_cproj"].reshape(D, 4, 64, 64)
                  .transpose(2, 1, 3, 0).reshape(16384, D))
    ww["makerJ"] = f(inputs["L0_ffn_rail_maker"].transpose(3, 0, 1, 2)
                     .reshape(NST, 4096))
    ww["upJ"] = f(inputs["L0_ffn_rail_up_proj"].transpose(3, 0, 1, 2)
                  .reshape(NST, 16384))
    ww["dtb"] = f(inputs["L0_attn_dt_b"].reshape(EC, 128).T)
    ww["dpv"] = f(inputs["L0_attn_Dp"].reshape(EC, 128).T)
    ww["n1w"] = f(inputs["L0_attn_norm1_w"].reshape(DC, 128).T)
    ww["fnw"] = f(inputs["L0_attn_ffn_norm_w"].reshape(DC, 128).T)
    ww["eye"] = np.eye(128, dtype=np.float32)
    selm = np.zeros((NST, NST * 128), np.float32)
    for n in range(NST):
        selm[n, n * 128:(n + 1) * 128] = 1.0
    ww["sel16"] = selm
    ww["ones1"] = np.ones((1, 128), np.float32)
    ww["onesc"] = np.ones((128, 1), np.float32)
    in_maps = []
    al = inputs["L0_attn_A_log"]
    for c in range(R):
        m = dict(ww)
        m["x"] = f(np.concatenate([x[0, c * TW:(c + 1) * TW],
                                   x[1, c * TW:(c + 1) * TW]], 0))
        m["a_log"] = f(al[c * 256:(c + 1) * 256].reshape(2, 128, NST)
                       .transpose(1, 0, 2).reshape(128, 2 * NST))
        in_maps.append(m)
    return in_maps


def kernel(**inputs):
    inputs = {k: np.asarray(v, dtype=np.float32) for k, v in inputs.items()}
    in_maps = _prep(inputs)
    nc = build(debug=False)
    res = run_bass_kernel_spmd(nc, in_maps, core_ids=list(range(R)))
    out = np.empty((2, L, D), np.float32)
    for c in range(R):
        o = res.results[c]["out"]
        out[0, c * TW:(c + 1) * TW] = o[:TW]
        out[1, c * TW:(c + 1) * TW] = o[TW:]
    return out


# version 03
# speedup vs baseline: 1.5177x; 1.5032x over previous
import os
os.environ.setdefault("TRN_ALLOW_ICI_ALL_TO_ALL", "1")
import numpy as np
import ml_dtypes
from collections import defaultdict
from contextlib import ExitStack

import concourse.bass as bass
import concourse.mybir as mybir
from concourse.bass_utils import run_bass_kernel_spmd

F32 = mybir.dt.float32
F32R = mybir.dt.float32r
BF16 = mybir.dt.bfloat16
AF = mybir.ActivationFunctionType
ALU = mybir.AluOpType

R = 8
NW = 2
L = 8192
D = 1024
E = 2048
NST = 16
H = 2752
TW = L // R
DC, EC = 8, 16
HC, H_LAST = 22, 64
TSA = 256
TSB = 256


class Bld:
    ENGINES = ("sync", "gpsimd", "vector", "scalar", "tensor")

    def __init__(self, nc):
        self.nc = nc
        self.ops = defaultdict(list)
        self.cnt = defaultdict(int)
        self.sems = {}
        self.pending = {}
        self.rr = {}

    KPOOL = {"sync": 8, "gpsimd": 4, "scalar": 4}

    def sem_names(self):
        names = ["vector", "scalar", "tensor", "gpsimd", "cc"]
        for q, k in self.KPOOL.items():
            names += [f"{q}d{i}" for i in range(k)]
        return names

    def fence(self, q):
        return [(f"{q}d{i}", self.cnt[f"{q}d{i}"])
                for i in range(self.KPOOL[q]) if self.cnt[f"{q}d{i}"] > 0]

    def fence_all(self):
        w = []
        for q in self.KPOOL:
            w += self.fence(q)
        return w

    def _emit(self, eng, waits, inc_sem, inc_n, fn):
        w = []
        for x in waits:
            if x is None:
                continue
            if isinstance(x, list):
                w.extend(x)
            else:
                w.append(x)
        w += self.pending.pop(eng, [])
        sems = self.sems

        def cl(e, fn=fn, w=w, inc_sem=inc_sem, inc_n=inc_n):
            for s, v in w:
                e.wait_ge(sems[s], v)
            ins = fn(e)
            if inc_sem is not None:
                ins.then_inc(sems[inc_sem], inc_n)
        self.ops[eng].append(cl)

    def op(self, eng, fn, waits=()):
        self.cnt[eng] += 1
        tok = (eng, self.cnt[eng])
        self._emit(eng, waits, eng, 1, fn)
        return tok

    def dma(self, q, out, in_, waits=()):
        i = self.rr.get(q, 0)
        self.rr[q] = (i + 1) % self.KPOOL[q]
        s = f"{q}d{i}"
        prev = (s, self.cnt[s])
        self.cnt[s] += 16
        tok = (s, self.cnt[s])
        self._emit(q, list(waits) + [prev], s, 16,
                   lambda e, out=out, in_=in_: e.dma_start(out=out, in_=in_))
        return tok

    def cc(self, kind, in_, out, waits=()):
        prev = ("cc", self.cnt["cc"])
        self.cnt["cc"] += 1
        tok = ("cc", self.cnt["cc"])
        self._emit("gpsimd", list(waits) + [prev], "cc", 1,
                   lambda e, k=kind, i=in_, o=out:
                   e.collective_compute(k, ALU.bypass, [list(range(R))],
                                        ins=[i.ap().opt()], outs=[o.ap().opt()]))
        return tok

    def barrier(self):
        snap = [(s, c) for s, c in self.cnt.items() if c > 0]
        for eng in self.ENGINES:
            self.pending[eng] = list(snap)

    def flush_barrier(self):
        sems = self.sems
        for eng in self.ENGINES:
            w = self.pending.pop(eng, [])
            if not w:
                continue

            def cl(e, w=w):
                for s, v in w:
                    e.wait_ge(sems[s], v)
            self.ops[eng].append(cl)


def build(debug=False):
    nc = bass.Bass("TRN2", target_bir_lowering=False)

    def din(name, shape, dt=F32):
        return nc.dram_tensor(name, shape, dt, kind="ExternalInput")

    def dout(name, shape, dt=F32):
        return nc.dram_tensor(name, shape, dt, kind="ExternalOutput")

    def dint(name, shape, dt=F32, dbg_ok=True):
        kind = "ExternalOutput" if (debug and dbg_ok) else "Internal"
        return nc.dram_tensor(name, shape, dt, kind=kind)

    x_in = din("x", [NW * TW, D])
    wig = din("wig", [D, 2 * E], F32R)
    dtT = din("dtT", [E, E], F32R)
    xpT = din("xpT", [128, EC * 2 * NST], F32R)
    outT = din("outT", [E, D], F32R)
    gupT = din("gupT", [D, 2 * H])
    dnT = din("dnT", [H, D])
    cpj = din("cpj", [16384, D])
    makerJ = din("makerJ", [NST, 4096], F32R)
    upJ = din("upJ", [NST, 16384], F32R)
    a_log = din("a_log", [128, 2 * NST])
    dtb = din("dtb", [128, EC])
    dpv = din("dpv", [128, EC])
    n1w = din("n1w", [128, DC])
    fnw = din("fnw", [128, DC])
    eye_c = din("eye", [128, 128])
    sel16 = din("sel16", [NST, NST * 128], F32R)
    ones1 = din("ones1", [1, 128], F32R)
    onesc = din("onesc", [128, 1], F32R)

    out_t = dout("out", [NW * TW, D])
    z_sp = dint("z_sp", [E, NW * TW], F32R)
    xfm_sp = dint("xfm_sp", [D, NW * TW], F32R)
    u0_sp = dint("u0_sp", [D, NW * TW])
    xa_sp = dint("xa_sp", [D, NW * TW], F32R)
    x1_sp = dint("x1_sp", [D, NW * TW], F32R)
    u1_sp = dint("u1_sp", [D, NW * TW], BF16, dbg_ok=False)
    wbig_sp = dint("wbig_sp", [128, 8 * 2048], BF16, dbg_ok=False)
    dt_i = [dint(f"dt_i{w}", [E, TW], dbg_ok=False) for w in range(NW)]
    dt_o = [dint(f"dt_o{w}", [E, TW], dbg_ok=False) for w in range(NW)]
    dz_i = [dint(f"dz_i{w}", [E, TW], dbg_ok=False) for w in range(NW)]
    dz_o = [dint(f"dz_o{w}", [E, TW], dbg_ok=False) for w in range(NW)]
    y_i = [dint(f"y_i{w}", [E, TW], dbg_ok=False) for w in range(NW)]
    y_o = [dint(f"y_o{w}", [E, TW], dbg_ok=False) for w in range(NW)]
    bc_i = [dint(f"bc_i{w}", [2 * NST, TW], dbg_ok=False) for w in range(NW)]
    bc_o = [dint(f"bc_o{w}", [R * 2 * NST, TW], dbg_ok=False) for w in range(NW)]
    if debug:
        dbg_y = [dout(f"dbg_y{w}", [E, TW]) for w in range(NW)]
        dbg_dt = [dout(f"dbg_dt{w}", [E, TW]) for w in range(NW)]
        dbg_dz = [dout(f"dbg_dz{w}", [E, TW]) for w in range(NW)]
        dbg_bc = [dout(f"dbg_bc{w}", [R * 2 * NST, TW]) for w in range(NW)]
        dbg_bci = [dout(f"dbg_bci{w}", [2 * NST, TW]) for w in range(NW)]
        dbg_zz = dout("dbg_zz", [128, 2 * TW])
        dbg_xp = dout("dbg_xp", [128, EC * 2 * NST])
        dbg_xp0 = dout("dbg_xp0", [128, EC * 2 * NST])
        dbg_btl = dout("dbg_btl", [NST, 2 * TW])

    bld = Bld(nc)
    stack = ExitStack()
    with stack:
        for s in bld.sem_names():
            bld.sems[s] = stack.enter_context(nc.semaphore(s))

        def sb(st, name, shape, dt=F32):
            return st.enter_context(nc.sbuf_tensor(name, shape, dt))

        psT = [stack.enter_context(nc.psum_tensor(f"ps{i}", [128, 1024], F32))
               for i in range(4)]

        def bank(i):
            return psT[i // 2][:, (i % 2) * 512:(i % 2 + 1) * 512]

        eye = sb(stack, "eye_s", [128, 128])
        sel = sb(stack, "sel_s", [NST, NST * 128], F32R)
        on1 = sb(stack, "on1_s", [1, 128], F32R)
        onc = sb(stack, "onc_s", [128, 1], F32R)
        an = sb(stack, "an_s", [128, 2 * NST])
        dtb_s = sb(stack, "dtb_s", [128, EC])
        dpv_s = sb(stack, "dpv_s", [128, EC])
        n1w_s = sb(stack, "n1w_s", [128, DC])
        fnw_s = sb(stack, "fnw_s", [128, DC])
        rows = sb(stack, "rows", [1, 4096], F32R)

        def rs(slot, width=TSA):
            return rows[0:1, slot * 512:slot * 512 + width]

        def TT(out, a, b, alu, waits=(), eng="vector"):
            return bld.op(eng, lambda e, out=out, a=a, b=b, alu=alu:
                          e.tensor_tensor(out, a, b, alu), waits)

        def CP(out, in_, waits=(), eng="vector"):
            return bld.op(eng, lambda e, out=out, in_=in_:
                          e.tensor_copy(out, in_), waits)

        def TSC(out, a, s1, alu, waits=(), eng="vector"):
            return bld.op(eng, lambda e, out=out, a=a, s1=s1, alu=alu:
                          e.tensor_single_scalar(out, a, s1, alu), waits)

        def STT(out, in0, sc, in1, op0, op1, waits=(), eng="vector"):
            return bld.op(eng,
                          lambda e, out=out, in0=in0, sc=sc, in1=in1,
                          op0=op0, op1=op1:
                          e.scalar_tensor_tensor(out, in0, sc, in1, op0, op1),
                          waits)

        def ACTI(out, in_, f, bias=0.0, scale=1.0, waits=()):
            return bld.op("scalar",
                          lambda e, out=out, in_=in_, f=f, b=bias, s=scale:
                          e.activation(out, in_, f, bias=b, scale=s), waits)

        def MM(out, lhsT, rhs, start, stop, waits=()):
            return bld.op("tensor",
                          lambda e, out=out, l=lhsT, r=rhs, st=start, sp=stop:
                          e.matmul(out, l, r, start=st, stop=sp), waits)

        def TRA(out, in_, waits=()):
            return bld.op("tensor",
                          lambda e, out=out, in_=in_:
                          e.transpose(out, in_, eye[:, :]), waits)

        def SCAN(out, d0, d1, init, waits=()):
            return bld.op("vector",
                          lambda e, out=out, d0=d0, d1=d1, init=init:
                          e.tensor_tensor_scan(out, d0, d1, init,
                                               ALU.mult, ALU.add), waits)

        def MSET(t, v, waits=(), eng="vector"):
            return bld.op(eng, lambda e, t=t, v=v: e.memset(t, v), waits)

        def stats_ln(src_tiles, sq_pair, width, with_mean=True):
            m1 = m2 = None
            for dc in range(DC):
                sv = ACTI(sq_pair[dc % 2][:, 0:width],
                          src_tiles[dc][:, 0:width].bitcast(F32), AF.Square)
                if with_mean:
                    m1 = MM(bank(0)[0:1, 0:width], onc[:, :],
                            src_tiles[dc][:, 0:width], dc == 0, dc == DC - 1)
                m2 = MM(bank(1)[0:1, 0:width], onc[:, :],
                        sq_pair[dc % 2][:, 0:width], dc == 0, dc == DC - 1,
                        waits=[sv])
            toks = []
            if with_mean:
                v1 = CP(rs(0, width), bank(0)[0:1, 0:width], waits=[m1])
                toks.append(v1)
            v2 = CP(rs(1, width), bank(1)[0:1, 0:width], waits=[m2])
            toks.append(v2)
            if with_mean:
                a = TSC(rs(2, width), rs(0, width), 1.0 / D, ALU.mult,
                        waits=[v1])
                a = TT(rs(7, width), rs(2, width), rs(2, width), ALU.mult,
                       waits=[a])
                a = STT(rs(3, width), rs(1, width), 1.0 / D, rs(7, width),
                        ALU.mult, ALU.subtract, waits=[a, v2])
                a = TSC(rs(7, width), rs(3, width), 1e-5, ALU.add, waits=[a])
            else:
                a = TSC(rs(3, width), rs(1, width), 1.0 / D, ALU.mult,
                        waits=[v2])
                a = TSC(rs(7, width), rs(3, width), 1e-6, ALU.add, waits=[a])
            s1 = ACTI(rs(4, width), rs(7, width).bitcast(F32), AF.Ln,
                      waits=[a])
            s1 = ACTI(rs(4, width), rs(4, width).bitcast(F32), AF.Exp,
                      scale=-0.5, waits=[s1])
            return s1, a

        for t, src in ((eye, eye_c), (sel, sel16), (on1, ones1), (onc, onesc),
                       (an, a_log), (dtb_s, dtb), (dpv_s, dpv),
                       (n1w_s, n1w), (fnw_s, fnw)):
            bld.dma("sync", t[:, :], src.ap())
        ld = bld.fence("sync")
        t_ = ACTI(an[:, :], an[:, :], AF.Exp, waits=[ld])
        bld.op("vector", lambda e: e.tensor_scalar_mul(an[:, :], an[:, :], -1.0),
               waits=[t_])

        STOP = int(os.environ.get("MK_STOP", "99"))
        with ExitStack() as pst:
            mkj = sb(pst, "mkj", [NST, 4096], F32R)
            upj = sb(pst, "upj", [NST, 16384], F32R)
            zt = sb(pst, "zt", [128, 2048], BF16)
            stg = sb(pst, "stg", [NST, 8 * 64], BF16)
            bld.dma("sync", mkj[:, :], makerJ.ap())
            bld.dma("sync", upj[:, :], upJ.ap())
            wtok = bld.fence("sync")
            zv = MSET(zt[:, :], 0.0)
            for g in range(8):
                bld.dma("sync", wbig_sp.ap()[:, g * 2048:(g + 1) * 2048],
                        zt[:, :], waits=[zv])
            zdone = bld.fence("sync")
            ev_tok = [None] * 8
            dm_tok = [None] * 8
            for h in range(4):
                for r in range(64):
                    hr = h * 64 + r
                    slot = hr % 8
                    pb = bank(slot)[0:NST, 0:64]
                    tt = MM(pb, mkj[:, hr * 16:hr * 16 + 16],
                            upj[:, hr * 64:hr * 64 + 64], True, True,
                            waits=[wtok, ev_tok[slot]])
                    ss_ = stg[0:NST, slot * 64:(slot + 1) * 64]
                    vv = CP(ss_, pb, waits=[tt, dm_tok[slot]])
                    g = r // 8
                    kcl = 2 * (r % 8) + h // 2
                    col = g * 2048 + kcl * 128 + (h % 2) * 64
                    dm_tok[slot] = bld.dma(
                        "sync",
                        wbig_sp.ap()[16 * (r % 8):16 * (r % 8) + 16,
                                     col:col + 64],
                        ss_, waits=[vv, zdone])
                    ev_tok[slot] = vv
        bld.barrier()

        with ExitStack() as pst:
            wig_s = sb(pst, "wig_s", [128, 8 * 4096], F32R)
            for k in range(8):
                bld.dma("scalar", wig_s[:, k * 4096:(k + 1) * 4096],
                        wig.ap()[k * 128:(k + 1) * 128, :])
            wig_tok = bld.fence("scalar")
            stage = [sb(pst, f"xstg{i}", [128, D]) for i in range(2)]
            xfm = [sb(pst, f"xfm{i}", [128, TSA], F32R) for i in range(DC)]
            sqp = [sb(pst, f"sq{i}", [128, TSA], F32R) for i in range(2)]
            u0t = [sb(pst, f"u0t{i}", [128, TSA]) for i in range(2)]
            h1t = sb(pst, "h1t", [128, TSA])
            h1 = [sb(pst, f"h1_{i}", [128, TSA], F32R) for i in range(DC)]
            sgt = [sb(pst, f"sgt{i}", [128, TSA]) for i in range(2)]
            zot = [sb(pst, f"zot{i}", [128, TSA], F32R) for i in range(2)]

            for w in range(NW if STOP >= 1 else 0):
                for ts in range(TW // TSA):
                    bld.barrier()
                    tb = w * TW + ts * TSA
                    ntt = TSA // 128
                    ltok = []
                    for tt in range(ntt):
                        ltok.append(bld.dma(
                            "sync", stage[tt][:, :],
                            x_in.ap()[tb + tt * 128:tb + (tt + 1) * 128, :]))
                    ev8 = []
                    tra_ev = [None, None]
                    for dc in range(DC):
                        bi = 4 + (dc % 2)
                        ttk = None
                        for tt in range(ntt):
                            ttk = TRA(bank(bi)[:, tt * 128:(tt + 1) * 128],
                                      stage[tt][:, dc * 128:(dc + 1) * 128],
                                      waits=[ltok[tt], tra_ev[dc % 2]])
                        ev = bld.op("scalar",
                                    lambda e, dc=dc, bi=bi:
                                    e.copy(xfm[dc][:, :], bank(bi)[:, 0:TSA]),
                                    waits=[ttk])
                        tra_ev[dc % 2] = ev
                        ev8.append(ev)
                        bld.dma("sync",
                                xfm_sp.ap()[dc * 128:(dc + 1) * 128,
                                            tb:tb + TSA],
                                xfm[dc][:, :], waits=[ev])
                    m1 = m2 = None
                    sq_rd = [None, None]
                    for dc in range(DC):
                        sv = ACTI(sqp[dc % 2][:, :], xfm[dc][:, :].bitcast(F32),
                                  AF.Square, waits=[ev8[dc], sq_rd[dc % 2]])
                        m1 = MM(bank(0)[0:1, 0:TSA], onc[:, :], xfm[dc][:, :],
                                dc == 0, dc == DC - 1, waits=[ev8[dc]])
                        m2 = MM(bank(1)[0:1, 0:TSA], onc[:, :],
                                sqp[dc % 2][:, :], dc == 0, dc == DC - 1,
                                waits=[sv])
                        sq_rd[dc % 2] = m2
                    v1 = CP(rs(0), bank(0)[0:1, 0:TSA], waits=[m1, m2])
                    v2 = CP(rs(1), bank(1)[0:1, 0:TSA], waits=[m2])
                    a = TSC(rs(2), rs(0), 1.0 / D, ALU.mult, waits=[v1])
                    a = TT(rs(7), rs(2), rs(2), ALU.mult, waits=[a])
                    a = STT(rs(3), rs(1), 1.0 / D, rs(7), ALU.mult,
                            ALU.subtract, waits=[a, v2])
                    a = TSC(rs(7), rs(3), 1e-5, ALU.add, waits=[a])
                    s1 = ACTI(rs(4), rs(7).bitcast(F32), AF.Ln, waits=[a])
                    s1 = ACTI(rs(4), rs(4).bitcast(F32), AF.Exp, scale=-0.5,
                              waits=[s1])
                    mr = TT(rs(6), rs(2), rs(4), ALU.mult, waits=[s1])
                    rc = bld.op("vector", lambda e: e.reciprocal(rs(0), rs(7)),
                                waits=[s1])
                    q = TT(rs(1), rs(3), rs(0), ALU.mult, waits=[rc])
                    q = TSC(rs(7), rs(1), 1e-6, ALU.add, waits=[q])
                    s2 = ACTI(rs(5), rs(7).bitcast(F32), AF.Ln, waits=[q])
                    s2 = ACTI(rs(5), rs(5).bitcast(F32), AF.Exp, scale=-0.5,
                              waits=[s2])
                    b_r1 = MM(bank(2)[:, 0:TSA], on1[:, :], rs(4), True, True,
                              waits=[s1])
                    b_mr = MM(bank(3)[:, 0:TSA], on1[:, :], rs(6), True, True,
                              waits=[mr])
                    b_s2 = MM(bank(4)[:, 0:TSA], on1[:, :], rs(5), True, True,
                              waits=[s2, tra_ev[0]])
                    h1_last = None
                    u0_fr = [None, None]
                    for dc in range(DC):
                        ui = dc % 2
                        a1 = TT(u0t[ui][:, :], xfm[dc][:, :].bitcast(F32),
                                bank(2)[:, 0:TSA], ALU.mult,
                                waits=[b_r1, u0_fr[ui]])
                        a2 = TT(u0t[ui][:, :], u0t[ui][:, :], bank(3)[:, 0:TSA],
                                ALU.subtract, waits=[a1, b_mr])
                        u0_fr[ui] = bld.dma(
                            "sync",
                            u0_sp.ap()[dc * 128:(dc + 1) * 128, tb:tb + TSA],
                            u0t[ui][:, :], waits=[a2])
                        a3 = TT(h1t[:, :], u0t[ui][:, :], bank(4)[:, 0:TSA],
                                ALU.mult, waits=[a2, b_s2, h1_last])
                        h1_last = ACTI(h1[dc][:, :], h1t[:, :], AF.Identity,
                                       scale=n1w_s[:, dc:dc + 1], waits=[a3])
                    zg_rd = {0: v1, 1: v2, 6: None, 7: None}
                    zo_fr = [None, None]
                    for ei in range(EC):
                        zb, gb = (6, 7) if ei % 2 == 0 else (0, 1)
                        mmz = mmg = None
                        for k in range(8):
                            mmz = MM(bank(zb)[:, 0:TSA],
                                     wig_s[:, k * 4096 + ei * 128:
                                           k * 4096 + (ei + 1) * 128],
                                     h1[k][:, :], k == 0, k == 7,
                                     waits=([h1_last, zg_rd[zb], wig_tok]
                                            if k == 0 else []))
                            mmg = MM(bank(gb)[:, 0:TSA],
                                     wig_s[:, k * 4096 + 2048 + ei * 128:
                                           k * 4096 + 2048 + (ei + 1) * 128],
                                     h1[k][:, :], k == 0, k == 7,
                                     waits=([zg_rd[gb]] if k == 0 else []))
                        sgv = ACTI(sgt[ei % 2][:, :], bank(gb)[:, 0:TSA],
                                   AF.Sigmoid, waits=[mmg, zo_fr[ei % 2]])
                        zv = TT(zot[ei % 2][:, :], bank(zb)[:, 0:TSA],
                                sgt[ei % 2][:, :], ALU.mult, waits=[mmz, sgv])
                        zo_fr[ei % 2] = bld.dma(
                            "sync",
                            z_sp.ap()[ei * 128:(ei + 1) * 128, tb:tb + TSA],
                            zot[ei % 2][:, :], waits=[zv])
                        zg_rd[zb] = zv
                        zg_rd[gb] = sgv
        bld.barrier()

        with ExitStack() as pst:
            if not os.environ.get("MK_A2_NODT"):
                dt_s = sb(pst, "dt_s", [128, 16 * 2048], F32R)
                for k in range(EC):
                    bld.dma("scalar", dt_s[:, k * 2048:(k + 1) * 2048],
                            dtT.ap()[k * 128:(k + 1) * 128, :])
            xp_s = sb(pst, "xp_s", [128, 16 * 32], F32R)
            xp_ld = bld.dma("scalar", xp_s[:, :], xpT.ap())
            if debug:
                bld.dma("sync", dbg_xp0.ap(), xp_s[:, :].bitcast(F32),
                        waits=[xp_ld])
            w_tok = bld.fence("scalar")
            TS2 = 256
            NZB = 4 if os.environ.get("MK_A2_ZZ4") else 1
            zz_all = [[sb(pst, f"zz{j}_{i}", [128, TS2], F32R)
                       for i in range(EC)] for j in range(NZB)]
            pt = [sb(pst, f"pt{i}", [128, TS2]) for i in range(2)]
            ab = [sb(pst, f"ab{i}", [128, TS2]) for i in range(2)]
            rl = [sb(pst, f"rl{i}", [128, TS2]) for i in range(2)]
            dtv = [sb(pst, f"dtv{i}", [128, TS2]) for i in range(2)]
            dzv = [sb(pst, f"dzv{i}", [128, TS2]) for i in range(2)]
            bt_l = sb(pst, "bt_l", [NST, TW])
            ct_l = sb(pst, "ct_l", [NST, TW])

            for w in range(NW if STOP >= 2 else 0):
                bc_ev = [None, None]
                for ts in range(TW // TS2):
                    bld.barrier()
                    zz = zz_all[ts % len(zz_all)]
                    tb = w * TW + ts * TS2
                    ztk = []
                    for k in range(EC):
                        ztk.append(bld.dma(
                            "sync", zz[k][:, :],
                            z_sp.ap()[k * 128:(k + 1) * 128, tb:tb + TS2]))
                    zdone = bld.fence("sync")
                    mb = mc_ = None
                    for k in range(EC):
                        mb = MM(bank(2)[0:NST, 0:TS2],
                                xp_s[:, k * 32:k * 32 + 16], zz[k][:, :],
                                k == 0, k == EC - 1,
                                waits=([zdone, w_tok] if k == 0 else []))
                        mc_ = MM(bank(3)[0:NST, 0:TS2],
                                 xp_s[:, k * 32 + 16:k * 32 + 32], zz[k][:, :],
                                 k == 0, k == EC - 1)
                    bc_ev[0] = CP(bt_l[:, ts * TS2:(ts + 1) * TS2],
                                  bank(2)[0:NST, 0:TS2], waits=[mb, mc_])
                    bc_ev[1] = CP(ct_l[:, ts * TS2:(ts + 1) * TS2],
                                  bank(3)[0:NST, 0:TS2], waits=[mc_])
                    if debug:
                        bld.dma("sync",
                                dbg_zz.ap()[:, w * TW + ts * TS2:
                                            w * TW + (ts + 1) * TS2],
                                zz[0][:, :].bitcast(F32), waits=[bc_ev[1]])
                    dtbank_rd = [None, None]
                    tmp_fr = [None, None]
                    for ei in range(0 if os.environ.get("MK_A2_NODT") else EC):
                        bi = ei % 2
                        mmd = None
                        for k in range(EC):
                            mmd = MM(bank(bi)[:, 0:TS2],
                                     dt_s[:, k * 2048 + ei * 128:
                                          k * 2048 + (ei + 1) * 128],
                                     zz[k][:, :], k == 0, k == EC - 1,
                                     waits=([zdone, w_tok, dtbank_rd[bi]]
                                            if k == 0 else []))
                        p1 = ACTI(pt[bi][:, :], bank(bi)[:, 0:TS2], AF.Identity,
                                  bias=dtb_s[:, ei:ei + 1],
                                  waits=[mmd, tmp_fr[bi]])
                        p2 = ACTI(ab[bi][:, :], bank(bi)[:, 0:TS2], AF.Abs,
                                  bias=dtb_s[:, ei:ei + 1], waits=[mmd, p1])
                        dtbank_rd[bi] = p2
                        p3 = ACTI(ab[bi][:, :], ab[bi][:, :], AF.Exp,
                                  scale=-1.0, waits=[p2])
                        p4 = ACTI(ab[bi][:, :], ab[bi][:, :], AF.Ln, bias=1.0,
                                  waits=[p3])
                        v1 = TSC(rl[bi][:, :], pt[bi][:, :], 0.0, ALU.max,
                                 waits=[p1, tmp_fr[bi]])
                        v2 = TT(dtv[bi][:, :], rl[bi][:, :], ab[bi][:, :],
                                ALU.add, waits=[v1, p4, tmp_fr[bi]])
                        d1 = bld.dma("sync",
                                     dt_i[w].ap()[ei * 128:(ei + 1) * 128,
                                                  ts * TS2:(ts + 1) * TS2],
                                     dtv[bi][:, :], waits=[v2])
                        v3 = TT(dzv[bi][:, :], dtv[bi][:, :],
                                zz[ei][:, :].bitcast(F32), ALU.mult,
                                waits=[v2, tmp_fr[bi]])
                        d2 = bld.dma("sync",
                                     dz_i[w].ap()[ei * 128:(ei + 1) * 128,
                                                  ts * TS2:(ts + 1) * TS2],
                                     dzv[bi][:, :], waits=[v3])
                        tmp_fr[bi] = d2
                bld.dma("sync", bc_i[w].ap()[0:NST, :], bt_l[:, :],
                        waits=[bc_ev[0]])
                if debug:
                    bld.dma("sync", dbg_btl.ap()[:, w * TW:(w + 1) * TW],
                            bt_l[:, :], waits=[bc_ev[0]])
                    if w == NW - 1:
                        bld.dma("sync", dbg_xp.ap(), xp_s[:, :].bitcast(F32))
                bld.dma("sync", bc_i[w].ap()[NST:2 * NST, :], ct_l[:, :],
                        waits=[bc_ev[1]])
                spill_done = bld.fence("sync")
                bld.cc("AllToAll", dt_i[w], dt_o[w], waits=[spill_done])
                bld.cc("AllToAll", dz_i[w], dz_o[w])
                bld.cc("AllGather", bc_i[w], bc_o[w])
        bld.barrier()

        with ExitStack() as pst:
            dtt = [sb(pst, f"dtt{i}", [128, TW]) for i in range(2)]
            dzt = [sb(pst, f"dzt{i}", [128, TW]) for i in range(2)]
            dat = [sb(pst, f"dat{i}", [128, TW]) for i in range(2)]
            dbz = [sb(pst, f"dbz{i}", [128, TW]) for i in range(2)]
            ht = [sb(pst, f"ht{i}", [128, TW]) for i in range(2)]
            ymt = [sb(pst, f"ymt{i}", [128, TW]) for i in range(2)]
            yac = [sb(pst, f"yac{i}", [128, TW]) for i in range(2)]
            btile = sb(pst, "btile", [NST, TW], F32R)
            ctile = sb(pst, "ctile", [NST, TW], F32R)
            carry = [sb(pst, f"carry{i}", [128, NST]) for i in range(2)]

            for w in range(NW if STOP >= 3 else 0):
                cctok = ("cc", 3 * (w + 1))
                bld.barrier()
                ct_toks = [MSET(carry[0][:, :], 0.0),
                           MSET(carry[1][:, :], 0.0)]
                carry_tok = ct_toks
                for c in range(R):
                    bld.barrier()
                    dtk, dzk = [], []
                    for dc in range(2):
                        dtk.append(bld.dma(
                            "sync", dtt[dc][:, :],
                            dt_o[w].ap()[c * 256 + dc * 128:
                                         c * 256 + (dc + 1) * 128, :],
                            waits=[cctok]))
                        dzk.append(bld.dma(
                            "sync", dzt[dc][:, :],
                            dz_o[w].ap()[c * 256 + dc * 128:
                                         c * 256 + (dc + 1) * 128, :],
                            waits=[cctok]))
                    btk = bld.dma("sync", btile[:, :],
                                  bc_o[w].ap()[32 * c:32 * c + NST, :]
                                  .bitcast(F32R), waits=[cctok])
                    ctk = bld.dma("sync", ctile[:, :],
                                  bc_o[w].ap()[32 * c + NST:32 * c + 2 * NST, :]
                                  .bitcast(F32R), waits=[cctok])
                    bb_rd = [None, None]
                    cc_rd = [None, None]
                    sc_tok = [None, None]
                    cv_tok = [None, None]
                    ya_tok = [None, None]
                    yac_fr = [None, None]
                    for n in range(NST):
                        p = n % 2
                        bbm = ccm = None
                        for hf in range(2):
                            bbm = MM(psT[2 * p][:, hf * 512:(hf + 1) * 512],
                                     sel[:, n * 128:(n + 1) * 128],
                                     btile[:, hf * 512:(hf + 1) * 512],
                                     True, True, waits=[btk, bb_rd[p]])
                            ccm = MM(psT[2 * p + 1][:, hf * 512:(hf + 1) * 512],
                                     sel[:, n * 128:(n + 1) * 128],
                                     ctile[:, hf * 512:(hf + 1) * 512],
                                     True, True, waits=[ctk, cc_rd[p]])
                        for dc in range(2):
                            i2 = (2 * n + dc) % 2
                            av = ACTI(dat[i2][:, :], dtt[dc][:, :], AF.Exp,
                                      scale=an[:, dc * 16 + n:dc * 16 + n + 1],
                                      waits=[dtk[dc], sc_tok[i2]])
                            bv = TT(dbz[i2][:, :], dzt[dc][:, :],
                                    psT[2 * p][:, :], ALU.mult,
                                    waits=[dzk[dc], bbm, sc_tok[i2]])
                            bb_rd[p] = bv
                            hv = SCAN(ht[i2][:, :], dat[i2][:, :],
                                      dbz[i2][:, :], carry[dc][:, n:n + 1],
                                      waits=[av, bv, carry_tok[dc],
                                             ya_tok[i2], cv_tok[i2]])
                            sc_tok[i2] = hv
                            cv = CP(carry[dc][:, n:n + 1], ht[i2][:, TW - 1:TW],
                                    waits=[hv])
                            cv_tok[i2] = cv
                            carry_tok[dc] = cv
                            yv = TT(ymt[i2][:, :], ht[i2][:, :],
                                    psT[2 * p + 1][:, :], ALU.mult,
                                    waits=[hv, ccm, ya_tok[i2]])
                            cc_rd[p] = yv
                            if n == 0:
                                ya = CP(yac[dc][:, :], ymt[i2][:, :],
                                        waits=[yv, yac_fr[dc]], eng="gpsimd")
                            else:
                                ya = TT(yac[dc][:, :], yac[dc][:, :],
                                        ymt[i2][:, :], ALU.add,
                                        waits=[yv, yac_fr[dc]], eng="gpsimd")
                            ya_tok[i2] = ya
                            yac_fr[dc] = ya
                    for dc in range(2):
                        bld.dma("sync",
                                y_i[w].ap()[c * 256 + dc * 128:
                                            c * 256 + (dc + 1) * 128, :],
                                yac[dc][:, :], waits=[yac_fr[dc]])
                ysp = bld.fence("sync")
                bld.cc("AllToAll", y_i[w], y_o[w], waits=[ysp])
        bld.barrier()
        if debug:
            with ExitStack() as pst:
                tb_ = sb(pst, "dbgt", [128, TW])
                prev = None
                for w in range(NW if STOP >= 4 else 0):
                    for src_t, dst_t, nch in ((y_o[w], dbg_y[w], EC),
                                              (dt_o[w], dbg_dt[w], EC),
                                              (dz_o[w], dbg_dz[w], EC),
                                              (bc_o[w], dbg_bc[w], 2)):
                        pass
                    for src_t, dst_t, nch in ((y_o[w], dbg_y[w], EC),
                                              (dt_o[w], dbg_dt[w], EC),
                                              (dz_o[w], dbg_dz[w], EC),
                                              (bc_o[w], dbg_bc[w], 2)):
                        for e_ in range(nch):
                            t1 = bld.dma("sync", tb_[:, :],
                                         src_t.ap()[e_ * 128:(e_ + 1) * 128, :],
                                         waits=[prev])
                            prev = bld.dma(
                                "sync",
                                dst_t.ap()[e_ * 128:(e_ + 1) * 128, :],
                                tb_[:, :], waits=[t1])
                    t1 = bld.dma("sync", tb_[0:32, :], bc_i[w].ap()[:, :],
                                 waits=[prev])
                    prev = bld.dma("sync", dbg_bci[w].ap()[:, :],
                                   tb_[0:32, :], waits=[t1])
            bld.barrier()

        with ExitStack() as pst:
            ot_s = sb(pst, "ot_s", [128, 16 * 1024], F32R)
            for k in range(EC):
                bld.dma("scalar", ot_s[:, k * 1024:(k + 1) * 1024],
                        outT.ap()[k * 128:(k + 1) * 128, :])
            w_tok = bld.fence("scalar")
            yy = [sb(pst, f"yy{i}", [128, TSB]) for i in range(EC)]
            zz2 = [sb(pst, f"zz2{i}", [128, TSB], F32R) for i in range(EC)]
            yf = [sb(pst, f"yf{i}", [128, TSB], F32R) for i in range(EC)]
            uu = [sb(pst, f"uu{i}", [128, TSB]) for i in range(DC)]
            xat = [sb(pst, f"xat{i}", [128, TSB], F32R) for i in range(2)]
            for w in range(NW if STOP >= 5 else 0):
                for ts in range(TW // TSB):
                    bld.barrier()
                    tb = w * TW + ts * TSB
                    yftk = []
                    for k in range(EC):
                        a_ = bld.dma("sync", yy[k][:, :],
                                     y_o[w].ap()[k * 128:(k + 1) * 128,
                                                 ts * TSB:(ts + 1) * TSB])
                        b_ = bld.dma("sync", zz2[k][:, :],
                                     z_sp.ap()[k * 128:(k + 1) * 128,
                                               tb:tb + TSB])
                        yftk.append(STT(yf[k][:, :], zz2[k][:, :].bitcast(F32),
                                        dpv_s[:, k:k + 1], yy[k][:, :],
                                        ALU.mult, ALU.add, waits=[a_, b_]))
                    utk = []
                    for dc in range(DC):
                        utk.append(bld.dma(
                            "sync", uu[dc][:, :],
                            u0_sp.ap()[dc * 128:(dc + 1) * 128, tb:tb + TSB]))
                    bank_rd = [None, None]
                    xa_fr = [None, None]
                    for dm in range(DC):
                        bi = dm % 2
                        mm_ = None
                        for k in range(EC):
                            mm_ = MM(bank(bi)[:, 0:TSB],
                                     ot_s[:, k * 1024 + dm * 128:
                                          k * 1024 + (dm + 1) * 128],
                                     yf[k][:, :], k == 0, k == EC - 1,
                                     waits=([yftk[-1], w_tok, bank_rd[bi]]
                                            if k == 0 else []))
                        xv = TT(xat[bi][:, :], bank(bi)[:, 0:TSB],
                                uu[dm][:, :], ALU.add,
                                waits=[mm_, utk[dm], xa_fr[bi]])
                        bank_rd[bi] = xv
                        xa_fr[bi] = bld.dma(
                            "sync",
                            xa_sp.ap()[dm * 128:(dm + 1) * 128, tb:tb + TSB],
                            xat[bi][:, :], waits=[xv])
        bld.barrier()

        with ExitStack() as pst:
            gup_s = sb(pst, "gup_s", [128, 8 * 5504], BF16)
            for k in range(8):
                bld.dma("gpsimd", gup_s[:, k * 5504:(k + 1) * 5504],
                        gupT.ap()[k * 128:(k + 1) * 128, :])
            dn_s = sb(pst, "dn_s", [128, 22 * 1024], BF16)
            for kc in range(HC):
                rows_n = 128 if kc < HC - 1 else H_LAST
                bld.dma("gpsimd", dn_s[0:rows_n, kc * 1024:(kc + 1) * 1024],
                        dnT.ap()[kc * 128:kc * 128 + rows_n, :])
            w_tok = bld.fence("gpsimd")
            xat2 = [sb(pst, f"xat2_{i}", [128, TSB], F32R) for i in range(DC)]
            hnb = [sb(pst, f"hnb{i}", [128, TSB], BF16) for i in range(DC)]
            hnt = sb(pst, "hnt", [128, TSB])
            sq2 = [sb(pst, f"sq2_{i}", [128, TSB], F32R) for i in range(2)]
            sgt2 = [sb(pst, f"sgt2_{i}", [128, TSB]) for i in range(2)]
            slab = sb(pst, "slab", [128, HC * TSB], BF16)
            xfmt = [sb(pst, f"xfmt{i}", [128, TSB]) for i in range(DC)]
            x1t = [sb(pst, f"x1t{i}", [128, TSB], F32R) for i in range(DC)]
            u1b = sb(pst, "u1b", [128, TSB], BF16)
            tmp2 = sb(pst, "tmp2", [128, TSB])

            for w in range(NW if STOP >= 6 else 0):
                for ts in range(TW // TSB):
                    bld.barrier()
                    tb = w * TW + ts * TSB
                    xtk = []
                    for dc in range(DC):
                        xtk.append(bld.dma(
                            "sync", xat2[dc][:, :],
                            xa_sp.ap()[dc * 128:(dc + 1) * 128, tb:tb + TSB]))
                    m2_ = None
                    sq_rd = [None, None]
                    for dc in range(DC):
                        sv = ACTI(sq2[dc % 2][:, :],
                                  xat2[dc][:, :].bitcast(F32),
                                  AF.Square, waits=[xtk[dc], sq_rd[dc % 2]])
                        m2_ = MM(bank(1)[0:1, 0:TSB], onc[:, :],
                                 sq2[dc % 2][:, :], dc == 0, dc == DC - 1,
                                 waits=[sv])
                        sq_rd[dc % 2] = m2_
                    v2 = CP(rs(1), bank(1)[0:1, 0:TSB], waits=[m2_])
                    a = TSC(rs(3), rs(1), 1.0 / D, ALU.mult, waits=[v2])
                    a = TSC(rs(7), rs(3), 1e-6, ALU.add, waits=[a])
                    s1 = ACTI(rs(4), rs(7).bitcast(F32), AF.Ln, waits=[a])
                    s1 = ACTI(rs(4), rs(4).bitcast(F32), AF.Exp, scale=-0.5,
                              waits=[s1])
                    bs = MM(bank(2)[:, 0:TSB], on1[:, :], rs(4), True, True,
                            waits=[s1])
                    hn_last = None
                    for dc in range(DC):
                        t1 = TT(hnt[:, :], xat2[dc][:, :].bitcast(F32),
                                bank(2)[:, 0:TSB], ALU.mult,
                                waits=[bs, hn_last])
                        hn_last = ACTI(hnb[dc][:, :], hnt[:, :], AF.Identity,
                                       scale=fnw_s[:, dc:dc + 1], waits=[t1])
                    gu_rd = [None, None]
                    sg_fr = [None, None]
                    pv = None
                    for mc in range(HC):
                        rows_n = 128 if mc < HC - 1 else H_LAST
                        mg = mu = None
                        for k in range(8):
                            mg = MM(bank(4)[0:rows_n, 0:TSB],
                                    gup_s[:, k * 5504 + mc * 128:
                                          k * 5504 + mc * 128 + rows_n],
                                    hnb[k][:, :], k == 0, k == 7,
                                    waits=([hn_last, w_tok, gu_rd[0]]
                                           if k == 0 else []))
                            mu = MM(bank(5)[0:rows_n, 0:TSB],
                                    gup_s[:, k * 5504 + H + mc * 128:
                                          k * 5504 + H + mc * 128 + rows_n],
                                    hnb[k][:, :], k == 0, k == 7,
                                    waits=([gu_rd[1]] if k == 0 else []))
                        sv = ACTI(sgt2[mc % 2][0:rows_n, :],
                                  bank(4)[0:rows_n, 0:TSB], AF.Silu,
                                  waits=[mg, sg_fr[mc % 2]])
                        pv = TT(slab[0:rows_n, mc * TSB:(mc + 1) * TSB],
                                sgt2[mc % 2][0:rows_n, :],
                                bank(5)[0:rows_n, 0:TSB], ALU.mult,
                                waits=[sv, mu])
                        gu_rd[0] = sv
                        gu_rd[1] = pv
                        sg_fr[mc % 2] = pv
                    prod_all = pv
                    dn_rd = [None, None]
                    ao_toks = []
                    for dm in range(DC):
                        bi = 6 + dm % 2
                        md = None
                        for kc in range(HC):
                            rows_n = 128 if kc < HC - 1 else H_LAST
                            md = MM(bank(bi)[:, 0:TSB],
                                    dn_s[0:rows_n, kc * 1024 + dm * 128:
                                         kc * 1024 + (dm + 1) * 128],
                                    slab[0:rows_n, kc * TSB:(kc + 1) * TSB],
                                    kc == 0, kc == HC - 1,
                                    waits=([prod_all, dn_rd[dm % 2]]
                                           if kc == 0 else []))
                        av = TT(xat2[dm][:, :], bank(bi)[:, 0:TSB],
                                xat2[dm][:, :].bitcast(F32), ALU.add,
                                waits=[md])
                        dn_rd[dm % 2] = av
                        ao_toks.append(av)
                    m1_ = m2_ = None
                    sq_rd = [None, None]
                    for dc in range(DC):
                        sv = ACTI(sq2[dc % 2][:, :],
                                  xat2[dc][:, :].bitcast(F32), AF.Square,
                                  waits=[ao_toks[dc], sq_rd[dc % 2]])
                        m1_ = MM(bank(0)[0:1, 0:TSB], onc[:, :],
                                 xat2[dc][:, :], dc == 0, dc == DC - 1,
                                 waits=[ao_toks[dc]])
                        m2_ = MM(bank(1)[0:1, 0:TSB], onc[:, :],
                                 sq2[dc % 2][:, :], dc == 0, dc == DC - 1,
                                 waits=[sv])
                        sq_rd[dc % 2] = m2_
                    v1 = CP(rs(0), bank(0)[0:1, 0:TSB], waits=[m1_, m2_])
                    v2 = CP(rs(1), bank(1)[0:1, 0:TSB], waits=[m2_])
                    a = TSC(rs(2), rs(0), 1.0 / D, ALU.mult, waits=[v1])
                    a = TT(rs(7), rs(2), rs(2), ALU.mult, waits=[a])
                    a = STT(rs(3), rs(1), 1.0 / D, rs(7), ALU.mult,
                            ALU.subtract, waits=[a, v2])
                    a = TSC(rs(7), rs(3), 1e-5, ALU.add, waits=[a])
                    s1 = ACTI(rs(4), rs(7).bitcast(F32), AF.Ln, waits=[a])
                    s1 = ACTI(rs(4), rs(4).bitcast(F32), AF.Exp, scale=-0.5,
                              waits=[s1])
                    bm = MM(bank(2)[:, 0:TSB], on1[:, :], rs(2), True, True,
                            waits=[a])
                    br = MM(bank(3)[:, 0:TSB], on1[:, :], rs(4), True, True,
                            waits=[s1])
                    xf_tk = []
                    for dc in range(DC):
                        xf_tk.append(bld.dma(
                            "sync", xfmt[dc][:, :],
                            xfm_sp.ap()[dc * 128:(dc + 1) * 128, tb:tb + TSB]
                            .bitcast(F32)))
                    x1_toks = []
                    for dc in range(DC):
                        t1 = TT(tmp2[:, :], xat2[dc][:, :].bitcast(F32),
                                bank(2)[:, 0:TSB], ALU.subtract,
                                waits=[bm, x1_toks[-1] if x1_toks else None])
                        t2 = TT(tmp2[:, :], tmp2[:, :], bank(3)[:, 0:TSB],
                                ALU.mult, waits=[t1, br])
                        t3 = TT(x1t[dc][:, :], tmp2[:, :], xfmt[dc][:, :],
                                ALU.add, waits=[t2, xf_tk[dc]])
                        bld.dma("sync",
                                x1_sp.ap()[dc * 128:(dc + 1) * 128,
                                           tb:tb + TSB],
                                x1t[dc][:, :], waits=[t3])
                        x1_toks.append(t3)
                    m1_ = m2_ = None
                    sq_rd = [None, None]
                    for dc in range(DC):
                        sv = ACTI(sq2[dc % 2][:, :], x1t[dc][:, :].bitcast(F32),
                                  AF.Square, waits=[x1_toks[dc], sq_rd[dc % 2]])
                        m1_ = MM(bank(0)[0:1, 0:TSB], onc[:, :],
                                 x1t[dc][:, :], dc == 0, dc == DC - 1,
                                 waits=[x1_toks[dc]])
                        m2_ = MM(bank(1)[0:1, 0:TSB], onc[:, :],
                                 sq2[dc % 2][:, :], dc == 0, dc == DC - 1,
                                 waits=[sv])
                        sq_rd[dc % 2] = m2_
                    v1 = CP(rs(0), bank(0)[0:1, 0:TSB], waits=[m1_, m2_])
                    v2 = CP(rs(1), bank(1)[0:1, 0:TSB], waits=[m2_])
                    a = TSC(rs(2), rs(0), 1.0 / D, ALU.mult, waits=[v1])
                    a = TT(rs(7), rs(2), rs(2), ALU.mult, waits=[a])
                    a = STT(rs(3), rs(1), 1.0 / D, rs(7), ALU.mult,
                            ALU.subtract, waits=[a, v2])
                    a = TSC(rs(7), rs(3), 1e-5, ALU.add, waits=[a])
                    s1 = ACTI(rs(4), rs(7).bitcast(F32), AF.Ln, waits=[a])
                    s1 = ACTI(rs(4), rs(4).bitcast(F32), AF.Exp, scale=-0.5,
                              waits=[s1])
                    bm = MM(bank(2)[:, 0:TSB], on1[:, :], rs(2), True, True,
                            waits=[a])
                    br = MM(bank(3)[:, 0:TSB], on1[:, :], rs(4), True, True,
                            waits=[s1])
                    u1_fr = None
                    for dc in range(DC):
                        t1 = TT(tmp2[:, :], x1t[dc][:, :].bitcast(F32),
                                bank(2)[:, 0:TSB], ALU.subtract,
                                waits=[bm, u1_fr])
                        t2 = TT(u1b[:, :], tmp2[:, :], bank(3)[:, 0:TSB],
                                ALU.mult, waits=[t1, br])
                        u1_fr = bld.dma(
                            "sync",
                            u1_sp.ap()[dc * 128:(dc + 1) * 128, tb:tb + TSB],
                            u1b[:, :], waits=[t2])
        bld.barrier()

        with ExitStack() as pst:
            TS3 = 512
            u1r = [[sb(pst, f"u1r{w}_{c}", [128, TW], BF16) for c in range(DC)]
                   for w in range(NW)]
            ptl = [[sb(pst, f"ptl{w}_{m}", [128, TW], F32R) for m in range(DC)]
                   for w in range(NW)]
            cpjt = [sb(pst, f"cpjt{i}", [128, 1024], BF16) for i in range(EC)]
            wbig = sb(pst, "wbig", [128, 2048], BF16)
            slab3 = sb(pst, "slab3", [128, 16 * TS3], BF16)
            x1r = [sb(pst, f"x1r{i}", [128, TS3]) for i in range(DC)]
            sq3 = [sb(pst, f"sq3_{i}", [128, TS3], F32R) for i in range(2)]
            tmp3 = sb(pst, "tmp3", [128, TS3])
            ostg = sb(pst, "ostg", [128, 1024])

            u1tk = {}
            for w in range(NW if STOP >= 7 else 0):
                for cch in range(DC):
                    u1tk[(w, cch)] = bld.dma(
                        "sync", u1r[w][cch][:, :],
                        u1_sp.ap()[cch * 128:(cch + 1) * 128,
                                   w * TW:(w + 1) * TW])
            pt_wr = {}
            for g in range(8 if STOP >= 7 else 0):
                bld.barrier()
                wb_tok = bld.dma("sync", wbig[:, :],
                                 wbig_sp.ap()[:, g * 2048:(g + 1) * 2048])
                ctk = []
                for kcl in range(EC):
                    ctk.append(bld.dma(
                        "gpsimd", cpjt[kcl][:, :],
                        cpj.ap()[2048 * g + kcl * 128:
                                 2048 * g + (kcl + 1) * 128, :]))
                cfence = bld.fence("gpsimd")
                for w in range(NW):
                    for ts in range(TW // TS3):
                        zb_rd = [None, None]
                        gl_toks = []
                        for kcl in range(EC):
                            bi = kcl % 2
                            zm = MM(bank(bi)[:, 0:TS3],
                                    wbig[:, kcl * 128:(kcl + 1) * 128],
                                    u1r[w][g][:, ts * TS3:(ts + 1) * TS3],
                                    True, True,
                                    waits=[wb_tok, u1tk[(w, g)], zb_rd[bi]])
                            gv = ACTI(slab3[:, kcl * TS3:(kcl + 1) * TS3],
                                      bank(bi)[:, 0:TS3], AF.Gelu, waits=[zm])
                            zb_rd[bi] = gv
                            gl_toks.append(gv)
                        gl_all = gl_toks[-1]
                        cp_rd = [None, None]
                        for dm in range(DC):
                            bi = 2 + dm % 2
                            md = None
                            for kcl in range(EC):
                                md = MM(bank(bi)[:, 0:TS3],
                                        cpjt[kcl][:, dm * 128:(dm + 1) * 128],
                                        slab3[:, kcl * TS3:(kcl + 1) * TS3],
                                        kcl == 0, kcl == EC - 1,
                                        waits=([gl_all, cfence,
                                                cp_rd[dm % 2]]
                                               if kcl == 0 else []))
                            dst = ptl[w][dm][:, ts * TS3:(ts + 1) * TS3]
                            if g == 0:
                                pv = CP(dst, bank(bi)[:, 0:TS3], waits=[md])
                            else:
                                pv = TT(dst, dst.bitcast(F32),
                                        bank(bi)[:, 0:TS3], ALU.add,
                                        waits=[md, pt_wr[(w, dm, ts)]])
                            cp_rd[dm % 2] = pv
                            pt_wr[(w, dm, ts)] = pv
            for w in range(NW if STOP >= 7 else 0):
                for ts in range(TW // TS3):
                    bld.barrier()
                    col = slice(ts * TS3, (ts + 1) * TS3)
                    tb = w * TW + ts * TS3
                    m1_ = m2_ = None
                    sq_rd = [None, None]
                    for dc in range(DC):
                        sv = ACTI(sq3[dc % 2][:, :],
                                  ptl[w][dc][:, col].bitcast(F32), AF.Square,
                                  waits=[sq_rd[dc % 2]])
                        m1_ = MM(bank(0)[0:1, 0:TS3], onc[:, :],
                                 ptl[w][dc][:, col], dc == 0, dc == DC - 1)
                        m2_ = MM(bank(1)[0:1, 0:TS3], onc[:, :],
                                 sq3[dc % 2][:, :], dc == 0, dc == DC - 1,
                                 waits=[sv])
                        sq_rd[dc % 2] = m2_
                    v1 = CP(rs(0, TS3), bank(0)[0:1, 0:TS3], waits=[m1_, m2_])
                    v2 = CP(rs(1, TS3), bank(1)[0:1, 0:TS3], waits=[m2_])
                    a = TSC(rs(2, TS3), rs(0, TS3), 1.0 / D, ALU.mult,
                            waits=[v1])
                    a = TT(rs(7, TS3), rs(2, TS3), rs(2, TS3), ALU.mult,
                           waits=[a])
                    a = STT(rs(3, TS3), rs(1, TS3), 1.0 / D, rs(7, TS3),
                            ALU.mult, ALU.subtract, waits=[a, v2])
                    a = TSC(rs(7, TS3), rs(3, TS3), 1e-5, ALU.add, waits=[a])
                    s1 = ACTI(rs(4, TS3), rs(7, TS3).bitcast(F32), AF.Ln,
                              waits=[a])
                    s1 = ACTI(rs(4, TS3), rs(4, TS3).bitcast(F32), AF.Exp,
                              scale=-0.5, waits=[s1])
                    bm = MM(bank(2)[:, 0:TS3], on1[:, :], rs(2, TS3),
                            True, True, waits=[a])
                    br = MM(bank(3)[:, 0:TS3], on1[:, :], rs(4, TS3),
                            True, True, waits=[s1])
                    x1tk = []
                    for dc in range(DC):
                        x1tk.append(bld.dma(
                            "sync", x1r[dc][:, :],
                            x1_sp.ap()[dc * 128:(dc + 1) * 128, tb:tb + TS3]
                            .bitcast(F32)))
                    x2_toks = []
                    for dc in range(DC):
                        t1 = TT(tmp3[:, :], ptl[w][dc][:, col].bitcast(F32),
                                bank(2)[:, 0:TS3], ALU.subtract,
                                waits=[bm, x2_toks[-1] if x2_toks else None])
                        t2 = TT(tmp3[:, :], tmp3[:, :], bank(3)[:, 0:TS3],
                                ALU.mult, waits=[t1, br])
                        t3 = TT(x1r[dc][:, :], tmp3[:, :], x1r[dc][:, :],
                                ALU.add, waits=[t2, x1tk[dc]])
                        x2_toks.append(t3)
                    x2_all = x2_toks[-1]
                    ev_fr = [None, None]
                    ostg_fr = None
                    for tt in range(TS3 // 128):
                        pi = 2 + tt % 2
                        trk = None
                        for dc in range(DC):
                            trk = TRA(psT[pi][:, dc * 128:(dc + 1) * 128],
                                      x1r[dc][:, tt * 128:(tt + 1) * 128],
                                      waits=[x2_all, ev_fr[tt % 2]])
                        ev = bld.op("scalar",
                                    lambda e, pi=pi: e.copy(ostg[:, :],
                                                            psT[pi][:, :]),
                                    waits=[trk, ostg_fr])
                        ev_fr[tt % 2] = bld.dma(
                            "sync",
                            out_t.ap()[tb + tt * 128:tb + (tt + 1) * 128, :],
                            ostg[:, :], waits=[ev])
                        ostg_fr = ev_fr[tt % 2]
        bld.barrier()
        bld.flush_barrier()

        with nc.allow_low_precision(reason="bf16 operands by design"), \
                nc.Block() as blk:
            @blk.sync
            def _(e):
                for f in bld.ops["sync"]:
                    f(e)

            @blk.gpsimd
            def _(e):
                for f in bld.ops["gpsimd"]:
                    f(e)

            @blk.vector
            def _(e):
                for f in bld.ops["vector"]:
                    f(e)

            @blk.scalar
            def _(e):
                for f in bld.ops["scalar"]:
                    f(e)

            @blk.tensor
            def _(e):
                for f in bld.ops["tensor"]:
                    f(e)
    return nc


def _prep(inputs):
    f = np.ascontiguousarray
    x = inputs["L0_attn_x"]
    ww = {}
    ww["wig"] = f(np.concatenate([inputs["L0_attn_in_proj_w"],
                                  inputs["L0_attn_gate_proj_w"]], 0).T)
    ww["dtT"] = f(inputs["L0_attn_dt_w"].T)
    ww["xpT"] = f(inputs["L0_attn_x_proj_w"].T.reshape(EC, 128, 2 * NST)
                  .transpose(1, 0, 2).reshape(128, EC * 2 * NST))
    ww["outT"] = f(inputs["L0_attn_out_proj_w"].T)
    ww["gupT"] = f(np.concatenate([inputs["L0_attn_ffn_gate_w"],
                                   inputs["L0_attn_ffn_up_w"]], 0).T)
    ww["dnT"] = f(inputs["L0_attn_ffn_down_w"].T)
    ww["cpj"] = f(inputs["L0_ffn_w_cproj"].reshape(D, 4, 64, 64)
                  .transpose(2, 1, 3, 0).reshape(16384, D))
    ww["makerJ"] = f(inputs["L0_ffn_rail_maker"].transpose(3, 0, 1, 2)
                     .reshape(NST, 4096))
    ww["upJ"] = f(inputs["L0_ffn_rail_up_proj"].transpose(3, 0, 1, 2)
                  .reshape(NST, 16384))
    ww["dtb"] = f(inputs["L0_attn_dt_b"].reshape(EC, 128).T)
    ww["dpv"] = f(inputs["L0_attn_Dp"].reshape(EC, 128).T)
    ww["n1w"] = f(inputs["L0_attn_norm1_w"].reshape(DC, 128).T)
    ww["fnw"] = f(inputs["L0_attn_ffn_norm_w"].reshape(DC, 128).T)
    ww["eye"] = np.eye(128, dtype=np.float32)
    selm = np.zeros((NST, NST * 128), np.float32)
    for n in range(NST):
        selm[n, n * 128:(n + 1) * 128] = 1.0
    ww["sel16"] = selm
    ww["ones1"] = np.ones((1, 128), np.float32)
    ww["onesc"] = np.ones((128, 1), np.float32)
    in_maps = []
    al = inputs["L0_attn_A_log"]
    for c in range(R):
        m = dict(ww)
        m["x"] = f(np.concatenate([x[0, c * TW:(c + 1) * TW],
                                   x[1, c * TW:(c + 1) * TW]], 0))
        m["a_log"] = f(al[c * 256:(c + 1) * 256].reshape(2, 128, NST)
                       .transpose(1, 0, 2).reshape(128, 2 * NST))
        in_maps.append(m)
    return in_maps


def kernel(**inputs):
    inputs = {k: np.asarray(v, dtype=np.float32) for k, v in inputs.items()}
    in_maps = _prep(inputs)
    nc = build(debug=False)
    res = run_bass_kernel_spmd(nc, in_maps, core_ids=list(range(R)))
    out = np.empty((2, L, D), np.float32)
    for c in range(R):
        o = res.results[c]["out"]
        out[0, c * TW:(c + 1) * TW] = o[:TW]
        out[1, c * TW:(c + 1) * TW] = o[TW:]
    return out


# version 05
# speedup vs baseline: 1.5195x; 1.0012x over previous
import os
os.environ.setdefault("TRN_ALLOW_ICI_ALL_TO_ALL", "1")
import numpy as np
import ml_dtypes
from collections import defaultdict
from contextlib import ExitStack

import concourse.bass as bass
import concourse.mybir as mybir
from concourse.bass_utils import run_bass_kernel_spmd

F32 = mybir.dt.float32
F32R = mybir.dt.float32r
BF16 = mybir.dt.bfloat16
AF = mybir.ActivationFunctionType
ALU = mybir.AluOpType

R = 8
NW = 2
L = 8192
D = 1024
E = 2048
NST = 16
H = 2752
TW = L // R
DC, EC = 8, 16
HC, H_LAST = 22, 64
TSA = 256
TSB = 256


class Bld:
    ENGINES = ("sync", "gpsimd", "vector", "scalar", "tensor")

    def __init__(self, nc):
        self.nc = nc
        self.ops = defaultdict(list)
        self.cnt = defaultdict(int)
        self.sems = {}
        self.pending = {}
        self.rr = {}

    KPOOL = {"sync": 8, "gpsimd": 4, "scalar": 4}

    def sem_names(self):
        names = ["vector", "scalar", "tensor", "gpsimd", "cc"]
        for q, k in self.KPOOL.items():
            names += [f"{q}d{i}" for i in range(k)]
        return names

    def fence(self, q):
        return [(f"{q}d{i}", self.cnt[f"{q}d{i}"])
                for i in range(self.KPOOL[q]) if self.cnt[f"{q}d{i}"] > 0]

    def fence_all(self):
        w = []
        for q in self.KPOOL:
            w += self.fence(q)
        return w

    def _emit(self, eng, waits, inc_sem, inc_n, fn):
        w = []
        for x in waits:
            if x is None:
                continue
            if isinstance(x, list):
                w.extend(x)
            else:
                w.append(x)
        w += self.pending.pop(eng, [])
        sems = self.sems

        def cl(e, fn=fn, w=w, inc_sem=inc_sem, inc_n=inc_n):
            for s, v in w:
                e.wait_ge(sems[s], v)
            ins = fn(e)
            if inc_sem is not None:
                ins.then_inc(sems[inc_sem], inc_n)
        self.ops[eng].append(cl)

    def op(self, eng, fn, waits=()):
        self.cnt[eng] += 1
        tok = (eng, self.cnt[eng])
        self._emit(eng, waits, eng, 1, fn)
        return tok

    def dma(self, q, out, in_, waits=()):
        i = self.rr.get(q, 0)
        self.rr[q] = (i + 1) % self.KPOOL[q]
        s = f"{q}d{i}"
        prev = (s, self.cnt[s])
        self.cnt[s] += 16
        tok = (s, self.cnt[s])
        self._emit(q, list(waits) + [prev], s, 16,
                   lambda e, out=out, in_=in_: e.dma_start(out=out, in_=in_))
        return tok

    def cc(self, kind, in_, out, waits=()):
        prev = ("cc", self.cnt["cc"])
        self.cnt["cc"] += 1
        tok = ("cc", self.cnt["cc"])
        self._emit("gpsimd", list(waits) + [prev], "cc", 1,
                   lambda e, k=kind, i=in_, o=out:
                   e.collective_compute(k, ALU.bypass, [list(range(R))],
                                        ins=[i.ap().opt()], outs=[o.ap().opt()]))
        return tok

    def barrier(self):
        snap = [(s, c) for s, c in self.cnt.items() if c > 0]
        for eng in self.ENGINES:
            self.pending[eng] = list(snap)

    def lag_barrier(self, key):
        if not hasattr(self, "lag"):
            self.lag = {}
        cur = [(s, c) for s, c in self.cnt.items() if c > 0]
        prev = self.lag.get(key)
        self.lag[key] = cur
        snap = prev if prev is not None else cur
        for eng in self.ENGINES:
            self.pending[eng] = list(snap)

    def flush_barrier(self):
        sems = self.sems
        for eng in self.ENGINES:
            w = self.pending.pop(eng, [])
            if not w:
                continue

            def cl(e, w=w):
                for s, v in w:
                    e.wait_ge(sems[s], v)
            self.ops[eng].append(cl)


def build(debug=False):
    nc = bass.Bass("TRN2", target_bir_lowering=False)

    def din(name, shape, dt=F32):
        return nc.dram_tensor(name, shape, dt, kind="ExternalInput")

    def dout(name, shape, dt=F32):
        return nc.dram_tensor(name, shape, dt, kind="ExternalOutput")

    def dint(name, shape, dt=F32, dbg_ok=True):
        kind = "ExternalOutput" if (debug and dbg_ok) else "Internal"
        return nc.dram_tensor(name, shape, dt, kind=kind)

    x_in = din("x", [NW * TW, D])
    wig = din("wig", [D, 2 * E], F32R)
    dtT = din("dtT", [E, E], F32R)
    xpT = din("xpT", [128, EC * 2 * NST], F32R)
    outT = din("outT", [E, D], F32R)
    gupT = din("gupT", [D, 2 * H])
    dnT = din("dnT", [H, D])
    cpj = din("cpj", [16384, D])
    makerJ = din("makerJ", [NST, 4096], F32R)
    upJ = din("upJ", [NST, 16384], F32R)
    a_log = din("a_log", [128, 2 * NST])
    dtb = din("dtb", [128, EC])
    dpv = din("dpv", [128, EC])
    n1w = din("n1w", [128, DC])
    fnw = din("fnw", [128, DC])
    eye_c = din("eye", [128, 128])
    sel16 = din("sel16", [NST, NST * 128], F32R)
    ones1 = din("ones1", [1, 128], F32R)
    onesc = din("onesc", [128, 1], F32R)

    out_t = dout("out", [NW * TW, D])
    z_sp = dint("z_sp", [E, NW * TW], F32R)
    xfm_sp = dint("xfm_sp", [D, NW * TW], F32R)
    u0_sp = dint("u0_sp", [D, NW * TW])
    xa_sp = dint("xa_sp", [D, NW * TW], F32R)
    x1_sp = dint("x1_sp", [D, NW * TW], F32R)
    u1_sp = dint("u1_sp", [D, NW * TW], BF16, dbg_ok=False)
    wbig_sp = dint("wbig_sp", [128, 8 * 2048], BF16, dbg_ok=False)
    dt_i = [dint(f"dt_i{w}", [E, TW], dbg_ok=False) for w in range(NW)]
    dt_o = [dint(f"dt_o{w}", [E, TW], dbg_ok=False) for w in range(NW)]
    dz_i = [dint(f"dz_i{w}", [E, TW], dbg_ok=False) for w in range(NW)]
    dz_o = [dint(f"dz_o{w}", [E, TW], dbg_ok=False) for w in range(NW)]
    y_i = [dint(f"y_i{w}", [E, TW], dbg_ok=False) for w in range(NW)]
    y_o = [dint(f"y_o{w}", [E, TW], dbg_ok=False) for w in range(NW)]
    bc_i = [dint(f"bc_i{w}", [2 * NST, TW], dbg_ok=False) for w in range(NW)]
    bc_o = [dint(f"bc_o{w}", [R * 2 * NST, TW], dbg_ok=False) for w in range(NW)]
    if debug:
        dbg_y = [dout(f"dbg_y{w}", [E, TW]) for w in range(NW)]
        dbg_dt = [dout(f"dbg_dt{w}", [E, TW]) for w in range(NW)]
        dbg_dz = [dout(f"dbg_dz{w}", [E, TW]) for w in range(NW)]
        dbg_bc = [dout(f"dbg_bc{w}", [R * 2 * NST, TW]) for w in range(NW)]
        dbg_bci = [dout(f"dbg_bci{w}", [2 * NST, TW]) for w in range(NW)]
        dbg_zz = dout("dbg_zz", [128, 2 * TW])
        dbg_xp = dout("dbg_xp", [128, EC * 2 * NST])
        dbg_xp0 = dout("dbg_xp0", [128, EC * 2 * NST])
        dbg_btl = dout("dbg_btl", [NST, 2 * TW])

    bld = Bld(nc)
    stack = ExitStack()
    with stack:
        for s in bld.sem_names():
            bld.sems[s] = stack.enter_context(nc.semaphore(s))

        def sb(st, name, shape, dt=F32):
            return st.enter_context(nc.sbuf_tensor(name, shape, dt))

        psT = [stack.enter_context(nc.psum_tensor(f"ps{i}", [128, 1024], F32))
               for i in range(4)]

        def bank(i):
            return psT[i // 2][:, (i % 2) * 512:(i % 2 + 1) * 512]

        eye = sb(stack, "eye_s", [128, 128])
        sel = sb(stack, "sel_s", [NST, NST * 128], F32R)
        on1 = sb(stack, "on1_s", [1, 128], F32R)
        onc = sb(stack, "onc_s", [128, 1], F32R)
        an = sb(stack, "an_s", [128, 2 * NST])
        dtb_s = sb(stack, "dtb_s", [128, EC])
        dpv_s = sb(stack, "dpv_s", [128, EC])
        n1w_s = sb(stack, "n1w_s", [128, DC])
        fnw_s = sb(stack, "fnw_s", [128, DC])
        rows_h = [None]
        _par = [0]

        def rs(slot, width=TSA):
            base = _par[0] * 4096 + slot * 512
            return rows_h[0][0:1, base:base + width]

        def TT(out, a, b, alu, waits=(), eng="vector"):
            return bld.op(eng, lambda e, out=out, a=a, b=b, alu=alu:
                          e.tensor_tensor(out, a, b, alu), waits)

        def CP(out, in_, waits=(), eng="vector"):
            return bld.op(eng, lambda e, out=out, in_=in_:
                          e.tensor_copy(out, in_), waits)

        def TSC(out, a, s1, alu, waits=(), eng="vector"):
            return bld.op(eng, lambda e, out=out, a=a, s1=s1, alu=alu:
                          e.tensor_single_scalar(out, a, s1, alu), waits)

        def STT(out, in0, sc, in1, op0, op1, waits=(), eng="vector"):
            return bld.op(eng,
                          lambda e, out=out, in0=in0, sc=sc, in1=in1,
                          op0=op0, op1=op1:
                          e.scalar_tensor_tensor(out, in0, sc, in1, op0, op1),
                          waits)

        def ACTI(out, in_, f, bias=0.0, scale=1.0, waits=()):
            return bld.op("scalar",
                          lambda e, out=out, in_=in_, f=f, b=bias, s=scale:
                          e.activation(out, in_, f, bias=b, scale=s), waits)

        def MM(out, lhsT, rhs, start, stop, waits=()):
            return bld.op("tensor",
                          lambda e, out=out, l=lhsT, r=rhs, st=start, sp=stop:
                          e.matmul(out, l, r, start=st, stop=sp), waits)

        def TRA(out, in_, waits=()):
            return bld.op("tensor",
                          lambda e, out=out, in_=in_:
                          e.transpose(out, in_, eye[:, :]), waits)

        def SCAN(out, d0, d1, init, waits=()):
            return bld.op("vector",
                          lambda e, out=out, d0=d0, d1=d1, init=init:
                          e.tensor_tensor_scan(out, d0, d1, init,
                                               ALU.mult, ALU.add), waits)

        def MSET(t, v, waits=(), eng="vector"):
            return bld.op(eng, lambda e, t=t, v=v: e.memset(t, v), waits)

        def stats_ln(src_tiles, sq_pair, width, with_mean=True):
            m1 = m2 = None
            for dc in range(DC):
                sv = ACTI(sq_pair[dc % 2][:, 0:width],
                          src_tiles[dc][:, 0:width].bitcast(F32), AF.Square)
                if with_mean:
                    m1 = MM(bank(0)[0:1, 0:width], onc[:, :],
                            src_tiles[dc][:, 0:width], dc == 0, dc == DC - 1)
                m2 = MM(bank(1)[0:1, 0:width], onc[:, :],
                        sq_pair[dc % 2][:, 0:width], dc == 0, dc == DC - 1,
                        waits=[sv])
            toks = []
            if with_mean:
                v1 = CP(rs(0, width), bank(0)[0:1, 0:width], waits=[m1])
                toks.append(v1)
            v2 = CP(rs(1, width), bank(1)[0:1, 0:width], waits=[m2])
            toks.append(v2)
            if with_mean:
                a = TSC(rs(2, width), rs(0, width), 1.0 / D, ALU.mult,
                        waits=[v1])
                a = TT(rs(7, width), rs(2, width), rs(2, width), ALU.mult,
                       waits=[a])
                a = STT(rs(3, width), rs(1, width), 1.0 / D, rs(7, width),
                        ALU.mult, ALU.subtract, waits=[a, v2])
                a = TSC(rs(7, width), rs(3, width), 1e-5, ALU.add, waits=[a])
            else:
                a = TSC(rs(3, width), rs(1, width), 1.0 / D, ALU.mult,
                        waits=[v2])
                a = TSC(rs(7, width), rs(3, width), 1e-6, ALU.add, waits=[a])
            s1 = ACTI(rs(4, width), rs(7, width).bitcast(F32), AF.Ln,
                      waits=[a])
            s1 = ACTI(rs(4, width), rs(4, width).bitcast(F32), AF.Exp,
                      scale=-0.5, waits=[s1])
            return s1, a

        for t, src in ((eye, eye_c), (sel, sel16), (on1, ones1), (onc, onesc),
                       (an, a_log), (dtb_s, dtb), (dpv_s, dpv),
                       (n1w_s, n1w), (fnw_s, fnw)):
            bld.dma("sync", t[:, :], src.ap())
        ld = bld.fence("sync")
        t_ = ACTI(an[:, :], an[:, :], AF.Exp, waits=[ld])
        bld.op("vector", lambda e: e.tensor_scalar_mul(an[:, :], an[:, :], -1.0),
               waits=[t_])

        STOP = int(os.environ.get("MK_STOP", "99"))
        with ExitStack() as pst:
            mkj = sb(pst, "mkj", [NST, 4096], F32R)
            upj = sb(pst, "upj", [NST, 16384], F32R)
            zt = sb(pst, "zt", [128, 2048], BF16)
            stg = sb(pst, "stg", [NST, 8 * 64], BF16)
            bld.dma("sync", mkj[:, :], makerJ.ap())
            bld.dma("sync", upj[:, :], upJ.ap())
            wtok = bld.fence("sync")
            zv = MSET(zt[:, :], 0.0)
            for g in range(8):
                bld.dma("sync", wbig_sp.ap()[:, g * 2048:(g + 1) * 2048],
                        zt[:, :], waits=[zv])
            zdone = bld.fence("sync")
            ev_tok = [None] * 8
            dm_tok = [None] * 8
            for h in range(4):
                for r in range(64):
                    hr = h * 64 + r
                    slot = hr % 8
                    pb = bank(slot)[0:NST, 0:64]
                    tt = MM(pb, mkj[:, hr * 16:hr * 16 + 16],
                            upj[:, hr * 64:hr * 64 + 64], True, True,
                            waits=[wtok, ev_tok[slot]])
                    ss_ = stg[0:NST, slot * 64:(slot + 1) * 64]
                    vv = CP(ss_, pb, waits=[tt, dm_tok[slot]])
                    g = r // 8
                    kcl = 2 * (r % 8) + h // 2
                    col = g * 2048 + kcl * 128 + (h % 2) * 64
                    dm_tok[slot] = bld.dma(
                        "sync",
                        wbig_sp.ap()[16 * (r % 8):16 * (r % 8) + 16,
                                     col:col + 64],
                        ss_, waits=[vv, zdone])
                    ev_tok[slot] = vv
        bld.barrier()

        with ExitStack() as pst:
            wig_s = sb(pst, "wig_s", [128, 8 * 4096], F32R)
            for k in range(8):
                bld.dma("scalar", wig_s[:, k * 4096:(k + 1) * 4096],
                        wig.ap()[k * 128:(k + 1) * 128, :])
            wig_tok = bld.fence("scalar")
            rows_h[0] = sb(pst, "rows0", [1, 8192], F32R)
            stage = [sb(pst, f"xstg{i}", [128, D]) for i in range(2)]
            xfm = [sb(pst, f"xfm{i}", [128, TSA], F32R) for i in range(DC)]
            sqp = [sb(pst, f"sq{i}", [128, TSA], F32R) for i in range(2)]
            u0t = [sb(pst, f"u0t{i}", [128, TSA]) for i in range(2)]
            h1t = sb(pst, "h1t", [128, TSA])
            h1 = [sb(pst, f"h1_{i}", [128, TSA], F32R) for i in range(DC)]
            sgt = [sb(pst, f"sgt{i}", [128, TSA]) for i in range(2)]
            zot = [sb(pst, f"zot{i}", [128, TSA], F32R) for i in range(2)]

            p_tra = [None, None]
            p_apply = {2: None, 3: None, 4: None}
            p_zg67 = [None, None]
            p_zg01 = [None, None]
            p_stage = [None, None]
            p_sq = [None, None]
            p_h1rd = None
            p_xfm_a1 = [None] * DC
            p_xfm_dma = [None] * DC
            u0_fr = [None, None]
            zo_fr = [None, None]
            h1_last = None
            for w in range(NW if STOP >= 1 else 0):
                for ts in range(TW // TSA):
                    it = w * (TW // TSA) + ts
                    par = it % 2
                    _par[0] = par
                    if os.environ.get("MK_A1_BAR"):
                        bld.barrier()
                    else:
                        bld.lag_barrier("a1")
                    tb = w * TW + ts * TSA
                    ntt = TSA // 128
                    ltok = []
                    for tt in range(ntt):
                        ltok.append(bld.dma(
                            "sync", stage[tt][:, :],
                            x_in.ap()[tb + tt * 128:tb + (tt + 1) * 128, :],
                            waits=[p_stage[tt]]))
                    ev8 = []
                    tra_ev = list(p_tra)
                    for dc in range(DC):
                        bi = 4 + (dc % 2)
                        ttk = None
                        for tt in range(ntt):
                            tw = [ltok[tt], tra_ev[dc % 2]]
                            if dc % 2 == 0:
                                tw.append(p_apply[4])
                            ttk = TRA(bank(bi)[:, tt * 128:(tt + 1) * 128],
                                      stage[tt][:, dc * 128:(dc + 1) * 128],
                                      waits=tw)
                            p_stage[tt] = ttk
                        ev = bld.op("scalar",
                                    lambda e, xf=xfm[dc], bi=bi:
                                    e.copy(xf[:, :], bank(bi)[:, 0:TSA]),
                                    waits=[ttk, p_xfm_a1[dc], p_xfm_dma[dc]])
                        tra_ev[dc % 2] = ev
                        ev8.append(ev)
                        p_xfm_dma[dc] = bld.dma(
                            "sync",
                            xfm_sp.ap()[dc * 128:(dc + 1) * 128,
                                        tb:tb + TSA],
                            xfm[dc][:, :], waits=[ev])
                    p_tra = tra_ev
                    m1 = m2 = None
                    sq_rd = p_sq
                    for dc in range(DC):
                        sq_ = sqp[dc % 2]
                        sv = ACTI(sq_[:, :], xfm[dc][:, :].bitcast(F32),
                                  AF.Square, waits=[ev8[dc], sq_rd[dc % 2]])
                        m1 = MM(bank(0)[0:1, 0:TSA], onc[:, :], xfm[dc][:, :],
                                dc == 0, dc == DC - 1,
                                waits=[ev8[dc], p_zg01[0]] if dc == 0
                                else [ev8[dc]])
                        m2 = MM(bank(1)[0:1, 0:TSA], onc[:, :],
                                sq_[:, :], dc == 0, dc == DC - 1,
                                waits=[sv, p_zg01[1]] if dc == 0 else [sv])
                        sq_rd[dc % 2] = m2
                    p_sq = sq_rd
                    v1 = CP(rs(0), bank(0)[0:1, 0:TSA], waits=[m1, m2])
                    v2 = CP(rs(1), bank(1)[0:1, 0:TSA], waits=[m2])
                    a = TSC(rs(2), rs(0), 1.0 / D, ALU.mult, waits=[v1])
                    a = TT(rs(7), rs(2), rs(2), ALU.mult, waits=[a])
                    a = STT(rs(3), rs(1), 1.0 / D, rs(7), ALU.mult,
                            ALU.subtract, waits=[a, v2])
                    a = TSC(rs(7), rs(3), 1e-5, ALU.add, waits=[a])
                    s1 = ACTI(rs(4), rs(7).bitcast(F32), AF.Ln, waits=[a])
                    s1 = ACTI(rs(4), rs(4).bitcast(F32), AF.Exp, scale=-0.5,
                              waits=[s1])
                    mr = TT(rs(6), rs(2), rs(4), ALU.mult, waits=[s1])
                    rc = bld.op("vector",
                                lambda e, o=rs(0), i_=rs(7): e.reciprocal(o, i_),
                                waits=[s1])
                    q = TT(rs(1), rs(3), rs(0), ALU.mult, waits=[rc])
                    q = TSC(rs(7), rs(1), 1e-6, ALU.add, waits=[q])
                    s2 = ACTI(rs(5), rs(7).bitcast(F32), AF.Ln, waits=[q])
                    s2 = ACTI(rs(5), rs(5).bitcast(F32), AF.Exp, scale=-0.5,
                              waits=[s2])
                    b_r1 = MM(bank(2)[:, 0:TSA], on1[:, :], rs(4), True, True,
                              waits=[s1, p_apply[2]])
                    b_mr = MM(bank(3)[:, 0:TSA], on1[:, :], rs(6), True, True,
                              waits=[mr, p_apply[3]])
                    b_s2 = MM(bank(4)[:, 0:TSA], on1[:, :], rs(5), True, True,
                              waits=[s2, tra_ev[0]])
                    a1 = a2 = a3 = None
                    for dc in range(DC):
                        ui = dc % 2
                        a1 = TT(u0t[ui][:, :], xfm[dc][:, :].bitcast(F32),
                                bank(2)[:, 0:TSA], ALU.mult,
                                waits=[b_r1, u0_fr[dc % 2]])
                        p_xfm_a1[dc] = a1
                        a2 = TT(u0t[ui][:, :], u0t[ui][:, :], bank(3)[:, 0:TSA],
                                ALU.subtract, waits=[a1, b_mr])
                        u0_fr[dc % 2] = bld.dma(
                            "sync",
                            u0_sp.ap()[dc * 128:(dc + 1) * 128, tb:tb + TSA],
                            u0t[ui][:, :], waits=[a2])
                        a3 = TT(h1t[:, :], u0t[ui][:, :], bank(4)[:, 0:TSA],
                                ALU.mult, waits=[a2, b_s2, h1_last])
                        h1_last = ACTI(h1[dc][:, :], h1t[:, :], AF.Identity,
                                       scale=n1w_s[:, dc:dc + 1],
                                       waits=[a3, p_h1rd if dc == 0 else None])
                    p_apply = {2: a1, 3: a2, 4: a3}
                    zg_rd = {0: v1, 1: v2, 6: p_zg67[0], 7: p_zg67[1]}
                    for ei in range(EC):
                        zb, gb = (6, 7) if ei % 2 == 0 else (0, 1)
                        mmz = mmg = None
                        for k in range(8):
                            mmz = MM(bank(zb)[:, 0:TSA],
                                     wig_s[:, k * 4096 + ei * 128:
                                           k * 4096 + (ei + 1) * 128],
                                     h1[k][:, :], k == 0, k == 7,
                                     waits=([h1_last, zg_rd[zb], wig_tok]
                                            if k == 0 else []))
                            mmg = MM(bank(gb)[:, 0:TSA],
                                     wig_s[:, k * 4096 + 2048 + ei * 128:
                                           k * 4096 + 2048 + (ei + 1) * 128],
                                     h1[k][:, :], k == 0, k == 7,
                                     waits=([zg_rd[gb]] if k == 0 else []))
                        si = ei % 2
                        sgv = ACTI(sgt[si][:, :], bank(gb)[:, 0:TSA],
                                   AF.Sigmoid, waits=[mmg, zo_fr[ei % 2]])
                        zv = TT(zot[si][:, :], bank(zb)[:, 0:TSA],
                                sgt[si][:, :], ALU.mult, waits=[mmz, sgv])
                        zo_fr[ei % 2] = bld.dma(
                            "sync",
                            z_sp.ap()[ei * 128:(ei + 1) * 128, tb:tb + TSA],
                            zot[si][:, :], waits=[zv])
                        zg_rd[zb] = zv
                        zg_rd[gb] = sgv
                    p_zg67 = [zg_rd[6], zg_rd[7]]
                    p_zg01 = [zg_rd[0], zg_rd[1]]
                    p_h1rd = ("tensor", bld.cnt["tensor"])
        bld.barrier()

        with ExitStack() as pst:
            if not os.environ.get("MK_A2_NODT"):
                dt_s = sb(pst, "dt_s", [128, 16 * 2048], F32R)
                for k in range(EC):
                    bld.dma("scalar", dt_s[:, k * 2048:(k + 1) * 2048],
                            dtT.ap()[k * 128:(k + 1) * 128, :])
            xp_s = sb(pst, "xp_s", [128, 16 * 32], F32R)
            xp_ld = bld.dma("scalar", xp_s[:, :], xpT.ap())
            if debug:
                bld.dma("sync", dbg_xp0.ap(), xp_s[:, :].bitcast(F32),
                        waits=[xp_ld])
            w_tok = bld.fence("scalar")
            TS2 = 256
            NZB = 2
            zz_all = [[sb(pst, f"zz{j}_{i}", [128, TS2], F32R)
                       for i in range(EC)] for j in range(NZB)]
            pt = [sb(pst, f"pt{i}", [128, TS2]) for i in range(2)]
            ab = [sb(pst, f"ab{i}", [128, TS2]) for i in range(2)]
            dtv = [sb(pst, f"dtv{i}", [128, TS2]) for i in range(2)]
            dzv = ab
            bt_l = sb(pst, "bt_l", [NST, TW])
            ct_l = sb(pst, "ct_l", [NST, TW])

            p_btl = [None, None]
            p_bc_mm2 = [None, None]
            dtbank_rd = [None, None]
            tmp_fr = [None, None]
            for w in range(NW if STOP >= 2 else 0):
                bc_ev = [None, None]
                for ts in range(TW // TS2):
                    it2 = w * (TW // TS2) + ts
                    _par[0] = it2 % 2
                    if os.environ.get("MK_A2_BAR"):
                        bld.barrier()
                    else:
                        bld.lag_barrier("a2")
                    zz = zz_all[it2 % 2]
                    tb = w * TW + ts * TS2
                    ztk = []
                    for k in range(EC):
                        ztk.append(bld.dma(
                            "sync", zz[k][:, :],
                            z_sp.ap()[k * 128:(k + 1) * 128, tb:tb + TS2]))
                    zdone = bld.fence("sync")
                    mb = mc_ = None
                    for k in range(EC):
                        mb = MM(bank(2)[0:NST, 0:TS2],
                                xp_s[:, k * 32:k * 32 + 16], zz[k][:, :],
                                k == 0, k == EC - 1,
                                waits=([zdone, w_tok] if k == 0 else []))
                        mc_ = MM(bank(3)[0:NST, 0:TS2],
                                 xp_s[:, k * 32 + 16:k * 32 + 32], zz[k][:, :],
                                 k == 0, k == EC - 1)
                    bc_ev[0] = CP(bt_l[:, ts * TS2:(ts + 1) * TS2],
                                  bank(2)[0:NST, 0:TS2], waits=[mb, mc_])
                    bc_ev[1] = CP(ct_l[:, ts * TS2:(ts + 1) * TS2],
                                  bank(3)[0:NST, 0:TS2], waits=[mc_])
                    if debug:
                        bld.dma("sync",
                                dbg_zz.ap()[:, w * TW + ts * TS2:
                                            w * TW + (ts + 1) * TS2],
                                zz[0][:, :].bitcast(F32), waits=[bc_ev[1]])
                    for ei in range(0 if os.environ.get("MK_A2_NODT") else EC):
                        bi = ei % 2
                        mmd = None
                        for k in range(EC):
                            mmd = MM(bank(bi)[:, 0:TS2],
                                     dt_s[:, k * 2048 + ei * 128:
                                          k * 2048 + (ei + 1) * 128],
                                     zz[k][:, :], k == 0, k == EC - 1,
                                     waits=([zdone, w_tok, dtbank_rd[bi]]
                                            if k == 0 else []))
                        p1 = ACTI(pt[bi][:, :], bank(bi)[:, 0:TS2], AF.Identity,
                                  bias=dtb_s[:, ei:ei + 1],
                                  waits=[mmd, tmp_fr[bi]])
                        p2 = ACTI(ab[bi][:, :], bank(bi)[:, 0:TS2], AF.Abs,
                                  bias=dtb_s[:, ei:ei + 1], waits=[mmd, p1])
                        dtbank_rd[bi] = p2
                        p3 = ACTI(ab[bi][:, :], ab[bi][:, :], AF.Exp,
                                  scale=-1.0, waits=[p2])
                        p4 = ACTI(ab[bi][:, :], ab[bi][:, :], AF.Ln, bias=1.0,
                                  waits=[p3])
                        v1 = TSC(pt[bi][:, :], pt[bi][:, :], 0.0, ALU.max,
                                 waits=[p1, tmp_fr[bi]])
                        v2 = TT(dtv[bi][:, :], pt[bi][:, :], ab[bi][:, :],
                                ALU.add, waits=[v1, p4, tmp_fr[bi]])
                        d1 = bld.dma("sync",
                                     dt_i[w].ap()[ei * 128:(ei + 1) * 128,
                                                  ts * TS2:(ts + 1) * TS2],
                                     dtv[bi][:, :], waits=[v2])
                        v3 = TT(dzv[bi][:, :], dtv[bi][:, :],
                                zz[ei][:, :].bitcast(F32), ALU.mult,
                                waits=[v2, tmp_fr[bi], d1])
                        d2 = bld.dma("sync",
                                     dz_i[w].ap()[ei * 128:(ei + 1) * 128,
                                                  ts * TS2:(ts + 1) * TS2],
                                     dzv[bi][:, :], waits=[v3])
                        tmp_fr[bi] = d2
                p_btl[0] = bld.dma("sync", bc_i[w].ap()[0:NST, :], bt_l[:, :],
                                   waits=[bc_ev[0]])
                if debug:
                    bld.dma("sync", dbg_btl.ap()[:, w * TW:(w + 1) * TW],
                            bt_l[:, :], waits=[bc_ev[0]])
                    if w == NW - 1:
                        bld.dma("sync", dbg_xp.ap(), xp_s[:, :].bitcast(F32))
                p_btl[1] = bld.dma("sync", bc_i[w].ap()[NST:2 * NST, :],
                                   ct_l[:, :], waits=[bc_ev[1]])
                spill_done = bld.fence("sync")
                bld.cc("AllToAll", dt_i[w], dt_o[w], waits=[spill_done])
                bld.cc("AllToAll", dz_i[w], dz_o[w])
                bld.cc("AllGather", bc_i[w], bc_o[w])
        bld.barrier()

        with ExitStack() as pst:
            dtt2 = [[sb(pst, f"dtt{p}_{i}", [128, TW]) for i in range(2)]
                    for p in range(2)]
            dzt2 = [[sb(pst, f"dzt{p}_{i}", [128, TW]) for i in range(2)]
                    for p in range(2)]
            dat = [sb(pst, f"dat{i}", [128, TW]) for i in range(2)]
            dbz = [sb(pst, f"dbz{i}", [128, TW]) for i in range(2)]
            ht = [sb(pst, f"ht{i}", [128, TW]) for i in range(2)]
            ymt = [sb(pst, f"ymt{i}", [128, TW]) for i in range(2)]
            yac = [sb(pst, f"yac{i}", [128, TW]) for i in range(2)]
            btile2 = [sb(pst, f"btile{p}", [NST, TW], F32R) for p in range(2)]
            ctile2 = [sb(pst, f"ctile{p}", [NST, TW], F32R) for p in range(2)]
            carry = [sb(pst, f"carry{i}", [128, NST]) for i in range(2)]

            bb_rd = [None, None]
            cc_rd = [None, None]
            sc_tok = [None, None]
            cv_tok = [None, None]
            ya_tok = [None, None]
            yac_fr = [None, None]
            p_dt_rd = [[None, None], [None, None]]
            p_bc_rd = [None, None]
            for w in range(NW if STOP >= 3 else 0):
                cctok = ("cc", 3 * (w + 1))
                bld.barrier()
                ct_toks = [MSET(carry[0][:, :], 0.0),
                           MSET(carry[1][:, :], 0.0)]
                carry_tok = ct_toks
                for c in range(R):
                    if os.environ.get("MK_PS_BAR"):
                        bld.barrier()
                    else:
                        bld.lag_barrier("ps")
                    par = c % 2
                    dtt, dzt = dtt2[par], dzt2[par]
                    btile, ctile = btile2[par], ctile2[par]
                    dtk, dzk = [], []
                    for dc in range(2):
                        dtk.append(bld.dma(
                            "sync", dtt[dc][:, :],
                            dt_o[w].ap()[c * 256 + dc * 128:
                                         c * 256 + (dc + 1) * 128, :],
                            waits=[cctok, p_dt_rd[par][dc]]))
                        dzk.append(bld.dma(
                            "sync", dzt[dc][:, :],
                            dz_o[w].ap()[c * 256 + dc * 128:
                                         c * 256 + (dc + 1) * 128, :],
                            waits=[cctok]))
                    btk = bld.dma("sync", btile[:, :],
                                  bc_o[w].ap()[32 * c:32 * c + NST, :]
                                  .bitcast(F32R),
                                  waits=[cctok, p_bc_rd[par]])
                    ctk = bld.dma("sync", ctile[:, :],
                                  bc_o[w].ap()[32 * c + NST:32 * c + 2 * NST, :]
                                  .bitcast(F32R), waits=[cctok])
                    for n in range(NST):
                        p = n % 2
                        bbm = ccm = None
                        for hf in range(2):
                            bbm = MM(psT[2 * p][:, hf * 512:(hf + 1) * 512],
                                     sel[:, n * 128:(n + 1) * 128],
                                     btile[:, hf * 512:(hf + 1) * 512],
                                     True, True, waits=[btk, bb_rd[p]])
                            ccm = MM(psT[2 * p + 1][:, hf * 512:(hf + 1) * 512],
                                     sel[:, n * 128:(n + 1) * 128],
                                     ctile[:, hf * 512:(hf + 1) * 512],
                                     True, True, waits=[ctk, cc_rd[p]])
                        for dc in range(2):
                            i2 = (2 * n + dc) % 2
                            av = ACTI(dat[i2][:, :], dtt[dc][:, :], AF.Exp,
                                      scale=an[:, dc * 16 + n:dc * 16 + n + 1],
                                      waits=[dtk[dc], sc_tok[i2]])
                            bv = TT(dbz[i2][:, :], dzt[dc][:, :],
                                    psT[2 * p][:, :], ALU.mult,
                                    waits=[dzk[dc], bbm, sc_tok[i2]])
                            bb_rd[p] = bv
                            hv = SCAN(ht[i2][:, :], dat[i2][:, :],
                                      dbz[i2][:, :], carry[dc][:, n:n + 1],
                                      waits=[av, bv, carry_tok[dc],
                                             ya_tok[i2], cv_tok[i2]])
                            sc_tok[i2] = hv
                            p_dt_rd[par][dc] = hv
                            cv = CP(carry[dc][:, n:n + 1], ht[i2][:, TW - 1:TW],
                                    waits=[hv])
                            cv_tok[i2] = cv
                            carry_tok[dc] = cv
                            yv = TT(ymt[i2][:, :], ht[i2][:, :],
                                    psT[2 * p + 1][:, :], ALU.mult,
                                    waits=[hv, ccm, ya_tok[i2]])
                            cc_rd[p] = yv
                            p_bc_rd[par] = yv
                            if n == 0:
                                ya = CP(yac[dc][:, :], ymt[i2][:, :],
                                        waits=[yv, yac_fr[dc]], eng="gpsimd")
                            else:
                                ya = TT(yac[dc][:, :], yac[dc][:, :],
                                        ymt[i2][:, :], ALU.add,
                                        waits=[yv, yac_fr[dc]], eng="gpsimd")
                            ya_tok[i2] = ya
                            yac_fr[dc] = ya
                    for dc in range(2):
                        yac_fr[dc] = bld.dma(
                            "sync",
                            y_i[w].ap()[c * 256 + dc * 128:
                                        c * 256 + (dc + 1) * 128, :],
                            yac[dc][:, :], waits=[yac_fr[dc]])
                ysp = bld.fence("sync")
                bld.cc("AllToAll", y_i[w], y_o[w], waits=[ysp])
        bld.barrier()
        if debug:
            with ExitStack() as pst:
                tb_ = sb(pst, "dbgt", [128, TW])
                prev = None
                for w in range(NW if STOP >= 4 else 0):
                    for src_t, dst_t, nch in ((y_o[w], dbg_y[w], EC),
                                              (dt_o[w], dbg_dt[w], EC),
                                              (dz_o[w], dbg_dz[w], EC),
                                              (bc_o[w], dbg_bc[w], 2)):
                        pass
                    for src_t, dst_t, nch in ((y_o[w], dbg_y[w], EC),
                                              (dt_o[w], dbg_dt[w], EC),
                                              (dz_o[w], dbg_dz[w], EC),
                                              (bc_o[w], dbg_bc[w], 2)):
                        for e_ in range(nch):
                            t1 = bld.dma("sync", tb_[:, :],
                                         src_t.ap()[e_ * 128:(e_ + 1) * 128, :],
                                         waits=[prev])
                            prev = bld.dma(
                                "sync",
                                dst_t.ap()[e_ * 128:(e_ + 1) * 128, :],
                                tb_[:, :], waits=[t1])
                    t1 = bld.dma("sync", tb_[0:32, :], bc_i[w].ap()[:, :],
                                 waits=[prev])
                    prev = bld.dma("sync", dbg_bci[w].ap()[:, :],
                                   tb_[0:32, :], waits=[t1])
            bld.barrier()

        with ExitStack() as pst:
            ot_s = sb(pst, "ot_s", [128, 16 * 1024], F32R)
            for k in range(EC):
                bld.dma("scalar", ot_s[:, k * 1024:(k + 1) * 1024],
                        outT.ap()[k * 128:(k + 1) * 128, :])
            w_tok = bld.fence("scalar")
            yy = [sb(pst, f"yy{i}", [128, TSB]) for i in range(EC)]
            zz2 = [sb(pst, f"zz2{i}", [128, TSB], F32R) for i in range(EC)]
            yf = [sb(pst, f"yf{i}", [128, TSB], F32R) for i in range(EC)]
            uu = [sb(pst, f"uu{i}", [128, TSB]) for i in range(DC)]
            xat = [sb(pst, f"xat{i}", [128, TSB], F32R) for i in range(2)]
            for w in range(NW if STOP >= 5 else 0):
                for ts in range(TW // TSB):
                    bld.barrier()
                    tb = w * TW + ts * TSB
                    yftk = []
                    for k in range(EC):
                        a_ = bld.dma("sync", yy[k][:, :],
                                     y_o[w].ap()[k * 128:(k + 1) * 128,
                                                 ts * TSB:(ts + 1) * TSB])
                        b_ = bld.dma("sync", zz2[k][:, :],
                                     z_sp.ap()[k * 128:(k + 1) * 128,
                                               tb:tb + TSB])
                        yftk.append(STT(yf[k][:, :], zz2[k][:, :].bitcast(F32),
                                        dpv_s[:, k:k + 1], yy[k][:, :],
                                        ALU.mult, ALU.add, waits=[a_, b_]))
                    utk = []
                    for dc in range(DC):
                        utk.append(bld.dma(
                            "sync", uu[dc][:, :],
                            u0_sp.ap()[dc * 128:(dc + 1) * 128, tb:tb + TSB]))
                    bank_rd = [None, None]
                    xa_fr = [None, None]
                    for dm in range(DC):
                        bi = dm % 2
                        mm_ = None
                        for k in range(EC):
                            mm_ = MM(bank(bi)[:, 0:TSB],
                                     ot_s[:, k * 1024 + dm * 128:
                                          k * 1024 + (dm + 1) * 128],
                                     yf[k][:, :], k == 0, k == EC - 1,
                                     waits=([yftk[-1], w_tok, bank_rd[bi]]
                                            if k == 0 else []))
                        xv = TT(xat[bi][:, :], bank(bi)[:, 0:TSB],
                                uu[dm][:, :], ALU.add,
                                waits=[mm_, utk[dm], xa_fr[bi]])
                        bank_rd[bi] = xv
                        xa_fr[bi] = bld.dma(
                            "sync",
                            xa_sp.ap()[dm * 128:(dm + 1) * 128, tb:tb + TSB],
                            xat[bi][:, :], waits=[xv])
        bld.barrier()

        with ExitStack() as pst:
            gup_s = sb(pst, "gup_s", [128, 8 * 5504], BF16)
            for k in range(8):
                bld.dma("gpsimd", gup_s[:, k * 5504:(k + 1) * 5504],
                        gupT.ap()[k * 128:(k + 1) * 128, :])
            dn_s = sb(pst, "dn_s", [128, 22 * 1024], BF16)
            for kc in range(HC):
                rows_n = 128 if kc < HC - 1 else H_LAST
                bld.dma("gpsimd", dn_s[0:rows_n, kc * 1024:(kc + 1) * 1024],
                        dnT.ap()[kc * 128:kc * 128 + rows_n, :])
            w_tok = bld.fence("gpsimd")
            rows_h[0] = sb(pst, "rows1", [1, 8192], F32R)
            xat2 = [sb(pst, f"xat2_{i}", [128, TSB], F32R) for i in range(DC)]
            hnb = [sb(pst, f"hnb{i}", [128, TSB], BF16) for i in range(DC)]
            tmp2 = sb(pst, "tmp2", [128, TSB])
            hnt = tmp2
            sq2 = [sb(pst, f"sq2_{i}", [128, TSB], F32R) for i in range(2)]
            sgt2 = [sb(pst, f"sgt2_{i}", [128, TSB]) for i in range(2)]
            slab = sb(pst, "slab", [128, HC * TSB], BF16)
            xfmt = [sb(pst, f"xfmt{i}", [128, TSB], F32R) for i in range(DC)]
            x1t = xfmt
            u1b = sb(pst, "u1b", [128, TSB], BF16)

            for w in range(NW if STOP >= 6 else 0):
                for ts in range(TW // TSB):
                    bld.barrier()
                    tb = w * TW + ts * TSB
                    xtk = []
                    for dc in range(DC):
                        xtk.append(bld.dma(
                            "sync", xat2[dc][:, :],
                            xa_sp.ap()[dc * 128:(dc + 1) * 128, tb:tb + TSB]))
                    m2_ = None
                    sq_rd = [None, None]
                    for dc in range(DC):
                        sv = ACTI(sq2[dc % 2][:, :],
                                  xat2[dc][:, :].bitcast(F32),
                                  AF.Square, waits=[xtk[dc], sq_rd[dc % 2]])
                        m2_ = MM(bank(1)[0:1, 0:TSB], onc[:, :],
                                 sq2[dc % 2][:, :], dc == 0, dc == DC - 1,
                                 waits=[sv])
                        sq_rd[dc % 2] = m2_
                    v2 = CP(rs(1), bank(1)[0:1, 0:TSB], waits=[m2_])
                    a = TSC(rs(3), rs(1), 1.0 / D, ALU.mult, waits=[v2])
                    a = TSC(rs(7), rs(3), 1e-6, ALU.add, waits=[a])
                    s1 = ACTI(rs(4), rs(7).bitcast(F32), AF.Ln, waits=[a])
                    s1 = ACTI(rs(4), rs(4).bitcast(F32), AF.Exp, scale=-0.5,
                              waits=[s1])
                    bs = MM(bank(2)[:, 0:TSB], on1[:, :], rs(4), True, True,
                            waits=[s1])
                    hn_last = None
                    for dc in range(DC):
                        t1 = TT(hnt[:, :], xat2[dc][:, :].bitcast(F32),
                                bank(2)[:, 0:TSB], ALU.mult,
                                waits=[bs, hn_last])
                        hn_last = ACTI(hnb[dc][:, :], hnt[:, :], AF.Identity,
                                       scale=fnw_s[:, dc:dc + 1], waits=[t1])
                    gu_rd = [None, None]
                    sg_fr = [None, None]
                    pv = None
                    for mc in range(HC):
                        rows_n = 128 if mc < HC - 1 else H_LAST
                        mg = mu = None
                        for k in range(8):
                            mg = MM(bank(4)[0:rows_n, 0:TSB],
                                    gup_s[:, k * 5504 + mc * 128:
                                          k * 5504 + mc * 128 + rows_n],
                                    hnb[k][:, :], k == 0, k == 7,
                                    waits=([hn_last, w_tok, gu_rd[0]]
                                           if k == 0 else []))
                            mu = MM(bank(5)[0:rows_n, 0:TSB],
                                    gup_s[:, k * 5504 + H + mc * 128:
                                          k * 5504 + H + mc * 128 + rows_n],
                                    hnb[k][:, :], k == 0, k == 7,
                                    waits=([gu_rd[1]] if k == 0 else []))
                        sv = ACTI(sgt2[mc % 2][0:rows_n, :],
                                  bank(4)[0:rows_n, 0:TSB], AF.Silu,
                                  waits=[mg, sg_fr[mc % 2]])
                        pv = TT(slab[0:rows_n, mc * TSB:(mc + 1) * TSB],
                                sgt2[mc % 2][0:rows_n, :],
                                bank(5)[0:rows_n, 0:TSB], ALU.mult,
                                waits=[sv, mu])
                        gu_rd[0] = sv
                        gu_rd[1] = pv
                        sg_fr[mc % 2] = pv
                    prod_all = pv
                    dn_rd = [None, None]
                    ao_toks = []
                    for dm in range(DC):
                        bi = 6 + dm % 2
                        md = None
                        for kc in range(HC):
                            rows_n = 128 if kc < HC - 1 else H_LAST
                            md = MM(bank(bi)[:, 0:TSB],
                                    dn_s[0:rows_n, kc * 1024 + dm * 128:
                                         kc * 1024 + (dm + 1) * 128],
                                    slab[0:rows_n, kc * TSB:(kc + 1) * TSB],
                                    kc == 0, kc == HC - 1,
                                    waits=([prod_all, dn_rd[dm % 2]]
                                           if kc == 0 else []))
                        av = TT(xat2[dm][:, :], bank(bi)[:, 0:TSB],
                                xat2[dm][:, :].bitcast(F32), ALU.add,
                                waits=[md])
                        dn_rd[dm % 2] = av
                        ao_toks.append(av)
                    m1_ = m2_ = None
                    sq_rd = [None, None]
                    for dc in range(DC):
                        sv = ACTI(sq2[dc % 2][:, :],
                                  xat2[dc][:, :].bitcast(F32), AF.Square,
                                  waits=[ao_toks[dc], sq_rd[dc % 2]])
                        m1_ = MM(bank(0)[0:1, 0:TSB], onc[:, :],
                                 xat2[dc][:, :], dc == 0, dc == DC - 1,
                                 waits=[ao_toks[dc]])
                        m2_ = MM(bank(1)[0:1, 0:TSB], onc[:, :],
                                 sq2[dc % 2][:, :], dc == 0, dc == DC - 1,
                                 waits=[sv])
                        sq_rd[dc % 2] = m2_
                    v1 = CP(rs(0), bank(0)[0:1, 0:TSB], waits=[m1_, m2_])
                    v2 = CP(rs(1), bank(1)[0:1, 0:TSB], waits=[m2_])
                    a = TSC(rs(2), rs(0), 1.0 / D, ALU.mult, waits=[v1])
                    a = TT(rs(7), rs(2), rs(2), ALU.mult, waits=[a])
                    a = STT(rs(3), rs(1), 1.0 / D, rs(7), ALU.mult,
                            ALU.subtract, waits=[a, v2])
                    a = TSC(rs(7), rs(3), 1e-5, ALU.add, waits=[a])
                    s1 = ACTI(rs(4), rs(7).bitcast(F32), AF.Ln, waits=[a])
                    s1 = ACTI(rs(4), rs(4).bitcast(F32), AF.Exp, scale=-0.5,
                              waits=[s1])
                    bm = MM(bank(2)[:, 0:TSB], on1[:, :], rs(2), True, True,
                            waits=[a])
                    br = MM(bank(3)[:, 0:TSB], on1[:, :], rs(4), True, True,
                            waits=[s1])
                    xf_tk = []
                    for dc in range(DC):
                        xf_tk.append(bld.dma(
                            "sync", xfmt[dc][:, :],
                            xfm_sp.ap()[dc * 128:(dc + 1) * 128,
                                        tb:tb + TSB]))
                    x1_toks = []
                    for dc in range(DC):
                        t1 = TT(tmp2[:, :], xat2[dc][:, :].bitcast(F32),
                                bank(2)[:, 0:TSB], ALU.subtract,
                                waits=[bm, x1_toks[-1] if x1_toks else None])
                        t2 = TT(tmp2[:, :], tmp2[:, :], bank(3)[:, 0:TSB],
                                ALU.mult, waits=[t1, br])
                        t3 = TT(x1t[dc][:, :], tmp2[:, :],
                                xfmt[dc][:, :].bitcast(F32),
                                ALU.add, waits=[t2, xf_tk[dc]])
                        bld.dma("sync",
                                x1_sp.ap()[dc * 128:(dc + 1) * 128,
                                           tb:tb + TSB],
                                x1t[dc][:, :], waits=[t3])
                        x1_toks.append(t3)
                    m1_ = m2_ = None
                    sq_rd = [None, None]
                    for dc in range(DC):
                        sv = ACTI(sq2[dc % 2][:, :], x1t[dc][:, :].bitcast(F32),
                                  AF.Square, waits=[x1_toks[dc], sq_rd[dc % 2]])
                        m1_ = MM(bank(0)[0:1, 0:TSB], onc[:, :],
                                 x1t[dc][:, :], dc == 0, dc == DC - 1,
                                 waits=[x1_toks[dc]])
                        m2_ = MM(bank(1)[0:1, 0:TSB], onc[:, :],
                                 sq2[dc % 2][:, :], dc == 0, dc == DC - 1,
                                 waits=[sv])
                        sq_rd[dc % 2] = m2_
                    v1 = CP(rs(0), bank(0)[0:1, 0:TSB], waits=[m1_, m2_])
                    v2 = CP(rs(1), bank(1)[0:1, 0:TSB], waits=[m2_])
                    a = TSC(rs(2), rs(0), 1.0 / D, ALU.mult, waits=[v1])
                    a = TT(rs(7), rs(2), rs(2), ALU.mult, waits=[a])
                    a = STT(rs(3), rs(1), 1.0 / D, rs(7), ALU.mult,
                            ALU.subtract, waits=[a, v2])
                    a = TSC(rs(7), rs(3), 1e-5, ALU.add, waits=[a])
                    s1 = ACTI(rs(4), rs(7).bitcast(F32), AF.Ln, waits=[a])
                    s1 = ACTI(rs(4), rs(4).bitcast(F32), AF.Exp, scale=-0.5,
                              waits=[s1])
                    bm = MM(bank(2)[:, 0:TSB], on1[:, :], rs(2), True, True,
                            waits=[a])
                    br = MM(bank(3)[:, 0:TSB], on1[:, :], rs(4), True, True,
                            waits=[s1])
                    u1_fr = None
                    for dc in range(DC):
                        t1 = TT(tmp2[:, :], x1t[dc][:, :].bitcast(F32),
                                bank(2)[:, 0:TSB], ALU.subtract,
                                waits=[bm, u1_fr])
                        t2 = TT(u1b[:, :], tmp2[:, :], bank(3)[:, 0:TSB],
                                ALU.mult, waits=[t1, br])
                        u1_fr = bld.dma(
                            "sync",
                            u1_sp.ap()[dc * 128:(dc + 1) * 128, tb:tb + TSB],
                            u1b[:, :], waits=[t2])
        bld.barrier()

        with ExitStack() as pst:
            rows_h[0] = sb(pst, "rows2", [1, 4096], F32R)
            _par[0] = 0
            TS3 = 512
            u1r = [[sb(pst, f"u1r{w}_{c}", [128, TW], BF16) for c in range(DC)]
                   for w in range(NW)]
            ptl = [[sb(pst, f"ptl{w}_{m}", [128, TW], F32R) for m in range(DC)]
                   for w in range(NW)]
            cpjt = [sb(pst, f"cpjt{i}", [128, 1024], BF16) for i in range(EC)]
            wbig = sb(pst, "wbig", [128, 2048], BF16)
            slab3 = sb(pst, "slab3", [128, 16 * TS3], BF16)
            x1r = [sb(pst, f"x1r{i}", [128, TS3]) for i in range(DC)]
            sq3 = [sb(pst, f"sq3_{i}", [128, TS3], F32R) for i in range(2)]
            tmp3 = sb(pst, "tmp3", [128, TS3])
            ostg = sb(pst, "ostg", [128, 1024])

            u1tk = {}
            for w in range(NW if STOP >= 7 else 0):
                for cch in range(DC):
                    u1tk[(w, cch)] = bld.dma(
                        "sync", u1r[w][cch][:, :],
                        u1_sp.ap()[cch * 128:(cch + 1) * 128,
                                   w * TW:(w + 1) * TW])
            pt_wr = {}
            for g in range(8 if STOP >= 7 else 0):
                bld.barrier()
                wb_tok = bld.dma("sync", wbig[:, :],
                                 wbig_sp.ap()[:, g * 2048:(g + 1) * 2048])
                ctk = []
                for kcl in range(EC):
                    ctk.append(bld.dma(
                        "gpsimd", cpjt[kcl][:, :],
                        cpj.ap()[2048 * g + kcl * 128:
                                 2048 * g + (kcl + 1) * 128, :]))
                cfence = bld.fence("gpsimd")
                for w in range(NW):
                    for ts in range(TW // TS3):
                        zb_rd = [None, None]
                        gl_toks = []
                        for kcl in range(EC):
                            bi = kcl % 2
                            zm = MM(bank(bi)[:, 0:TS3],
                                    wbig[:, kcl * 128:(kcl + 1) * 128],
                                    u1r[w][g][:, ts * TS3:(ts + 1) * TS3],
                                    True, True,
                                    waits=[wb_tok, u1tk[(w, g)], zb_rd[bi]])
                            gv = ACTI(slab3[:, kcl * TS3:(kcl + 1) * TS3],
                                      bank(bi)[:, 0:TS3], AF.Gelu, waits=[zm])
                            zb_rd[bi] = gv
                            gl_toks.append(gv)
                        gl_all = gl_toks[-1]
                        cp_rd = [None, None]
                        for dm in range(DC):
                            bi = 2 + dm % 2
                            md = None
                            for kcl in range(EC):
                                md = MM(bank(bi)[:, 0:TS3],
                                        cpjt[kcl][:, dm * 128:(dm + 1) * 128],
                                        slab3[:, kcl * TS3:(kcl + 1) * TS3],
                                        kcl == 0, kcl == EC - 1,
                                        waits=([gl_all, cfence,
                                                cp_rd[dm % 2]]
                                               if kcl == 0 else []))
                            dst = ptl[w][dm][:, ts * TS3:(ts + 1) * TS3]
                            if g == 0:
                                pv = CP(dst, bank(bi)[:, 0:TS3], waits=[md])
                            else:
                                pv = TT(dst, dst.bitcast(F32),
                                        bank(bi)[:, 0:TS3], ALU.add,
                                        waits=[md, pt_wr[(w, dm, ts)]])
                            cp_rd[dm % 2] = pv
                            pt_wr[(w, dm, ts)] = pv
            for w in range(NW if STOP >= 7 else 0):
                for ts in range(TW // TS3):
                    bld.barrier()
                    col = slice(ts * TS3, (ts + 1) * TS3)
                    tb = w * TW + ts * TS3
                    m1_ = m2_ = None
                    sq_rd = [None, None]
                    for dc in range(DC):
                        sv = ACTI(sq3[dc % 2][:, :],
                                  ptl[w][dc][:, col].bitcast(F32), AF.Square,
                                  waits=[sq_rd[dc % 2]])
                        m1_ = MM(bank(0)[0:1, 0:TS3], onc[:, :],
                                 ptl[w][dc][:, col], dc == 0, dc == DC - 1)
                        m2_ = MM(bank(1)[0:1, 0:TS3], onc[:, :],
                                 sq3[dc % 2][:, :], dc == 0, dc == DC - 1,
                                 waits=[sv])
                        sq_rd[dc % 2] = m2_
                    v1 = CP(rs(0, TS3), bank(0)[0:1, 0:TS3], waits=[m1_, m2_])
                    v2 = CP(rs(1, TS3), bank(1)[0:1, 0:TS3], waits=[m2_])
                    a = TSC(rs(2, TS3), rs(0, TS3), 1.0 / D, ALU.mult,
                            waits=[v1])
                    a = TT(rs(7, TS3), rs(2, TS3), rs(2, TS3), ALU.mult,
                           waits=[a])
                    a = STT(rs(3, TS3), rs(1, TS3), 1.0 / D, rs(7, TS3),
                            ALU.mult, ALU.subtract, waits=[a, v2])
                    a = TSC(rs(7, TS3), rs(3, TS3), 1e-5, ALU.add, waits=[a])
                    s1 = ACTI(rs(4, TS3), rs(7, TS3).bitcast(F32), AF.Ln,
                              waits=[a])
                    s1 = ACTI(rs(4, TS3), rs(4, TS3).bitcast(F32), AF.Exp,
                              scale=-0.5, waits=[s1])
                    bm = MM(bank(2)[:, 0:TS3], on1[:, :], rs(2, TS3),
                            True, True, waits=[a])
                    br = MM(bank(3)[:, 0:TS3], on1[:, :], rs(4, TS3),
                            True, True, waits=[s1])
                    x1tk = []
                    for dc in range(DC):
                        x1tk.append(bld.dma(
                            "sync", x1r[dc][:, :],
                            x1_sp.ap()[dc * 128:(dc + 1) * 128, tb:tb + TS3]
                            .bitcast(F32)))
                    x2_toks = []
                    for dc in range(DC):
                        t1 = TT(tmp3[:, :], ptl[w][dc][:, col].bitcast(F32),
                                bank(2)[:, 0:TS3], ALU.subtract,
                                waits=[bm, x2_toks[-1] if x2_toks else None])
                        t2 = TT(tmp3[:, :], tmp3[:, :], bank(3)[:, 0:TS3],
                                ALU.mult, waits=[t1, br])
                        t3 = TT(x1r[dc][:, :], tmp3[:, :], x1r[dc][:, :],
                                ALU.add, waits=[t2, x1tk[dc]])
                        x2_toks.append(t3)
                    x2_all = x2_toks[-1]
                    ev_fr = [None, None]
                    ostg_fr = None
                    for tt in range(TS3 // 128):
                        pi = 2 + tt % 2
                        trk = None
                        for dc in range(DC):
                            trk = TRA(psT[pi][:, dc * 128:(dc + 1) * 128],
                                      x1r[dc][:, tt * 128:(tt + 1) * 128],
                                      waits=[x2_all, ev_fr[tt % 2]])
                        ev = bld.op("scalar",
                                    lambda e, pi=pi: e.copy(ostg[:, :],
                                                            psT[pi][:, :]),
                                    waits=[trk, ostg_fr])
                        ev_fr[tt % 2] = bld.dma(
                            "sync",
                            out_t.ap()[tb + tt * 128:tb + (tt + 1) * 128, :],
                            ostg[:, :], waits=[ev])
                        ostg_fr = ev_fr[tt % 2]
        bld.barrier()
        bld.flush_barrier()

        with nc.allow_low_precision(reason="bf16 operands by design"), \
                nc.Block() as blk:
            @blk.sync
            def _(e):
                for f in bld.ops["sync"]:
                    f(e)

            @blk.gpsimd
            def _(e):
                for f in bld.ops["gpsimd"]:
                    f(e)

            @blk.vector
            def _(e):
                for f in bld.ops["vector"]:
                    f(e)

            @blk.scalar
            def _(e):
                for f in bld.ops["scalar"]:
                    f(e)

            @blk.tensor
            def _(e):
                for f in bld.ops["tensor"]:
                    f(e)
    return nc


def _prep(inputs):
    f = np.ascontiguousarray
    x = inputs["L0_attn_x"]
    ww = {}
    ww["wig"] = f(np.concatenate([inputs["L0_attn_in_proj_w"],
                                  inputs["L0_attn_gate_proj_w"]], 0).T)
    ww["dtT"] = f(inputs["L0_attn_dt_w"].T)
    ww["xpT"] = f(inputs["L0_attn_x_proj_w"].T.reshape(EC, 128, 2 * NST)
                  .transpose(1, 0, 2).reshape(128, EC * 2 * NST))
    ww["outT"] = f(inputs["L0_attn_out_proj_w"].T)
    ww["gupT"] = f(np.concatenate([inputs["L0_attn_ffn_gate_w"],
                                   inputs["L0_attn_ffn_up_w"]], 0).T)
    ww["dnT"] = f(inputs["L0_attn_ffn_down_w"].T)
    ww["cpj"] = f(inputs["L0_ffn_w_cproj"].reshape(D, 4, 64, 64)
                  .transpose(2, 1, 3, 0).reshape(16384, D))
    ww["makerJ"] = f(inputs["L0_ffn_rail_maker"].transpose(3, 0, 1, 2)
                     .reshape(NST, 4096))
    ww["upJ"] = f(inputs["L0_ffn_rail_up_proj"].transpose(3, 0, 1, 2)
                  .reshape(NST, 16384))
    ww["dtb"] = f(inputs["L0_attn_dt_b"].reshape(EC, 128).T)
    ww["dpv"] = f(inputs["L0_attn_Dp"].reshape(EC, 128).T)
    ww["n1w"] = f(inputs["L0_attn_norm1_w"].reshape(DC, 128).T)
    ww["fnw"] = f(inputs["L0_attn_ffn_norm_w"].reshape(DC, 128).T)
    ww["eye"] = np.eye(128, dtype=np.float32)
    selm = np.zeros((NST, NST * 128), np.float32)
    for n in range(NST):
        selm[n, n * 128:(n + 1) * 128] = 1.0
    ww["sel16"] = selm
    ww["ones1"] = np.ones((1, 128), np.float32)
    ww["onesc"] = np.ones((128, 1), np.float32)
    in_maps = []
    al = inputs["L0_attn_A_log"]
    for c in range(R):
        m = dict(ww)
        m["x"] = f(np.concatenate([x[0, c * TW:(c + 1) * TW],
                                   x[1, c * TW:(c + 1) * TW]], 0))
        m["a_log"] = f(al[c * 256:(c + 1) * 256].reshape(2, 128, NST)
                       .transpose(1, 0, 2).reshape(128, 2 * NST))
        in_maps.append(m)
    return in_maps


def kernel(**inputs):
    inputs = {k: np.asarray(v, dtype=np.float32) for k, v in inputs.items()}
    in_maps = _prep(inputs)
    nc = build(debug=False)
    res = run_bass_kernel_spmd(nc, in_maps, core_ids=list(range(R)))
    out = np.empty((2, L, D), np.float32)
    for c in range(R):
        o = res.results[c]["out"]
        out[0, c * TW:(c + 1) * TW] = o[:TW]
        out[1, c * TW:(c + 1) * TW] = o[TW:]
    return out


# version 06
# speedup vs baseline: 1.5235x; 1.0027x over previous
import os
os.environ.setdefault("TRN_ALLOW_ICI_ALL_TO_ALL", "1")
import numpy as np
import ml_dtypes
from collections import defaultdict
from contextlib import ExitStack

import concourse.bass as bass
import concourse.mybir as mybir
from concourse.bass_utils import run_bass_kernel_spmd

F32 = mybir.dt.float32
F32R = mybir.dt.float32r
BF16 = mybir.dt.bfloat16
AF = mybir.ActivationFunctionType
ALU = mybir.AluOpType

R = 8
NW = 2
L = 8192
D = 1024
E = 2048
NST = 16
H = 2752
TW = L // R
DC, EC = 8, 16
HC, H_LAST = 22, 64
TSA = 256
TSB = 256


class Bld:
    ENGINES = ("sync", "gpsimd", "vector", "scalar", "tensor")

    def __init__(self, nc):
        self.nc = nc
        self.ops = defaultdict(list)
        self.cnt = defaultdict(int)
        self.sems = {}
        self.pending = {}
        self.rr = {}

    KPOOL = {"sync": 8, "gpsimd": 4, "scalar": 4}

    def sem_names(self):
        names = ["vector", "scalar", "tensor", "gpsimd", "cc"]
        for q, k in self.KPOOL.items():
            names += [f"{q}d{i}" for i in range(k)]
        return names

    def fence(self, q):
        return [(f"{q}d{i}", self.cnt[f"{q}d{i}"])
                for i in range(self.KPOOL[q]) if self.cnt[f"{q}d{i}"] > 0]

    def fence_all(self):
        w = []
        for q in self.KPOOL:
            w += self.fence(q)
        return w

    def _emit(self, eng, waits, inc_sem, inc_n, fn):
        w = []
        for x in waits:
            if x is None:
                continue
            if isinstance(x, list):
                w.extend(x)
            else:
                w.append(x)
        w += self.pending.pop(eng, [])
        sems = self.sems

        def cl(e, fn=fn, w=w, inc_sem=inc_sem, inc_n=inc_n):
            for s, v in w:
                e.wait_ge(sems[s], v)
            ins = fn(e)
            if inc_sem is not None:
                ins.then_inc(sems[inc_sem], inc_n)
        self.ops[eng].append(cl)

    def op(self, eng, fn, waits=()):
        self.cnt[eng] += 1
        tok = (eng, self.cnt[eng])
        self._emit(eng, waits, eng, 1, fn)
        return tok

    def dma(self, q, out, in_, waits=()):
        i = self.rr.get(q, 0)
        self.rr[q] = (i + 1) % self.KPOOL[q]
        s = f"{q}d{i}"
        prev = (s, self.cnt[s])
        self.cnt[s] += 16
        tok = (s, self.cnt[s])
        self._emit(q, list(waits) + [prev], s, 16,
                   lambda e, out=out, in_=in_: e.dma_start(out=out, in_=in_))
        return tok

    def cc(self, kind, in_, out, waits=()):
        prev = ("cc", self.cnt["cc"])
        self.cnt["cc"] += 1
        tok = ("cc", self.cnt["cc"])
        self._emit("gpsimd", list(waits) + [prev], "cc", 1,
                   lambda e, k=kind, i=in_, o=out:
                   e.collective_compute(k, ALU.bypass, [list(range(R))],
                                        ins=[i.ap().opt()], outs=[o.ap().opt()]))
        return tok

    def barrier(self):
        snap = [(s, c) for s, c in self.cnt.items() if c > 0]
        for eng in self.ENGINES:
            self.pending[eng] = list(snap)

    def lag_barrier(self, key):
        if not hasattr(self, "lag"):
            self.lag = {}
        cur = [(s, c) for s, c in self.cnt.items() if c > 0]
        prev = self.lag.get(key)
        self.lag[key] = cur
        snap = prev if prev is not None else cur
        for eng in self.ENGINES:
            self.pending[eng] = list(snap)

    def flush_barrier(self):
        sems = self.sems
        for eng in self.ENGINES:
            w = self.pending.pop(eng, [])
            if not w:
                continue

            def cl(e, w=w):
                for s, v in w:
                    e.wait_ge(sems[s], v)
            self.ops[eng].append(cl)


def build(debug=False):
    nc = bass.Bass("TRN2", target_bir_lowering=False)

    def din(name, shape, dt=F32):
        return nc.dram_tensor(name, shape, dt, kind="ExternalInput")

    def dout(name, shape, dt=F32):
        return nc.dram_tensor(name, shape, dt, kind="ExternalOutput")

    def dint(name, shape, dt=F32, dbg_ok=True):
        kind = "ExternalOutput" if (debug and dbg_ok) else "Internal"
        return nc.dram_tensor(name, shape, dt, kind=kind)

    x_in = din("x", [NW * TW, D])
    wig = din("wig", [D, 2 * E], F32R)
    dtT = din("dtT", [E, E], F32R)
    xpT = din("xpT", [128, EC * 2 * NST], F32R)
    outT = din("outT", [E, D], F32R)
    gupT = din("gupT", [D, 2 * H])
    dnT = din("dnT", [H, D])
    cpj = din("cpj", [16384, D])
    makerJ = din("makerJ", [NST, 4096], F32R)
    upJ = din("upJ", [NST, 16384], F32R)
    a_log = din("a_log", [128, 2 * NST])
    dtb = din("dtb", [128, EC])
    dpv = din("dpv", [128, EC])
    n1w = din("n1w", [128, DC])
    fnw = din("fnw", [128, DC])
    eye_c = din("eye", [128, 128])
    sel16 = din("sel16", [NST, NST * 128], F32R)
    ones1 = din("ones1", [1, 128], F32R)
    onesc = din("onesc", [128, 1], F32R)

    out_t = dout("out", [NW * TW, D])
    z_sp = dint("z_sp", [E, NW * TW], F32R)
    xfm_sp = dint("xfm_sp", [D, NW * TW], F32R)
    u0_sp = dint("u0_sp", [D, NW * TW])
    xa_sp = dint("xa_sp", [D, NW * TW], F32R)
    x1_sp = dint("x1_sp", [D, NW * TW], F32R)
    u1_sp = dint("u1_sp", [D, NW * TW], BF16, dbg_ok=False)
    wbig_sp = dint("wbig_sp", [128, 8 * 2048], BF16, dbg_ok=False)
    dt_i = [dint(f"dt_i{w}", [E, TW], dbg_ok=False) for w in range(NW)]
    dt_o = [dint(f"dt_o{w}", [E, TW], dbg_ok=False) for w in range(NW)]
    dz_i = [dint(f"dz_i{w}", [E, TW], dbg_ok=False) for w in range(NW)]
    dz_o = [dint(f"dz_o{w}", [E, TW], dbg_ok=False) for w in range(NW)]
    y_i = [dint(f"y_i{w}", [E, TW], dbg_ok=False) for w in range(NW)]
    y_o = [dint(f"y_o{w}", [E, TW], dbg_ok=False) for w in range(NW)]
    bc_i = [dint(f"bc_i{w}", [2 * NST, TW], dbg_ok=False) for w in range(NW)]
    bc_o = [dint(f"bc_o{w}", [R * 2 * NST, TW], dbg_ok=False) for w in range(NW)]
    if debug:
        dbg_y = [dout(f"dbg_y{w}", [E, TW]) for w in range(NW)]
        dbg_dt = [dout(f"dbg_dt{w}", [E, TW]) for w in range(NW)]
        dbg_dz = [dout(f"dbg_dz{w}", [E, TW]) for w in range(NW)]
        dbg_bc = [dout(f"dbg_bc{w}", [R * 2 * NST, TW]) for w in range(NW)]
        dbg_bci = [dout(f"dbg_bci{w}", [2 * NST, TW]) for w in range(NW)]
        dbg_zz = dout("dbg_zz", [128, 2 * TW])
        dbg_xp = dout("dbg_xp", [128, EC * 2 * NST])
        dbg_xp0 = dout("dbg_xp0", [128, EC * 2 * NST])
        dbg_btl = dout("dbg_btl", [NST, 2 * TW])

    bld = Bld(nc)
    stack = ExitStack()
    with stack:
        for s in bld.sem_names():
            bld.sems[s] = stack.enter_context(nc.semaphore(s))

        def sb(st, name, shape, dt=F32):
            return st.enter_context(nc.sbuf_tensor(name, shape, dt))

        psT = [stack.enter_context(nc.psum_tensor(f"ps{i}", [128, 1024], F32))
               for i in range(4)]

        def bank(i):
            return psT[i // 2][:, (i % 2) * 512:(i % 2 + 1) * 512]

        eye = sb(stack, "eye_s", [128, 128])
        sel = sb(stack, "sel_s", [NST, NST * 128], F32R)
        on1 = sb(stack, "on1_s", [1, 128], F32R)
        onc = sb(stack, "onc_s", [128, 1], F32R)
        an = sb(stack, "an_s", [128, 2 * NST])
        dtb_s = sb(stack, "dtb_s", [128, EC])
        dpv_s = sb(stack, "dpv_s", [128, EC])
        n1w_s = sb(stack, "n1w_s", [128, DC])
        fnw_s = sb(stack, "fnw_s", [128, DC])
        rows_h = [None]
        _par = [0]

        def rs(slot, width=TSA):
            base = _par[0] * 4096 + slot * 512
            return rows_h[0][0:1, base:base + width]

        def TT(out, a, b, alu, waits=(), eng="vector"):
            return bld.op(eng, lambda e, out=out, a=a, b=b, alu=alu:
                          e.tensor_tensor(out, a, b, alu), waits)

        def CP(out, in_, waits=(), eng="vector"):
            return bld.op(eng, lambda e, out=out, in_=in_:
                          e.tensor_copy(out, in_), waits)

        def TSC(out, a, s1, alu, waits=(), eng="vector"):
            return bld.op(eng, lambda e, out=out, a=a, s1=s1, alu=alu:
                          e.tensor_single_scalar(out, a, s1, alu), waits)

        def STT(out, in0, sc, in1, op0, op1, waits=(), eng="vector"):
            return bld.op(eng,
                          lambda e, out=out, in0=in0, sc=sc, in1=in1,
                          op0=op0, op1=op1:
                          e.scalar_tensor_tensor(out, in0, sc, in1, op0, op1),
                          waits)

        def ACTI(out, in_, f, bias=0.0, scale=1.0, waits=()):
            return bld.op("scalar",
                          lambda e, out=out, in_=in_, f=f, b=bias, s=scale:
                          e.activation(out, in_, f, bias=b, scale=s), waits)

        def MM(out, lhsT, rhs, start, stop, waits=()):
            return bld.op("tensor",
                          lambda e, out=out, l=lhsT, r=rhs, st=start, sp=stop:
                          e.matmul(out, l, r, start=st, stop=sp), waits)

        def TRA(out, in_, waits=()):
            return bld.op("tensor",
                          lambda e, out=out, in_=in_:
                          e.transpose(out, in_, eye[:, :]), waits)

        def SCAN(out, d0, d1, init, waits=()):
            return bld.op("vector",
                          lambda e, out=out, d0=d0, d1=d1, init=init:
                          e.tensor_tensor_scan(out, d0, d1, init,
                                               ALU.mult, ALU.add), waits)

        def MSET(t, v, waits=(), eng="vector"):
            return bld.op(eng, lambda e, t=t, v=v: e.memset(t, v), waits)

        def stats_ln(src_tiles, sq_pair, width, with_mean=True):
            m1 = m2 = None
            for dc in range(DC):
                sv = ACTI(sq_pair[dc % 2][:, 0:width],
                          src_tiles[dc][:, 0:width].bitcast(F32), AF.Square)
                if with_mean:
                    m1 = MM(bank(0)[0:1, 0:width], onc[:, :],
                            src_tiles[dc][:, 0:width], dc == 0, dc == DC - 1)
                m2 = MM(bank(1)[0:1, 0:width], onc[:, :],
                        sq_pair[dc % 2][:, 0:width], dc == 0, dc == DC - 1,
                        waits=[sv])
            toks = []
            if with_mean:
                v1 = CP(rs(0, width), bank(0)[0:1, 0:width], waits=[m1])
                toks.append(v1)
            v2 = CP(rs(1, width), bank(1)[0:1, 0:width], waits=[m2])
            toks.append(v2)
            if with_mean:
                a = TSC(rs(2, width), rs(0, width), 1.0 / D, ALU.mult,
                        waits=[v1])
                a = TT(rs(7, width), rs(2, width), rs(2, width), ALU.mult,
                       waits=[a])
                a = STT(rs(3, width), rs(1, width), 1.0 / D, rs(7, width),
                        ALU.mult, ALU.subtract, waits=[a, v2])
                a = TSC(rs(7, width), rs(3, width), 1e-5, ALU.add, waits=[a])
            else:
                a = TSC(rs(3, width), rs(1, width), 1.0 / D, ALU.mult,
                        waits=[v2])
                a = TSC(rs(7, width), rs(3, width), 1e-6, ALU.add, waits=[a])
            s1 = ACTI(rs(4, width), rs(7, width).bitcast(F32), AF.Ln,
                      waits=[a])
            s1 = ACTI(rs(4, width), rs(4, width).bitcast(F32), AF.Exp,
                      scale=-0.5, waits=[s1])
            return s1, a

        for t, src in ((eye, eye_c), (sel, sel16), (on1, ones1), (onc, onesc),
                       (an, a_log), (dtb_s, dtb), (dpv_s, dpv),
                       (n1w_s, n1w), (fnw_s, fnw)):
            bld.dma("sync", t[:, :], src.ap())
        ld = bld.fence("sync")
        t_ = ACTI(an[:, :], an[:, :], AF.Exp, waits=[ld])
        bld.op("vector", lambda e: e.tensor_scalar_mul(an[:, :], an[:, :], -1.0),
               waits=[t_])

        STOP = int(os.environ.get("MK_STOP", "99"))
        with ExitStack() as pst:
            mkj = sb(pst, "mkj", [NST, 4096], F32R)
            upj = sb(pst, "upj", [NST, 16384], F32R)
            zt = sb(pst, "zt", [128, 2048], BF16)
            stg = sb(pst, "stg", [NST, 8 * 64], BF16)
            bld.dma("sync", mkj[:, :], makerJ.ap())
            bld.dma("sync", upj[:, :], upJ.ap())
            wtok = bld.fence("sync")
            zv = MSET(zt[:, :], 0.0)
            for g in range(8):
                bld.dma("sync", wbig_sp.ap()[:, g * 2048:(g + 1) * 2048],
                        zt[:, :], waits=[zv])
            zdone = bld.fence("sync")
            ev_tok = [None] * 8
            dm_tok = [None] * 8
            for h in range(4):
                for r in range(64):
                    hr = h * 64 + r
                    slot = hr % 8
                    pb = bank(slot)[0:NST, 0:64]
                    tt = MM(pb, mkj[:, hr * 16:hr * 16 + 16],
                            upj[:, hr * 64:hr * 64 + 64], True, True,
                            waits=[wtok, ev_tok[slot]])
                    ss_ = stg[0:NST, slot * 64:(slot + 1) * 64]
                    vv = CP(ss_, pb, waits=[tt, dm_tok[slot]])
                    g = r // 8
                    kcl = 2 * (r % 8) + h // 2
                    col = g * 2048 + kcl * 128 + (h % 2) * 64
                    dm_tok[slot] = bld.dma(
                        "sync",
                        wbig_sp.ap()[16 * (r % 8):16 * (r % 8) + 16,
                                     col:col + 64],
                        ss_, waits=[vv, zdone])
                    ev_tok[slot] = vv
        bld.barrier()

        with ExitStack() as pst:
            wig_s = sb(pst, "wig_s", [128, 8 * 4096], F32R)
            for k in range(8):
                bld.dma("scalar", wig_s[:, k * 4096:(k + 1) * 4096],
                        wig.ap()[k * 128:(k + 1) * 128, :])
            wig_tok = bld.fence("scalar")
            rows_h[0] = sb(pst, "rows0", [1, 8192], F32R)
            stage = [sb(pst, f"xstg{i}", [128, D]) for i in range(2)]
            xfm = [sb(pst, f"xfm{i}", [128, TSA], F32R) for i in range(DC)]
            sqp = [sb(pst, f"sq{i}", [128, TSA], F32R) for i in range(2)]
            u0t = [sb(pst, f"u0t{i}", [128, TSA]) for i in range(2)]
            h1t = sb(pst, "h1t", [128, TSA])
            h1 = [sb(pst, f"h1_{i}", [128, TSA], F32R) for i in range(DC)]
            sgt = [sb(pst, f"sgt{i}", [128, TSA]) for i in range(2)]
            zot = [sb(pst, f"zot{i}", [128, TSA], F32R) for i in range(2)]

            p_tra = [None, None]
            p_apply = {2: None, 3: None, 4: None}
            p_zg67 = [None, None]
            p_zg01 = [None, None]
            p_stage = [None, None]
            p_sq = [None, None]
            p_h1rd = None
            p_xfm_a1 = [None] * DC
            p_xfm_dma = [None] * DC
            u0_fr = [None, None]
            zo_fr = [None, None]
            h1_last = None
            for w in range(NW if STOP >= 1 else 0):
                for ts in range(TW // TSA):
                    it = w * (TW // TSA) + ts
                    par = it % 2
                    _par[0] = par
                    if os.environ.get("MK_A1_BAR"):
                        bld.barrier()
                    else:
                        bld.lag_barrier("a1")
                    tb = w * TW + ts * TSA
                    ntt = TSA // 128
                    ltok = []
                    for tt in range(ntt):
                        ltok.append(bld.dma(
                            "sync", stage[tt][:, :],
                            x_in.ap()[tb + tt * 128:tb + (tt + 1) * 128, :],
                            waits=[p_stage[tt]]))
                    ev8 = []
                    tra_ev = list(p_tra)
                    for dc in range(DC):
                        bi = 4 + (dc % 2)
                        ttk = None
                        for tt in range(ntt):
                            tw = [ltok[tt], tra_ev[dc % 2]]
                            if dc % 2 == 0:
                                tw.append(p_apply[4])
                            ttk = TRA(bank(bi)[:, tt * 128:(tt + 1) * 128],
                                      stage[tt][:, dc * 128:(dc + 1) * 128],
                                      waits=tw)
                            p_stage[tt] = ttk
                        ev = bld.op("scalar",
                                    lambda e, xf=xfm[dc], bi=bi:
                                    e.copy(xf[:, :], bank(bi)[:, 0:TSA]),
                                    waits=[ttk, p_xfm_a1[dc], p_xfm_dma[dc]])
                        tra_ev[dc % 2] = ev
                        ev8.append(ev)
                        p_xfm_dma[dc] = bld.dma(
                            "sync",
                            xfm_sp.ap()[dc * 128:(dc + 1) * 128,
                                        tb:tb + TSA],
                            xfm[dc][:, :], waits=[ev])
                    p_tra = tra_ev
                    m1 = m2 = None
                    sq_rd = p_sq
                    for dc in range(DC):
                        sq_ = sqp[dc % 2]
                        sv = ACTI(sq_[:, :], xfm[dc][:, :].bitcast(F32),
                                  AF.Square, waits=[ev8[dc], sq_rd[dc % 2]])
                        m1 = MM(bank(0)[0:1, 0:TSA], onc[:, :], xfm[dc][:, :],
                                dc == 0, dc == DC - 1,
                                waits=[ev8[dc], p_zg01[0]] if dc == 0
                                else [ev8[dc]])
                        m2 = MM(bank(1)[0:1, 0:TSA], onc[:, :],
                                sq_[:, :], dc == 0, dc == DC - 1,
                                waits=[sv, p_zg01[1]] if dc == 0 else [sv])
                        sq_rd[dc % 2] = m2
                    p_sq = sq_rd
                    v1 = CP(rs(0), bank(0)[0:1, 0:TSA], waits=[m1, m2])
                    v2 = CP(rs(1), bank(1)[0:1, 0:TSA], waits=[m2])
                    a = TSC(rs(2), rs(0), 1.0 / D, ALU.mult, waits=[v1])
                    a = TT(rs(7), rs(2), rs(2), ALU.mult, waits=[a])
                    a = STT(rs(3), rs(1), 1.0 / D, rs(7), ALU.mult,
                            ALU.subtract, waits=[a, v2])
                    a = TSC(rs(7), rs(3), 1e-5, ALU.add, waits=[a])
                    s1 = ACTI(rs(4), rs(7).bitcast(F32), AF.Ln, waits=[a])
                    s1 = ACTI(rs(4), rs(4).bitcast(F32), AF.Exp, scale=-0.5,
                              waits=[s1])
                    mr = TT(rs(6), rs(2), rs(4), ALU.mult, waits=[s1])
                    rc = bld.op("vector",
                                lambda e, o=rs(0), i_=rs(7): e.reciprocal(o, i_),
                                waits=[s1])
                    q = TT(rs(1), rs(3), rs(0), ALU.mult, waits=[rc])
                    q = TSC(rs(7), rs(1), 1e-6, ALU.add, waits=[q])
                    s2 = ACTI(rs(5), rs(7).bitcast(F32), AF.Ln, waits=[q])
                    s2 = ACTI(rs(5), rs(5).bitcast(F32), AF.Exp, scale=-0.5,
                              waits=[s2])
                    b_r1 = MM(bank(2)[:, 0:TSA], on1[:, :], rs(4), True, True,
                              waits=[s1, p_apply[2]])
                    b_mr = MM(bank(3)[:, 0:TSA], on1[:, :], rs(6), True, True,
                              waits=[mr, p_apply[3]])
                    b_s2 = MM(bank(4)[:, 0:TSA], on1[:, :], rs(5), True, True,
                              waits=[s2, tra_ev[0]])
                    a1 = a2 = a3 = None
                    for dc in range(DC):
                        ui = dc % 2
                        a1 = TT(u0t[ui][:, :], xfm[dc][:, :].bitcast(F32),
                                bank(2)[:, 0:TSA], ALU.mult,
                                waits=[b_r1, u0_fr[dc % 2]])
                        p_xfm_a1[dc] = a1
                        a2 = TT(u0t[ui][:, :], u0t[ui][:, :], bank(3)[:, 0:TSA],
                                ALU.subtract, waits=[a1, b_mr])
                        u0_fr[dc % 2] = bld.dma(
                            "sync",
                            u0_sp.ap()[dc * 128:(dc + 1) * 128, tb:tb + TSA],
                            u0t[ui][:, :], waits=[a2])
                        a3 = TT(h1t[:, :], u0t[ui][:, :], bank(4)[:, 0:TSA],
                                ALU.mult, waits=[a2, b_s2, h1_last])
                        h1_last = ACTI(h1[dc][:, :], h1t[:, :], AF.Identity,
                                       scale=n1w_s[:, dc:dc + 1],
                                       waits=[a3, p_h1rd if dc == 0 else None])
                    p_apply = {2: a1, 3: a2, 4: a3}
                    zg_rd = {0: v1, 1: v2, 6: p_zg67[0], 7: p_zg67[1]}
                    for ei in range(EC):
                        zb, gb = (6, 7) if ei % 2 == 0 else (0, 1)
                        mmz = mmg = None
                        for k in range(8):
                            mmz = MM(bank(zb)[:, 0:TSA],
                                     wig_s[:, k * 4096 + ei * 128:
                                           k * 4096 + (ei + 1) * 128],
                                     h1[k][:, :], k == 0, k == 7,
                                     waits=([h1_last, zg_rd[zb], wig_tok]
                                            if k == 0 else []))
                            mmg = MM(bank(gb)[:, 0:TSA],
                                     wig_s[:, k * 4096 + 2048 + ei * 128:
                                           k * 4096 + 2048 + (ei + 1) * 128],
                                     h1[k][:, :], k == 0, k == 7,
                                     waits=([zg_rd[gb]] if k == 0 else []))
                        si = ei % 2
                        sgv = ACTI(sgt[si][:, :], bank(gb)[:, 0:TSA],
                                   AF.Sigmoid, waits=[mmg, zo_fr[ei % 2]])
                        zv = TT(zot[si][:, :], bank(zb)[:, 0:TSA],
                                sgt[si][:, :], ALU.mult, waits=[mmz, sgv])
                        zo_fr[ei % 2] = bld.dma(
                            "sync",
                            z_sp.ap()[ei * 128:(ei + 1) * 128, tb:tb + TSA],
                            zot[si][:, :], waits=[zv])
                        zg_rd[zb] = zv
                        zg_rd[gb] = sgv
                    p_zg67 = [zg_rd[6], zg_rd[7]]
                    p_zg01 = [zg_rd[0], zg_rd[1]]
                    p_h1rd = ("tensor", bld.cnt["tensor"])
        bld.barrier()

        with ExitStack() as pst:
            if not os.environ.get("MK_A2_NODT"):
                dt_s = sb(pst, "dt_s", [128, 16 * 2048], F32R)
                for k in range(EC):
                    bld.dma("scalar", dt_s[:, k * 2048:(k + 1) * 2048],
                            dtT.ap()[k * 128:(k + 1) * 128, :])
            xp_s = sb(pst, "xp_s", [128, 16 * 32], F32R)
            xp_ld = bld.dma("scalar", xp_s[:, :], xpT.ap())
            if debug:
                bld.dma("sync", dbg_xp0.ap(), xp_s[:, :].bitcast(F32),
                        waits=[xp_ld])
            w_tok = bld.fence("scalar")
            TS2 = 256
            NZB = 2
            zz_all = [[sb(pst, f"zz{j}_{i}", [128, TS2], F32R)
                       for i in range(EC)] for j in range(NZB)]
            pt = [sb(pst, f"pt{i}", [128, TS2]) for i in range(2)]
            ab = [sb(pst, f"ab{i}", [128, TS2]) for i in range(2)]
            dtv = [sb(pst, f"dtv{i}", [128, TS2]) for i in range(2)]
            dzv = ab
            bt_l = sb(pst, "bt_l", [NST, TW])
            ct_l = sb(pst, "ct_l", [NST, TW])

            p_btl = [None, None]
            p_bc_mm2 = [None, None]
            dtbank_rd = [None, None]
            tmp_fr = [None, None]
            for w in range(NW if STOP >= 2 else 0):
                bc_ev = [None, None]
                for ts in range(TW // TS2):
                    it2 = w * (TW // TS2) + ts
                    _par[0] = it2 % 2
                    if os.environ.get("MK_A2_BAR"):
                        bld.barrier()
                    else:
                        bld.lag_barrier("a2")
                    zz = zz_all[it2 % 2]
                    tb = w * TW + ts * TS2
                    ztk = []
                    for k in range(EC):
                        ztk.append(bld.dma(
                            "sync", zz[k][:, :],
                            z_sp.ap()[k * 128:(k + 1) * 128, tb:tb + TS2]))
                    zdone = bld.fence("sync")
                    mb = mc_ = None
                    for k in range(EC):
                        mb = MM(bank(2)[0:NST, 0:TS2],
                                xp_s[:, k * 32:k * 32 + 16], zz[k][:, :],
                                k == 0, k == EC - 1,
                                waits=([zdone, w_tok] if k == 0 else []))
                        mc_ = MM(bank(3)[0:NST, 0:TS2],
                                 xp_s[:, k * 32 + 16:k * 32 + 32], zz[k][:, :],
                                 k == 0, k == EC - 1)
                    bc_ev[0] = CP(bt_l[:, ts * TS2:(ts + 1) * TS2],
                                  bank(2)[0:NST, 0:TS2], waits=[mb, mc_])
                    bc_ev[1] = CP(ct_l[:, ts * TS2:(ts + 1) * TS2],
                                  bank(3)[0:NST, 0:TS2], waits=[mc_])
                    if debug:
                        bld.dma("sync",
                                dbg_zz.ap()[:, w * TW + ts * TS2:
                                            w * TW + (ts + 1) * TS2],
                                zz[0][:, :].bitcast(F32), waits=[bc_ev[1]])
                    for ei in range(0 if os.environ.get("MK_A2_NODT") else EC):
                        bi = ei % 2
                        mmd = None
                        for k in range(EC):
                            mmd = MM(bank(bi)[:, 0:TS2],
                                     dt_s[:, k * 2048 + ei * 128:
                                          k * 2048 + (ei + 1) * 128],
                                     zz[k][:, :], k == 0, k == EC - 1,
                                     waits=([zdone, w_tok, dtbank_rd[bi]]
                                            if k == 0 else []))
                        p1 = ACTI(pt[bi][:, :], bank(bi)[:, 0:TS2], AF.Identity,
                                  bias=dtb_s[:, ei:ei + 1],
                                  waits=[mmd, tmp_fr[bi]])
                        p2 = ACTI(ab[bi][:, :], bank(bi)[:, 0:TS2], AF.Abs,
                                  bias=dtb_s[:, ei:ei + 1], waits=[mmd, p1])
                        dtbank_rd[bi] = p2
                        p3 = ACTI(ab[bi][:, :], ab[bi][:, :], AF.Exp,
                                  scale=-1.0, waits=[p2])
                        p4 = ACTI(ab[bi][:, :], ab[bi][:, :], AF.Ln, bias=1.0,
                                  waits=[p3])
                        v1 = TSC(pt[bi][:, :], pt[bi][:, :], 0.0, ALU.max,
                                 waits=[p1, tmp_fr[bi]])
                        v2 = TT(dtv[bi][:, :], pt[bi][:, :], ab[bi][:, :],
                                ALU.add, waits=[v1, p4, tmp_fr[bi]])
                        d1 = bld.dma("sync",
                                     dt_i[w].ap()[ei * 128:(ei + 1) * 128,
                                                  ts * TS2:(ts + 1) * TS2],
                                     dtv[bi][:, :], waits=[v2])
                        v3 = TT(dzv[bi][:, :], dtv[bi][:, :],
                                zz[ei][:, :].bitcast(F32), ALU.mult,
                                waits=[v2, tmp_fr[bi], d1])
                        d2 = bld.dma("sync",
                                     dz_i[w].ap()[ei * 128:(ei + 1) * 128,
                                                  ts * TS2:(ts + 1) * TS2],
                                     dzv[bi][:, :], waits=[v3])
                        tmp_fr[bi] = d2
                p_btl[0] = bld.dma("sync", bc_i[w].ap()[0:NST, :], bt_l[:, :],
                                   waits=[bc_ev[0]])
                if debug:
                    bld.dma("sync", dbg_btl.ap()[:, w * TW:(w + 1) * TW],
                            bt_l[:, :], waits=[bc_ev[0]])
                    if w == NW - 1:
                        bld.dma("sync", dbg_xp.ap(), xp_s[:, :].bitcast(F32))
                p_btl[1] = bld.dma("sync", bc_i[w].ap()[NST:2 * NST, :],
                                   ct_l[:, :], waits=[bc_ev[1]])
                spill_done = bld.fence("sync")
                bld.cc("AllToAll", dt_i[w], dt_o[w], waits=[spill_done])
                bld.cc("AllToAll", dz_i[w], dz_o[w])
                bld.cc("AllGather", bc_i[w], bc_o[w])
        bld.barrier()

        with ExitStack() as pst:
            dtt2 = [[sb(pst, f"dtt{p}_{i}", [128, TW]) for i in range(2)]
                    for p in range(2)]
            dzt2 = [[sb(pst, f"dzt{p}_{i}", [128, TW]) for i in range(2)]
                    for p in range(2)]
            NR = 4
            dat = [sb(pst, f"dat{i}", [128, TW]) for i in range(NR)]
            dbz = [sb(pst, f"dbz{i}", [128, TW]) for i in range(NR)]
            ht = [sb(pst, f"ht{i}", [128, TW]) for i in range(NR)]
            ymt = [sb(pst, f"ymt{i}", [128, TW]) for i in range(NR)]
            yac = [sb(pst, f"yac{i}", [128, TW]) for i in range(2)]
            btile2 = [sb(pst, f"btile{p}", [NST, TW], F32R) for p in range(2)]
            ctile2 = [sb(pst, f"ctile{p}", [NST, TW], F32R) for p in range(2)]
            carry = [sb(pst, f"carry{i}", [128, NST]) for i in range(2)]

            bb_rd = [None, None]
            cc_rd = [None, None]
            sc_tok = [None] * 4
            cv_tok = [None] * 4
            ya_tok = [None] * 4
            yac_fr = [None, None]
            p_dt_rd = [[None, None], [None, None]]
            p_bc_rd = [None, None]
            for w in range(NW if STOP >= 3 else 0):
                cctok = ("cc", 3 * (w + 1))
                bld.barrier()
                ct_toks = [MSET(carry[0][:, :], 0.0),
                           MSET(carry[1][:, :], 0.0)]
                carry_tok = {(dc, n): ct_toks[dc]
                             for dc in range(2) for n in range(NST)}
                for c in range(R):
                    if os.environ.get("MK_PS_BAR"):
                        bld.barrier()
                    else:
                        bld.lag_barrier("ps")
                    par = c % 2
                    dtt, dzt = dtt2[par], dzt2[par]
                    btile, ctile = btile2[par], ctile2[par]
                    dtk, dzk = [], []
                    for dc in range(2):
                        dtk.append(bld.dma(
                            "sync", dtt[dc][:, :],
                            dt_o[w].ap()[c * 256 + dc * 128:
                                         c * 256 + (dc + 1) * 128, :],
                            waits=[cctok, p_dt_rd[par][dc]]))
                        dzk.append(bld.dma(
                            "sync", dzt[dc][:, :],
                            dz_o[w].ap()[c * 256 + dc * 128:
                                         c * 256 + (dc + 1) * 128, :],
                            waits=[cctok]))
                    btk = bld.dma("sync", btile[:, :],
                                  bc_o[w].ap()[32 * c:32 * c + NST, :]
                                  .bitcast(F32R),
                                  waits=[cctok, p_bc_rd[par]])
                    ctk = bld.dma("sync", ctile[:, :],
                                  bc_o[w].ap()[32 * c + NST:32 * c + 2 * NST, :]
                                  .bitcast(F32R), waits=[cctok])
                    for n in range(NST):
                        p = n % 2
                        bbm = ccm = None
                        for hf in range(2):
                            bbm = MM(psT[2 * p][:, hf * 512:(hf + 1) * 512],
                                     sel[:, n * 128:(n + 1) * 128],
                                     btile[:, hf * 512:(hf + 1) * 512],
                                     True, True, waits=[btk, bb_rd[p]])
                            ccm = MM(psT[2 * p + 1][:, hf * 512:(hf + 1) * 512],
                                     sel[:, n * 128:(n + 1) * 128],
                                     ctile[:, hf * 512:(hf + 1) * 512],
                                     True, True, waits=[ctk, cc_rd[p]])
                        for dc in range(2):
                            i2 = (2 * n + dc) % 4
                            av = ACTI(dat[i2][:, :], dtt[dc][:, :], AF.Exp,
                                      scale=an[:, dc * 16 + n:dc * 16 + n + 1],
                                      waits=[dtk[dc], sc_tok[i2]])
                            bv = TT(dbz[i2][:, :], dzt[dc][:, :],
                                    psT[2 * p][:, :], ALU.mult,
                                    waits=[dzk[dc], bbm, sc_tok[i2]])
                            bb_rd[p] = bv
                            hv = SCAN(ht[i2][:, :], dat[i2][:, :],
                                      dbz[i2][:, :], carry[dc][:, n:n + 1],
                                      waits=[av, bv, carry_tok[(dc, n)],
                                             ya_tok[i2], cv_tok[i2]])
                            sc_tok[i2] = hv
                            p_dt_rd[par][dc] = hv
                            cv = CP(carry[dc][:, n:n + 1], ht[i2][:, TW - 1:TW],
                                    waits=[hv])
                            cv_tok[i2] = cv
                            carry_tok[(dc, n)] = cv
                            yv = TT(ymt[i2][:, :], ht[i2][:, :],
                                    psT[2 * p + 1][:, :], ALU.mult,
                                    waits=[hv, ccm, ya_tok[i2]])
                            cc_rd[p] = yv
                            p_bc_rd[par] = yv
                            if n == 0:
                                ya = CP(yac[dc][:, :], ymt[i2][:, :],
                                        waits=[yv, yac_fr[dc]], eng="gpsimd")
                            else:
                                ya = TT(yac[dc][:, :], yac[dc][:, :],
                                        ymt[i2][:, :], ALU.add,
                                        waits=[yv, yac_fr[dc]], eng="gpsimd")
                            ya_tok[i2] = ya
                            yac_fr[dc] = ya
                    for dc in range(2):
                        yac_fr[dc] = bld.dma(
                            "sync",
                            y_i[w].ap()[c * 256 + dc * 128:
                                        c * 256 + (dc + 1) * 128, :],
                            yac[dc][:, :], waits=[yac_fr[dc]])
                ysp = bld.fence("sync")
                bld.cc("AllToAll", y_i[w], y_o[w], waits=[ysp])
        bld.barrier()
        if debug:
            with ExitStack() as pst:
                tb_ = sb(pst, "dbgt", [128, TW])
                prev = None
                for w in range(NW if STOP >= 4 else 0):
                    for src_t, dst_t, nch in ((y_o[w], dbg_y[w], EC),
                                              (dt_o[w], dbg_dt[w], EC),
                                              (dz_o[w], dbg_dz[w], EC),
                                              (bc_o[w], dbg_bc[w], 2)):
                        pass
                    for src_t, dst_t, nch in ((y_o[w], dbg_y[w], EC),
                                              (dt_o[w], dbg_dt[w], EC),
                                              (dz_o[w], dbg_dz[w], EC),
                                              (bc_o[w], dbg_bc[w], 2)):
                        for e_ in range(nch):
                            t1 = bld.dma("sync", tb_[:, :],
                                         src_t.ap()[e_ * 128:(e_ + 1) * 128, :],
                                         waits=[prev])
                            prev = bld.dma(
                                "sync",
                                dst_t.ap()[e_ * 128:(e_ + 1) * 128, :],
                                tb_[:, :], waits=[t1])
                    t1 = bld.dma("sync", tb_[0:32, :], bc_i[w].ap()[:, :],
                                 waits=[prev])
                    prev = bld.dma("sync", dbg_bci[w].ap()[:, :],
                                   tb_[0:32, :], waits=[t1])
            bld.barrier()

        with ExitStack() as pst:
            ot_s = sb(pst, "ot_s", [128, 16 * 1024], F32R)
            for k in range(EC):
                bld.dma("scalar", ot_s[:, k * 1024:(k + 1) * 1024],
                        outT.ap()[k * 128:(k + 1) * 128, :])
            w_tok = bld.fence("scalar")
            yy = [sb(pst, f"yy{i}", [128, TSB]) for i in range(EC)]
            zz2 = [sb(pst, f"zz2{i}", [128, TSB], F32R) for i in range(EC)]
            yf = [sb(pst, f"yf{i}", [128, TSB], F32R) for i in range(EC)]
            uu = [sb(pst, f"uu{i}", [128, TSB]) for i in range(DC)]
            xat = [sb(pst, f"xat{i}", [128, TSB], F32R) for i in range(2)]
            for w in range(NW if STOP >= 5 else 0):
                for ts in range(TW // TSB):
                    bld.barrier()
                    tb = w * TW + ts * TSB
                    yftk = []
                    for k in range(EC):
                        a_ = bld.dma("sync", yy[k][:, :],
                                     y_o[w].ap()[k * 128:(k + 1) * 128,
                                                 ts * TSB:(ts + 1) * TSB])
                        b_ = bld.dma("sync", zz2[k][:, :],
                                     z_sp.ap()[k * 128:(k + 1) * 128,
                                               tb:tb + TSB])
                        yftk.append(STT(yf[k][:, :], zz2[k][:, :].bitcast(F32),
                                        dpv_s[:, k:k + 1], yy[k][:, :],
                                        ALU.mult, ALU.add, waits=[a_, b_]))
                    utk = []
                    for dc in range(DC):
                        utk.append(bld.dma(
                            "sync", uu[dc][:, :],
                            u0_sp.ap()[dc * 128:(dc + 1) * 128, tb:tb + TSB]))
                    bank_rd = [None, None]
                    xa_fr = [None, None]
                    for dm in range(DC):
                        bi = dm % 2
                        mm_ = None
                        for k in range(EC):
                            mm_ = MM(bank(bi)[:, 0:TSB],
                                     ot_s[:, k * 1024 + dm * 128:
                                          k * 1024 + (dm + 1) * 128],
                                     yf[k][:, :], k == 0, k == EC - 1,
                                     waits=([yftk[-1], w_tok, bank_rd[bi]]
                                            if k == 0 else []))
                        xv = TT(xat[bi][:, :], bank(bi)[:, 0:TSB],
                                uu[dm][:, :], ALU.add,
                                waits=[mm_, utk[dm], xa_fr[bi]])
                        bank_rd[bi] = xv
                        xa_fr[bi] = bld.dma(
                            "sync",
                            xa_sp.ap()[dm * 128:(dm + 1) * 128, tb:tb + TSB],
                            xat[bi][:, :], waits=[xv])
        bld.barrier()

        with ExitStack() as pst:
            gup_s = sb(pst, "gup_s", [128, 8 * 5504], BF16)
            for k in range(8):
                bld.dma("gpsimd", gup_s[:, k * 5504:(k + 1) * 5504],
                        gupT.ap()[k * 128:(k + 1) * 128, :])
            dn_s = sb(pst, "dn_s", [128, 22 * 1024], BF16)
            for kc in range(HC):
                rows_n = 128 if kc < HC - 1 else H_LAST
                bld.dma("gpsimd", dn_s[0:rows_n, kc * 1024:(kc + 1) * 1024],
                        dnT.ap()[kc * 128:kc * 128 + rows_n, :])
            w_tok = bld.fence("gpsimd")
            rows_h[0] = sb(pst, "rows1", [1, 8192], F32R)
            xat2 = [sb(pst, f"xat2_{i}", [128, TSB], F32R) for i in range(DC)]
            hnb = [sb(pst, f"hnb{i}", [128, TSB], BF16) for i in range(DC)]
            tmp2 = sb(pst, "tmp2", [128, TSB])
            hnt = tmp2
            sq2 = [sb(pst, f"sq2_{i}", [128, TSB], F32R) for i in range(2)]
            sgt2 = [sb(pst, f"sgt2_{i}", [128, TSB]) for i in range(2)]
            slab = sb(pst, "slab", [128, HC * TSB], BF16)
            xfmt = [sb(pst, f"xfmt{i}", [128, TSB], F32R) for i in range(DC)]
            x1t = xfmt
            u1b = sb(pst, "u1b", [128, TSB], BF16)

            for w in range(NW if STOP >= 6 else 0):
                for ts in range(TW // TSB):
                    bld.barrier()
                    tb = w * TW + ts * TSB
                    xtk = []
                    for dc in range(DC):
                        xtk.append(bld.dma(
                            "sync", xat2[dc][:, :],
                            xa_sp.ap()[dc * 128:(dc + 1) * 128, tb:tb + TSB]))
                    m2_ = None
                    sq_rd = [None, None]
                    for dc in range(DC):
                        sv = ACTI(sq2[dc % 2][:, :],
                                  xat2[dc][:, :].bitcast(F32),
                                  AF.Square, waits=[xtk[dc], sq_rd[dc % 2]])
                        m2_ = MM(bank(1)[0:1, 0:TSB], onc[:, :],
                                 sq2[dc % 2][:, :], dc == 0, dc == DC - 1,
                                 waits=[sv])
                        sq_rd[dc % 2] = m2_
                    v2 = CP(rs(1), bank(1)[0:1, 0:TSB], waits=[m2_])
                    a = TSC(rs(3), rs(1), 1.0 / D, ALU.mult, waits=[v2])
                    a = TSC(rs(7), rs(3), 1e-6, ALU.add, waits=[a])
                    s1 = ACTI(rs(4), rs(7).bitcast(F32), AF.Ln, waits=[a])
                    s1 = ACTI(rs(4), rs(4).bitcast(F32), AF.Exp, scale=-0.5,
                              waits=[s1])
                    bs = MM(bank(2)[:, 0:TSB], on1[:, :], rs(4), True, True,
                            waits=[s1])
                    hn_last = None
                    for dc in range(DC):
                        t1 = TT(hnt[:, :], xat2[dc][:, :].bitcast(F32),
                                bank(2)[:, 0:TSB], ALU.mult,
                                waits=[bs, hn_last])
                        hn_last = ACTI(hnb[dc][:, :], hnt[:, :], AF.Identity,
                                       scale=fnw_s[:, dc:dc + 1], waits=[t1])
                    gu_rd = [None, None]
                    sg_fr = [None, None]
                    pv = None
                    for mc in range(HC):
                        rows_n = 128 if mc < HC - 1 else H_LAST
                        mg = mu = None
                        for k in range(8):
                            mg = MM(bank(4)[0:rows_n, 0:TSB],
                                    gup_s[:, k * 5504 + mc * 128:
                                          k * 5504 + mc * 128 + rows_n],
                                    hnb[k][:, :], k == 0, k == 7,
                                    waits=([hn_last, w_tok, gu_rd[0]]
                                           if k == 0 else []))
                            mu = MM(bank(5)[0:rows_n, 0:TSB],
                                    gup_s[:, k * 5504 + H + mc * 128:
                                          k * 5504 + H + mc * 128 + rows_n],
                                    hnb[k][:, :], k == 0, k == 7,
                                    waits=([gu_rd[1]] if k == 0 else []))
                        sv = ACTI(sgt2[mc % 2][0:rows_n, :],
                                  bank(4)[0:rows_n, 0:TSB], AF.Silu,
                                  waits=[mg, sg_fr[mc % 2]])
                        pv = TT(slab[0:rows_n, mc * TSB:(mc + 1) * TSB],
                                sgt2[mc % 2][0:rows_n, :],
                                bank(5)[0:rows_n, 0:TSB], ALU.mult,
                                waits=[sv, mu])
                        gu_rd[0] = sv
                        gu_rd[1] = pv
                        sg_fr[mc % 2] = pv
                    prod_all = pv
                    dn_rd = [None, None]
                    ao_toks = []
                    for dm in range(DC):
                        bi = 6 + dm % 2
                        md = None
                        for kc in range(HC):
                            rows_n = 128 if kc < HC - 1 else H_LAST
                            md = MM(bank(bi)[:, 0:TSB],
                                    dn_s[0:rows_n, kc * 1024 + dm * 128:
                                         kc * 1024 + (dm + 1) * 128],
                                    slab[0:rows_n, kc * TSB:(kc + 1) * TSB],
                                    kc == 0, kc == HC - 1,
                                    waits=([prod_all, dn_rd[dm % 2]]
                                           if kc == 0 else []))
                        av = TT(xat2[dm][:, :], bank(bi)[:, 0:TSB],
                                xat2[dm][:, :].bitcast(F32), ALU.add,
                                waits=[md])
                        dn_rd[dm % 2] = av
                        ao_toks.append(av)
                    m1_ = m2_ = None
                    sq_rd = [None, None]
                    for dc in range(DC):
                        sv = ACTI(sq2[dc % 2][:, :],
                                  xat2[dc][:, :].bitcast(F32), AF.Square,
                                  waits=[ao_toks[dc], sq_rd[dc % 2]])
                        m1_ = MM(bank(0)[0:1, 0:TSB], onc[:, :],
                                 xat2[dc][:, :], dc == 0, dc == DC - 1,
                                 waits=[ao_toks[dc]])
                        m2_ = MM(bank(1)[0:1, 0:TSB], onc[:, :],
                                 sq2[dc % 2][:, :], dc == 0, dc == DC - 1,
                                 waits=[sv])
                        sq_rd[dc % 2] = m2_
                    v1 = CP(rs(0), bank(0)[0:1, 0:TSB], waits=[m1_, m2_])
                    v2 = CP(rs(1), bank(1)[0:1, 0:TSB], waits=[m2_])
                    a = TSC(rs(2), rs(0), 1.0 / D, ALU.mult, waits=[v1])
                    a = TT(rs(7), rs(2), rs(2), ALU.mult, waits=[a])
                    a = STT(rs(3), rs(1), 1.0 / D, rs(7), ALU.mult,
                            ALU.subtract, waits=[a, v2])
                    a = TSC(rs(7), rs(3), 1e-5, ALU.add, waits=[a])
                    s1 = ACTI(rs(4), rs(7).bitcast(F32), AF.Ln, waits=[a])
                    s1 = ACTI(rs(4), rs(4).bitcast(F32), AF.Exp, scale=-0.5,
                              waits=[s1])
                    bm = MM(bank(2)[:, 0:TSB], on1[:, :], rs(2), True, True,
                            waits=[a])
                    br = MM(bank(3)[:, 0:TSB], on1[:, :], rs(4), True, True,
                            waits=[s1])
                    xf_tk = []
                    for dc in range(DC):
                        xf_tk.append(bld.dma(
                            "sync", xfmt[dc][:, :],
                            xfm_sp.ap()[dc * 128:(dc + 1) * 128,
                                        tb:tb + TSB]))
                    x1_toks = []
                    for dc in range(DC):
                        t1 = TT(tmp2[:, :], xat2[dc][:, :].bitcast(F32),
                                bank(2)[:, 0:TSB], ALU.subtract,
                                waits=[bm, x1_toks[-1] if x1_toks else None])
                        t2 = TT(tmp2[:, :], tmp2[:, :], bank(3)[:, 0:TSB],
                                ALU.mult, waits=[t1, br])
                        t3 = TT(x1t[dc][:, :], tmp2[:, :],
                                xfmt[dc][:, :].bitcast(F32),
                                ALU.add, waits=[t2, xf_tk[dc]])
                        bld.dma("sync",
                                x1_sp.ap()[dc * 128:(dc + 1) * 128,
                                           tb:tb + TSB],
                                x1t[dc][:, :], waits=[t3])
                        x1_toks.append(t3)
                    m1_ = m2_ = None
                    sq_rd = [None, None]
                    for dc in range(DC):
                        sv = ACTI(sq2[dc % 2][:, :], x1t[dc][:, :].bitcast(F32),
                                  AF.Square, waits=[x1_toks[dc], sq_rd[dc % 2]])
                        m1_ = MM(bank(0)[0:1, 0:TSB], onc[:, :],
                                 x1t[dc][:, :], dc == 0, dc == DC - 1,
                                 waits=[x1_toks[dc]])
                        m2_ = MM(bank(1)[0:1, 0:TSB], onc[:, :],
                                 sq2[dc % 2][:, :], dc == 0, dc == DC - 1,
                                 waits=[sv])
                        sq_rd[dc % 2] = m2_
                    v1 = CP(rs(0), bank(0)[0:1, 0:TSB], waits=[m1_, m2_])
                    v2 = CP(rs(1), bank(1)[0:1, 0:TSB], waits=[m2_])
                    a = TSC(rs(2), rs(0), 1.0 / D, ALU.mult, waits=[v1])
                    a = TT(rs(7), rs(2), rs(2), ALU.mult, waits=[a])
                    a = STT(rs(3), rs(1), 1.0 / D, rs(7), ALU.mult,
                            ALU.subtract, waits=[a, v2])
                    a = TSC(rs(7), rs(3), 1e-5, ALU.add, waits=[a])
                    s1 = ACTI(rs(4), rs(7).bitcast(F32), AF.Ln, waits=[a])
                    s1 = ACTI(rs(4), rs(4).bitcast(F32), AF.Exp, scale=-0.5,
                              waits=[s1])
                    bm = MM(bank(2)[:, 0:TSB], on1[:, :], rs(2), True, True,
                            waits=[a])
                    br = MM(bank(3)[:, 0:TSB], on1[:, :], rs(4), True, True,
                            waits=[s1])
                    u1_fr = None
                    for dc in range(DC):
                        t1 = TT(tmp2[:, :], x1t[dc][:, :].bitcast(F32),
                                bank(2)[:, 0:TSB], ALU.subtract,
                                waits=[bm, u1_fr])
                        t2 = TT(u1b[:, :], tmp2[:, :], bank(3)[:, 0:TSB],
                                ALU.mult, waits=[t1, br])
                        u1_fr = bld.dma(
                            "sync",
                            u1_sp.ap()[dc * 128:(dc + 1) * 128, tb:tb + TSB],
                            u1b[:, :], waits=[t2])
        bld.barrier()

        with ExitStack() as pst:
            rows_h[0] = sb(pst, "rows2", [1, 4096], F32R)
            _par[0] = 0
            TS3 = 512
            u1r = [[sb(pst, f"u1r{w}_{c}", [128, TW], BF16) for c in range(DC)]
                   for w in range(NW)]
            ptl = [[sb(pst, f"ptl{w}_{m}", [128, TW], F32R) for m in range(DC)]
                   for w in range(NW)]
            cpjt = [sb(pst, f"cpjt{i}", [128, 1024], BF16) for i in range(EC)]
            wbig = sb(pst, "wbig", [128, 2048], BF16)
            slab3 = sb(pst, "slab3", [128, 16 * TS3], BF16)
            x1r = [sb(pst, f"x1r{i}", [128, TS3]) for i in range(DC)]
            sq3 = [sb(pst, f"sq3_{i}", [128, TS3], F32R) for i in range(2)]
            tmp3 = sb(pst, "tmp3", [128, TS3])
            ostg = sb(pst, "ostg", [128, 1024])

            u1tk = {}
            for w in range(NW if STOP >= 7 else 0):
                for cch in range(DC):
                    u1tk[(w, cch)] = bld.dma(
                        "sync", u1r[w][cch][:, :],
                        u1_sp.ap()[cch * 128:(cch + 1) * 128,
                                   w * TW:(w + 1) * TW])
            pt_wr = {}
            for g in range(8 if STOP >= 7 else 0):
                bld.barrier()
                wb_tok = bld.dma("sync", wbig[:, :],
                                 wbig_sp.ap()[:, g * 2048:(g + 1) * 2048])
                ctk = []
                for kcl in range(EC):
                    ctk.append(bld.dma(
                        "gpsimd", cpjt[kcl][:, :],
                        cpj.ap()[2048 * g + kcl * 128:
                                 2048 * g + (kcl + 1) * 128, :]))
                cfence = bld.fence("gpsimd")
                for w in range(NW):
                    for ts in range(TW // TS3):
                        zb_rd = [None, None]
                        gl_toks = []
                        for kcl in range(EC):
                            bi = kcl % 2
                            zm = MM(bank(bi)[:, 0:TS3],
                                    wbig[:, kcl * 128:(kcl + 1) * 128],
                                    u1r[w][g][:, ts * TS3:(ts + 1) * TS3],
                                    True, True,
                                    waits=[wb_tok, u1tk[(w, g)], zb_rd[bi]])
                            gv = ACTI(slab3[:, kcl * TS3:(kcl + 1) * TS3],
                                      bank(bi)[:, 0:TS3], AF.Gelu, waits=[zm])
                            zb_rd[bi] = gv
                            gl_toks.append(gv)
                        gl_all = gl_toks[-1]
                        cp_rd = [None, None]
                        for dm in range(DC):
                            bi = 2 + dm % 2
                            md = None
                            for kcl in range(EC):
                                md = MM(bank(bi)[:, 0:TS3],
                                        cpjt[kcl][:, dm * 128:(dm + 1) * 128],
                                        slab3[:, kcl * TS3:(kcl + 1) * TS3],
                                        kcl == 0, kcl == EC - 1,
                                        waits=([gl_all, cfence,
                                                cp_rd[dm % 2]]
                                               if kcl == 0 else []))
                            dst = ptl[w][dm][:, ts * TS3:(ts + 1) * TS3]
                            if g == 0:
                                pv = CP(dst, bank(bi)[:, 0:TS3], waits=[md])
                            else:
                                pv = TT(dst, dst.bitcast(F32),
                                        bank(bi)[:, 0:TS3], ALU.add,
                                        waits=[md, pt_wr[(w, dm, ts)]])
                            cp_rd[dm % 2] = pv
                            pt_wr[(w, dm, ts)] = pv
            for w in range(NW if STOP >= 7 else 0):
                for ts in range(TW // TS3):
                    bld.barrier()
                    col = slice(ts * TS3, (ts + 1) * TS3)
                    tb = w * TW + ts * TS3
                    m1_ = m2_ = None
                    sq_rd = [None, None]
                    for dc in range(DC):
                        sv = ACTI(sq3[dc % 2][:, :],
                                  ptl[w][dc][:, col].bitcast(F32), AF.Square,
                                  waits=[sq_rd[dc % 2]])
                        m1_ = MM(bank(0)[0:1, 0:TS3], onc[:, :],
                                 ptl[w][dc][:, col], dc == 0, dc == DC - 1)
                        m2_ = MM(bank(1)[0:1, 0:TS3], onc[:, :],
                                 sq3[dc % 2][:, :], dc == 0, dc == DC - 1,
                                 waits=[sv])
                        sq_rd[dc % 2] = m2_
                    v1 = CP(rs(0, TS3), bank(0)[0:1, 0:TS3], waits=[m1_, m2_])
                    v2 = CP(rs(1, TS3), bank(1)[0:1, 0:TS3], waits=[m2_])
                    a = TSC(rs(2, TS3), rs(0, TS3), 1.0 / D, ALU.mult,
                            waits=[v1])
                    a = TT(rs(7, TS3), rs(2, TS3), rs(2, TS3), ALU.mult,
                           waits=[a])
                    a = STT(rs(3, TS3), rs(1, TS3), 1.0 / D, rs(7, TS3),
                            ALU.mult, ALU.subtract, waits=[a, v2])
                    a = TSC(rs(7, TS3), rs(3, TS3), 1e-5, ALU.add, waits=[a])
                    s1 = ACTI(rs(4, TS3), rs(7, TS3).bitcast(F32), AF.Ln,
                              waits=[a])
                    s1 = ACTI(rs(4, TS3), rs(4, TS3).bitcast(F32), AF.Exp,
                              scale=-0.5, waits=[s1])
                    bm = MM(bank(2)[:, 0:TS3], on1[:, :], rs(2, TS3),
                            True, True, waits=[a])
                    br = MM(bank(3)[:, 0:TS3], on1[:, :], rs(4, TS3),
                            True, True, waits=[s1])
                    x1tk = []
                    for dc in range(DC):
                        x1tk.append(bld.dma(
                            "sync", x1r[dc][:, :],
                            x1_sp.ap()[dc * 128:(dc + 1) * 128, tb:tb + TS3]
                            .bitcast(F32)))
                    x2_toks = []
                    for dc in range(DC):
                        t1 = TT(tmp3[:, :], ptl[w][dc][:, col].bitcast(F32),
                                bank(2)[:, 0:TS3], ALU.subtract,
                                waits=[bm, x2_toks[-1] if x2_toks else None])
                        t2 = TT(tmp3[:, :], tmp3[:, :], bank(3)[:, 0:TS3],
                                ALU.mult, waits=[t1, br])
                        t3 = TT(x1r[dc][:, :], tmp3[:, :], x1r[dc][:, :],
                                ALU.add, waits=[t2, x1tk[dc]])
                        x2_toks.append(t3)
                    x2_all = x2_toks[-1]
                    ev_fr = [None, None]
                    ostg_fr = None
                    for tt in range(TS3 // 128):
                        pi = 2 + tt % 2
                        trk = None
                        for dc in range(DC):
                            trk = TRA(psT[pi][:, dc * 128:(dc + 1) * 128],
                                      x1r[dc][:, tt * 128:(tt + 1) * 128],
                                      waits=[x2_all, ev_fr[tt % 2]])
                        ev = bld.op("scalar",
                                    lambda e, pi=pi: e.copy(ostg[:, :],
                                                            psT[pi][:, :]),
                                    waits=[trk, ostg_fr])
                        ev_fr[tt % 2] = bld.dma(
                            "sync",
                            out_t.ap()[tb + tt * 128:tb + (tt + 1) * 128, :],
                            ostg[:, :], waits=[ev])
                        ostg_fr = ev_fr[tt % 2]
        bld.barrier()
        bld.flush_barrier()

        with nc.allow_low_precision(reason="bf16 operands by design"), \
                nc.Block() as blk:
            @blk.sync
            def _(e):
                for f in bld.ops["sync"]:
                    f(e)

            @blk.gpsimd
            def _(e):
                for f in bld.ops["gpsimd"]:
                    f(e)

            @blk.vector
            def _(e):
                for f in bld.ops["vector"]:
                    f(e)

            @blk.scalar
            def _(e):
                for f in bld.ops["scalar"]:
                    f(e)

            @blk.tensor
            def _(e):
                for f in bld.ops["tensor"]:
                    f(e)
    return nc


def _prep(inputs):
    f = np.ascontiguousarray
    x = inputs["L0_attn_x"]
    ww = {}
    ww["wig"] = f(np.concatenate([inputs["L0_attn_in_proj_w"],
                                  inputs["L0_attn_gate_proj_w"]], 0).T)
    ww["dtT"] = f(inputs["L0_attn_dt_w"].T)
    ww["xpT"] = f(inputs["L0_attn_x_proj_w"].T.reshape(EC, 128, 2 * NST)
                  .transpose(1, 0, 2).reshape(128, EC * 2 * NST))
    ww["outT"] = f(inputs["L0_attn_out_proj_w"].T)
    ww["gupT"] = f(np.concatenate([inputs["L0_attn_ffn_gate_w"],
                                   inputs["L0_attn_ffn_up_w"]], 0).T)
    ww["dnT"] = f(inputs["L0_attn_ffn_down_w"].T)
    ww["cpj"] = f(inputs["L0_ffn_w_cproj"].reshape(D, 4, 64, 64)
                  .transpose(2, 1, 3, 0).reshape(16384, D))
    ww["makerJ"] = f(inputs["L0_ffn_rail_maker"].transpose(3, 0, 1, 2)
                     .reshape(NST, 4096))
    ww["upJ"] = f(inputs["L0_ffn_rail_up_proj"].transpose(3, 0, 1, 2)
                  .reshape(NST, 16384))
    ww["dtb"] = f(inputs["L0_attn_dt_b"].reshape(EC, 128).T)
    ww["dpv"] = f(inputs["L0_attn_Dp"].reshape(EC, 128).T)
    ww["n1w"] = f(inputs["L0_attn_norm1_w"].reshape(DC, 128).T)
    ww["fnw"] = f(inputs["L0_attn_ffn_norm_w"].reshape(DC, 128).T)
    ww["eye"] = np.eye(128, dtype=np.float32)
    selm = np.zeros((NST, NST * 128), np.float32)
    for n in range(NST):
        selm[n, n * 128:(n + 1) * 128] = 1.0
    ww["sel16"] = selm
    ww["ones1"] = np.ones((1, 128), np.float32)
    ww["onesc"] = np.ones((128, 1), np.float32)
    in_maps = []
    al = inputs["L0_attn_A_log"]
    for c in range(R):
        m = dict(ww)
        m["x"] = f(np.concatenate([x[0, c * TW:(c + 1) * TW],
                                   x[1, c * TW:(c + 1) * TW]], 0))
        m["a_log"] = f(al[c * 256:(c + 1) * 256].reshape(2, 128, NST)
                       .transpose(1, 0, 2).reshape(128, 2 * NST))
        in_maps.append(m)
    return in_maps


def kernel(**inputs):
    inputs = {k: np.asarray(v, dtype=np.float32) for k, v in inputs.items()}
    in_maps = _prep(inputs)
    nc = build(debug=False)
    res = run_bass_kernel_spmd(nc, in_maps, core_ids=list(range(R)))
    out = np.empty((2, L, D), np.float32)
    for c in range(R):
        o = res.results[c]["out"]
        out[0, c * TW:(c + 1) * TW] = o[:TW]
        out[1, c * TW:(c + 1) * TW] = o[TW:]
    return out
